# version 24
# baseline (speedup 1.0000x reference)
import numpy as np
import ml_dtypes
import concourse.bass as bass
import concourse.mybir as mybir
from concourse.bass_utils import run_bass_kernel_spmd
from contextlib import ExitStack

F32 = mybir.dt.float32
BF16 = mybir.dt.bfloat16
I32 = mybir.dt.int32
AF = mybir.ActivationFunctionType
ALU = mybir.AluOpType
AX = mybir.AxisListType.X

NCORES = 8
T = 16
TOK = 128
D = 1024
NH = 16
HD = 64
LH = 4
DI = 2816
NIT = 22
KC = 8
NEG = -1.0e9
EPS = 1e-5
QHI = 127.49
QLO = -128.49
R1024 = 1.0 / 1024.0
R127 = 1.0 / 127.0
R128 = 1.0 / 128.0


def MM(o, l, r, s, st):
    return lambda e, o=o, l=l, r=r, s=s, st=st: e.matmul(o, l, r, start=s,
                                                         stop=st)


def TRP(o, i, ident):
    return lambda e, o=o, i=i, d=ident: e.transpose(o, i, d)


def ACTV(o, i, f, bias=0.0, scale=1.0, accum=None):
    return lambda e, o=o, i=i, f=f, b=bias, s=scale, a=accum: e.activation(
        out=o, in_=i, func=f, bias=b, scale=s, accum_out=a)


def SCOPY(o, i):
    return lambda e, o=o, i=i: e.copy(out=o, in_=i)


def VCOPY(o, i):
    return lambda e, o=o, i=i: e.tensor_copy(out=o, in_=i)


def TT(o, a, b, op):
    return lambda e, o=o, a=a, b=b, op=op: e.tensor_tensor(out=o, in0=a,
                                                           in1=b, op=op)


def TS(o, a, s1, s2, o0, o1=ALU.bypass):
    if s2 is None:
        return lambda e, o=o, a=a, s1=s1, o0=o0: e.tensor_scalar(
            out=o, in0=a, scalar1=s1, scalar2=None, op0=o0)
    return lambda e, o=o, a=a, s1=s1, s2=s2, o0=o0, o1=o1: e.tensor_scalar(
        out=o, in0=a, scalar1=s1, scalar2=s2, op0=o0, op1=o1)


def STT(o, a, s, b, o0, o1):
    return lambda e, o=o, a=a, s=s, b=b, o0=o0, o1=o1: e.scalar_tensor_tensor(
        out=o, in0=a, scalar=s, in1=b, op0=o0, op1=o1)


def RSUM(o, i, neg=False):
    return lambda e, o=o, i=i, n=neg: e.reduce_sum(out=o, in_=i, axis=AX,
                                                   negate=n)


def RMAXA(o, i):
    return lambda e, o=o, i=i: e.reduce_max(out=o, in_=i, axis=AX,
                                            apply_absolute_value=True)


def RMAX(o, i):
    return lambda e, o=o, i=i: e.reduce_max(out=o, in_=i, axis=AX)


def RED3(o, i):
    return lambda e, o=o, i=i: e.tensor_reduce(out=o, in_=i, axis=AX,
                                               op=ALU.add)


def RECIP(o, i):
    return lambda e, o=o, i=i: e.reciprocal(out=o, in_=i)


def DMA(o, i):
    return lambda e, o=o, i=i: e.dma_start(out=o, in_=i)


def MEMSET(o, v):
    return lambda e, o=o, v=v: e.memset(o, v)


def NOP():
    return lambda e: e.nop()


class Sem:
    def __init__(self, h):
        self.h = h
        self.n = 0

    def cur(self):
        return (self, self.n)

    def at(self, v):
        return (self, v)


class Ctx:
    def __init__(self, nc):
        self.nc = nc
        self.ops = []

    def retro_inc(self, eng, sem, dn=1):
        for i in range(len(self.ops) - 1, -1, -1):
            if self.ops[i][0] == eng:
                self.ops[i][3].append((sem, dn))
                sem.n += dn
                return
        raise RuntimeError("no prior op on " + eng)

    def emit(self, eng, fn, waits=(), inc=None, dn=1):
        incs = [(inc, dn)] if inc is not None else []
        self.ops.append((eng, fn, [w for w in waits if w is not None], incs))
        if inc is not None:
            inc.n += dn

    def run_block(self):
        ops = self.ops
        with self.nc.Block() as block:
            def mk(name):
                def f(eng):
                    waited = {}
                    for (e, fn, waits, incs) in ops:
                        if e != name:
                            continue
                        for (sem, val) in waits:
                            if val <= 0 or waited.get(id(sem), -1) >= val:
                                continue
                            eng.wait_ge(sem.h, val)
                            waited[id(sem)] = val
                        inst = fn(eng)
                        for (sem, v) in incs:
                            inst = inst.then_inc(sem.h, v)
                return f
            block.sync(mk("sync"))
            block.tensor(mk("tensor"))
            block.vector(mk("vector"))
            block.scalar(mk("scalar"))
            block.gpsimd(mk("gpsimd"))
        self.ops = []


def build_nc(stage=3, sub=3, lsub=3, dbg=False):
    nc = bass.Bass("TRN2", target_bir_lowering=False)

    def din(name, shape, dt=F32):
        return nc.dram_tensor(name, shape, dt, kind="ExternalInput").ap()

    x_d = din("x", [T * TOK, D])
    xh_d = din("xh", [TOK, D])
    wqkv_d = din("wqkv", [D, 3 * D])
    wrq_d = din("wrq", [D, 256])
    wrkv_d = din("wrkv", [D, 512])
    wrout_d = din("wrout", [256, D])
    wout_d = din("wout", [D, D])
    qkvb_col_d = din("qkvb_col", [128, 16])
    qkvb_vrow_d = din("qkvb_vrow", [1, D])
    rqb_col_d = din("rqb_col", [128, 2])
    rkvb_col_d = din("rkvb_col", [128, 2])
    rvb_row_d = din("rvb_row", [1, 256])
    routb_row_d = din("routb_row", [1, D])
    outb_row_d = din("outb_row", [1, D])
    glb_d = din("glb", [128, D])
    grb_d = din("grb", [128, D])
    mask0_d = din("mask0", [128, 1024], BF16)
    maskE_d = din("maskE", [128, 1024], BF16)
    maskO_d = din("maskO", [128, 1024], BF16)
    z_col_d = din("z_col", [128, 1])
    zrow_d = din("z_row", [1, 128])
    lmask_d = din("lmask", [65, T * 65], BF16)
    clipx_d = din("clipx", [1, 1])
    clipd_d = din("clipd", [1, 1])
    wg_d = din("wg", [DI, D])
    wu_d = din("wu", [DI, D])
    wd_d = din("wd", [D, DI])

    out_d = nc.dram_tensor("out", [T * TOK, D], F32, kind="ExternalOutput").ap()
    dbg_d = {}
    if dbg:
        for nm, shape, dt in [("d_kr", [128, 2048], BF16), ("d_vr", [128, 2048], BF16),
                              ("d_qfm", [128, 1024], BF16), ("d_att", [128, 2048], BF16),
                              ("d_loc", [128, 1024], F32), ("d_recip", [128, 16], F32),
                              ("d_rem", [128, 1024], F32), ("d_gtm", [128, 16], F32),
                              ("d_mixed", [128, 1024], BF16), ("d_ybf", [128, 1024], BF16),
                              ("d_lattn", [128, 256], BF16), ("d_rkfm", [128, 128], BF16),
                              ("d_rvlm", [64, 256], BF16), ("d_latfm", [128, 256], BF16),
                              ("d_psw", [128, 512], F32), ("d_den", [128, 16], F32),
                              ("d_sc", [1, 16], F32), ("d_swdrow", [1, 1024], F32),
                              ("d_rshb", [128, 1], F32), ("d_hq", [128, 2816], BF16),
                              ("d_hsb", [128, 2048], F32), ("d_xq", [128, 2048], BF16),
                              ("d_swsxg", [128, 1], F32), ("d_hmx", [128, 1], F32),
                              ("d_y2", [128, 1024], F32), ("d_nm2", [128, 16], F32),
                              ("d_rs2", [128, 16], F32)]:
            dbg_d[nm] = nc.dram_tensor(nm, shape, dt, kind="ExternalOutput").ap()

    xmid_d = nc.dram_tensor("xmid_dram", [T * TOK, D], F32).ap()
    h_d = nc.dram_tensor("h_dram", [DI, T * TOK], F32).ap()
    ag_in = nc.dram_tensor("ag_in", [16, D], F32).ap()
    ag_out = nc.dram_tensor("ag_out", [64, D], F32).ap()
    ar1_in = nc.dram_tensor("ar1_in", [1, 128], F32).ap()
    ar1_out = nc.dram_tensor("ar1_out", [1, 128], F32).ap()
    ar2_in = nc.dram_tensor("ar2_in", [1, 128], F32).ap()
    ar2_out = nc.dram_tensor("ar2_out", [1, 128], F32).ap()

    id_bf_d = nc.inline_tensor(np.eye(128).astype(ml_dtypes.bfloat16),
                               "id_bf").ap()
    id_f32_d = nc.inline_tensor(np.eye(128, dtype=np.float32), "id_f32").ap()

    G2 = [[0, 1, 2, 3], [4, 5, 6, 7]]
    G8 = [list(range(8))]

    with ExitStack() as P:
        def psb(name, shape, dt):
            return P.enter_context(nc.sbuf_tensor(name, shape, dt))

        def psem(name):
            return Sem(P.enter_context(nc.semaphore(name)))

        nm1 = psb("nm1", [128, T + 1], F32)
        rs1 = psb("rs1", [128, T + 1], F32)
        nm2 = psb("nm2", [128, T], F32)
        rs2 = psb("rs2", [128, T], F32)
        y2mx = psb("y2mx", [128, 1], F32)
        hmx = psb("hmx", [128, 1], F32)
        ones_bf = psb("ones_bf", [128, 128], BF16)
        ones_f32 = psb("ones_f32", [1, 128], F32)
        id_bf = psb("id_bf_s", [128, 128], BF16)
        id_f32 = psb("id_f32_s", [128, 128], F32)
        qkvb_col = psb("qkvb_col_s", [128, 16], F32)
        qkvb_vrow = psb("qkvb_vrow_s", [1, D], BF16)
        rqb_col = psb("rqb_col_s", [128, 2], F32)
        rkvb_col = psb("rkvb_col_s", [128, 2], F32)
        rvb_row = psb("rvb_row_s", [1, 256], BF16)
        rvb_sc = psb("rvb_sc", [1, 256], BF16)
        routb_row = psb("routb_row_s", [1, D], BF16)
        outb_row = psb("outb_row_s", [1, D], BF16)
        glb = psb("glb_s", [128, D], F32)
        grb = psb("grb_s", [128, D], F32)
        mask0 = psb("mask0_s", [128, 1024], BF16)
        maskE = psb("maskE_s", [128, 1024], BF16)
        maskO = psb("maskO_s", [128, 1024], BF16)
        z_col = psb("z_col_s", [128, 1], F32)
        z_row = psb("z_row_s", [1, 128], BF16)
        lmask = psb("lmask_s", [65, T * 65], BF16)
        clipx = psb("clipx_s", [1, 1], F32)
        clipd = psb("clipd_s", [1, 1], F32)
        sc = psb("sc", [1, 16], F32)
        sxb = psb("sxb", [128, 1], F32)
        rsxb = psb("rsxb", [128, 1], F32)
        rshb = psb("rshb", [128, 1], F32)
        m128 = psb("m128", [1, 128], F32)
        eps_col = psb("eps_col", [128, 1], F32)

        sAld = psem("sAld")
        sAld2 = psem("sAld2")
        sBld = psem("sBld")
        sWld = psem("sWld")
        sPC = psem("sPC")
        sAG = psem("sAG")
        sAR1 = psem("sAR1")
        sAR2 = psem("sAR2")
        sCCR = psem("sCCR")
        sXMs = psem("sXMs")
        sXMl = psem("sXMl")
        sHw = psem("sHw")
        sHr = psem("sHr")
        sOUT = psem("sOUT")
        sFW = psem("sFW")
        sPE = psem("sPE")
        sDVE = psem("sDVE")
        sACT = psem("sACT")
        sPL = psem("sPL")

        ctx = Ctx(nc)
        E = ctx.emit

        wdT_done = [None]

        with ExitStack() as A:
            def asb(name, shape, dt):
                return A.enter_context(nc.sbuf_tensor(name, shape, dt))

            wqkv = asb("wqkv_s", [128, KC * 3 * D], BF16)
            wrq = asb("wrq_s", [128, KC * 256], BF16)
            wrkv = asb("wrkv_s", [128, KC * 512], BF16)
            wrout = asb("wrout_s", [128, 2 * D], BF16)
            wout = asb("wout_s", [128, KC * D], BF16)
            xa = asb("xa", [128, 2 * D], F32)
            xr = asb("xr", [128, 3 * D], F32)
            sq_scr = asb("sq_scr", [128, D], BF16)
            y2_scr = asb("y2_scr", [128, D], F32)
            gd = y2_scr
            gdB = asb("gdB", [128, D], F32)
            ssq = asb("ssq", [128, 1], F32)
            lnv = asb("lnv", [128, 1], F32)
            y_bf = asb("y_bf", [128, D], BF16)
            y_fm = asb("y_fm", [128, 2 * D], BF16)
            pp = asb("pp", [1, D], F32)
            kr = asb("kr", [128, KC * 3 * 128], BF16)
            vr = asb("vr", [128, 3 * D], BF16)
            q_fm = asb("q_fm", [128, 2 * D], BF16)
            rq_fm = asb("rq_fm", [128, 2 * 256], BF16)
            att = asb("att", [128, 2 * D], BF16)
            attT = asb("attT", [128, 2 * D], BF16)
            den = asb("den", [128, 16], F32)
            recip = asb("recip", [128, 16], F32)
            loc = asb("loc", [128, D], F32)
            pooled_all = asb("pooled_all", [64, D], F32)
            pooled_bf = asb("pooled_bf", [64, D], BF16)
            pooled_fm = asb("pooled_fm", [128, 512], BF16)
            rk_fm = asb("rk_fm", [128, 128], BF16)
            rv_lm = asb("rv_lm", [64, 256], BF16)
            latt = asb("latt", [128, 4 * 66], BF16)
            lattn = asb("lattn", [128, 256], BF16)
            lattnT = asb("lattnT", [64, 512], BF16)
            lsum = asb("lsum", [128, 4], F32)
            lrecip = asb("lrecip", [128, 4], F32)
            lat_fm = asb("lat_fm", [128, 256], BF16)
            rem = asb("rem", [128, D], F32)
            
            g1 = asb("g1", [128, 16], F32)
            g2 = asb("g2", [128, 16], F32)
            gex = asb("gex", [128, 16], F32)
            g_tm = asb("g_tm", [128, 16], F32)
            dif = asb("dif", [128, D], F32)
            mixed = asb("mixed", [128, D], BF16)
            mixed_fm = asb("mixed_fm", [128, D], BF16)
            xmid = asb("xmid", [128, 2 * D], F32)
            tmp1 = asb("tmp1", [128, 1], F32)
            if dbg:
                snap_recip = asb("snap_recip", [128, 16], F32)
                snap_loc = asb("snap_loc", [128, D], F32)
                snap_rem = asb("snap_rem", [128, D], F32)
                snap_den = asb("snap_den", [128, 512], F32)

            psSC0 = A.enter_context(nc.psum_tensor("psSC0", [128, 1024], F32))
            psSC1 = A.enter_context(nc.psum_tensor("psSC1", [128, 1024], F32))
            psW0 = A.enter_context(nc.psum_tensor("psW0", [128, 512], F32))
            psW1 = A.enter_context(nc.psum_tensor("psW1", [128, 512], F32))
            psT0 = A.enter_context(nc.psum_tensor("psT0", [128, 1024], BF16))
            psT1 = A.enter_context(nc.psum_tensor("psT1", [128, 1024], BF16))

            E("gpsimd", MEMSET(ones_bf[:, :], 1.0), inc=sPL)
            E("gpsimd", MEMSET(ones_f32[:, :], 1.0), inc=sPL)
            ones_rdy = sPL.cur()
            for (dst, src) in [
                (id_bf[:, :], id_bf_d), (id_f32[:, :], id_f32_d),
                (qkvb_col[:, :], qkvb_col_d), (qkvb_vrow[:, :], qkvb_vrow_d),
                (rqb_col[:, :], rqb_col_d), (rkvb_col[:, :], rkvb_col_d),
                (rvb_row[:, :], rvb_row_d), (routb_row[:, :], routb_row_d),
                (outb_row[:, :], outb_row_d),
                (z_col[:, :], z_col_d), (z_row[:, :], zrow_d),
                (clipx[:, :], clipx_d), (clipd[:, :], clipd_d),
            ]:
                E("gpsimd", DMA(dst, src), inc=sWld, dn=16)
            wldC = sWld.cur()
            for (dst, src) in [
                (wqkv[:, :].rearrange("p (k n) -> p k n", n=3 * D),
                 wqkv_d.rearrange("(k p) n -> p k n", p=128)),
                (wrq[:, :].rearrange("p (k n) -> p k n", n=256),
                 wrq_d.rearrange("(k p) n -> p k n", p=128)),
            ]:
                E("gpsimd", DMA(dst, src), inc=sWld, dn=16)
            wldA = sWld.cur()
            for (dst, src) in [
                (glb[:, :], glb_d), (grb[:, :], grb_d),
                (mask0[:, :], mask0_d), (maskE[:, :], maskE_d),
                (maskO[:, :], maskO_d), (lmask[:, :], lmask_d),
            ]:
                E("gpsimd", DMA(dst, src), inc=sWld, dn=16)
            wldM = sWld.cur()
            E("gpsimd", DMA(wrkv[:, :].rearrange("p (k n) -> p k n", n=512),
                            wrkv_d.rearrange("(k p) n -> p k n", p=128)),
              inc=sWld, dn=16)
            wldR = sWld.cur()
            E("gpsimd", DMA(wrout[:, :].rearrange("p (k n) -> p k n", n=D),
                            wrout_d.rearrange("(k p) n -> p k n", p=128)),
              inc=sWld, dn=16)
            wldRO = sWld.cur()
            E("gpsimd", DMA(wout[:, :].rearrange("p (k n) -> p k n", n=D),
                            wout_d.rearrange("(k p) n -> p k n", p=128)),
              inc=sWld, dn=16)
            wldO = sWld.cur()
            wld = wldO

            E("vector", MEMSET(y2mx[:, :], 0.0))
            E("vector", MEMSET(eps_col[:, :], EPS))
            E("vector", MEMSET(hmx[:, :], 0.0), inc=sDVE)
            E("scalar", SCOPY(rvb_sc[:, :], rvb_row[:, :]), waits=[wldC])
            E("scalar", lambda e: e.mul(out=rvb_sc[:, :], in_=rvb_sc[:, :],
                                        mul=128.0), inc=sACT)

            a_free = {}
            pp_free = {}
            evA = {}
            rsA = {}
            pool_free = {}
            aldm = {}

            def emit_pooled(pa):
                pv = att[:, (pa % 2) * D:(pa % 2) * D + D]
                E("vector", TS(pv, xa[:, (pa % 2) * D:(pa % 2) * D + D],
                               nm1[:, pa:pa + 1], rs1[:, pa:pa + 1],
                               ALU.add, ALU.mult),
                  waits=[rsA[pa], pool_free.get(pa - 2)], inc=sDVE)
                ym2 = sDVE.cur()
                a_free[pa] = ym2
                E("tensor", MM(psW0[0:1, 0:512], ones_bf[:, 0:1],
                               pv[:, 0:512], True, True),
                  waits=[ym2, ones_rdy, evA.get(pa - 1)])
                E("tensor", MM(psW1[0:1, 0:512], ones_bf[:, 0:1],
                               pv[:, 512:1024], True, True), inc=sPE)
                pem = sPE.cur()
                pool_free[pa] = pem
                ppv = pp[0:1, 0:D]
                E("scalar", SCOPY(ppv[:, 0:512], psW0[0:1, 0:512]),
                  waits=[pem, pp_free.get(pa - 1)])
                E("scalar", SCOPY(ppv[:, 512:1024], psW1[0:1, 0:512]),
                  inc=sACT)
                evA[pa] = sACT.cur()
                E("sync", DMA(ag_in[pa:pa + 1, :], ppv),
                  waits=[evA[pa]], inc=sPC, dn=16)
                pp_free[pa] = sPC.cur()

            for seq in range(T + 1):
                tau = seq if seq < T else -1
                col = tau if tau >= 0 else T
                r = seq % 2
                src = x_d[tau * TOK:(tau + 1) * TOK, :] if tau >= 0 else xh_d
                xav = xa[:, r * D:(r + 1) * D]
                if seq % 2 == 0:
                    E("sync", DMA(xav, src), waits=[a_free.get(seq - 2)],
                      inc=sAld, dn=16)
                    aldm[seq] = sAld.cur()
                else:
                    E("scalar", DMA(xav, src), waits=[a_free.get(seq - 2)],
                      inc=sAld2, dn=16)
                    aldm[seq] = sAld2.cur()
                nmc = nm1[:, col:col + 1]
                rsc = rs1[:, col:col + 1]
                E("vector", RSUM(nmc, xav, neg=True), waits=[aldm[seq]],
                  inc=sDVE)
                E("vector", TS(nmc, nmc, R1024, None, ALU.mult),
                  waits=[sDVE.cur()], inc=sDVE)
                nm_m = sDVE.cur()
                E("scalar", ACTV(sq_scr[:, :], xav, AF.Square, bias=nmc,
                                 accum=ssq[:, 0:1]), waits=[nm_m], inc=sACT)
                E("scalar", ACTV(lnv[:, 0:1], ssq[:, 0:1], AF.Ln,
                                 scale=R1024, bias=eps_col[:, 0:1]),
                  waits=[sACT.cur()], inc=sACT)
                E("scalar", ACTV(rsc, lnv[:, 0:1], AF.Exp, scale=-0.5),
                  waits=[sACT.cur()], inc=sACT)
                rsA[col] = sACT.cur()
                if tau < 0:
                    a_free[seq] = rsA[col]
                if seq >= 1:
                    emit_pooled(seq - 1)

            E("gpsimd", lambda e: e.collective_compute(
                "AllGather", ALU.bypass, replica_groups=G2,
                ins=[ag_in.opt()], outs=[ag_out.opt()]),
              waits=[sPC.cur()], inc=sAG)
            E("sync", DMA(pooled_all[:, :], ag_out), waits=[sAG.cur()],
              inc=sCCR, dn=16)
            E("scalar", SCOPY(pooled_bf[:, :], pooled_all[:, :]),
              waits=[sCCR.cur()], inc=sACT)
            pbf_m = sACT.cur()

            if stage >= 1:
                psW_free = [evA.get(T - 1), evA.get(T - 1)]
                psSC_free = [None, None]
                psT_free = [None, None]
                kv_rdy = {}
                sc3 = {}
                av3 = {}
                b_free = {}
                xm_free = {}
                ybf_free = [None]
                yfm_mark = {}
                vmm_mark = {}
                qm_mark = {}
                rqm_mark = {}
                lsm_mark = {}
                att_free = [[pool_free.get(T - 2), pool_free.get(T - 1)]]
                attT_free = [[None, None]]
                loc_free = [None]
                rem_free = [None]
                gex_free = [None]
                mix_free = [None]
                mfm_free = [None]
                latt_free = [None]
                lattn_free = [None]
                lattnT_free = [None]
                latfm_free = [None]
                rkrv_m = [None]
                rkv_done = [False]

                def xld(tau):
                    return sBld.at(16 * (tau + 2))

                def ln_apply(col, xv, extra):
                    E("vector", NOP(), waits=[w for w in extra if w])
                    E("vector", TS(y_bf[:, :], xv, nm1[:, col:col + 1],
                                   rs1[:, col:col + 1], ALU.add, ALU.mult),
                      waits=[rsA[col]], inc=sDVE)
                    return sDVE.cur()

                def fm_transposes(dst, src_bf, first_waits, evac_waits):
                    for f in range(8):
                        w = first_waits if f == 0 else ()
                        E("tensor", TRP(psT0[:, 128 * f:128 * f + 128],
                                        src_bf[:, 128 * f:128 * f + 128],
                                        id_bf[:, :]), waits=w)
                    ctx.retro_inc("tensor", sPE)
                    tm = sPE.cur()
                    E("vector", VCOPY(dst[:, 0:1024], psT0[:, 0:1024]),
                      waits=[tm] + [w for w in evac_waits if w], inc=sDVE)
                    psT_free[0] = sDVE.cur()
                    return tm, sDVE.cur()

                def h1a(tau):
                    slot = tau % 2
                    sx = tau % 3
                    xv = xr[:, sx * D:(sx + 1) * D]
                    col = tau if tau >= 0 else T
                    ym = ln_apply(col, xv, [xld(tau), ybf_free[0]])
                    tm, yfm_m = fm_transposes(
                        y_fm[:, slot * D:(slot + 1) * D], y_bf,
                        [ym, psT_free[0], wldC], [vmm_mark.get(tau - 2)])
                    ybf_free[0] = tm
                    yfm_mark[tau] = yfm_m
                    return ym

                def h1b_batches(tau):
                    slot = tau % 2
                    s3 = tau % 3
                    yv = y_fm[:, slot * D:(slot + 1) * D]
                    yfm_m = yfm_mark[tau]

                    def wbatch(kind, h0, nch):
                        def go():
                            nb_ = min(4, nch - h0)
                            bank = (h0 // 4) % 2
                            ps = psW0 if bank == 0 else psW1
                            for mi in range(nb_):
                                m = h0 + mi
                                ccol = 128 * mi
                                for kc in range(KC):
                                    if kind == "q":
                                        lh = wqkv[:, 3072 * kc + 128 * m:
                                                  3072 * kc + 128 * m + 128]
                                    elif kind == "k":
                                        lh = wqkv[:, 3072 * kc + 1024 + 128 * m:
                                                  3072 * kc + 1024 + 128 * m + 128]
                                    else:
                                        lh = wrq[:, 256 * kc + 128 * m:
                                                 256 * kc + 128 * m + 128]
                                    w = [yfm_m, wldA, psW_free[bank]] \
                                        if (mi == 0 and kc == 0) else ()
                                    E("tensor", MM(ps[:, ccol:ccol + 128], lh,
                                                   yv[:, 128 * kc:128 * kc + 128],
                                                   kc == 0, kc == KC - 1), waits=w)
                            ctx.retro_inc("tensor", sPE)
                            mmm = sPE.cur()
                            for mi in range(nb_):
                                m = h0 + mi
                                ccol = 128 * mi
                                if kind == "q":
                                    dst = q_fm[:, slot * D + 128 * m:
                                               slot * D + 128 * m + 128]
                                    bias = qkvb_col[:, m:m + 1]
                                    guard = sc3.get(tau - 2)
                                elif kind == "k":
                                    dst = kr[:, (m * 3 + s3) * 128:
                                             (m * 3 + s3) * 128 + 128]
                                    bias = qkvb_col[:, 8 + m:9 + m]
                                    guard = sc3.get(tau - 2)
                                else:
                                    dst = rq_fm[:, slot * 256 + 128 * m:
                                                slot * 256 + 128 * m + 128]
                                    bias = rqb_col[:, m:m + 1]
                                    guard = lsm_mark.get(tau - 2)
                                E("scalar", ACTV(dst, ps[:, ccol:ccol + 128],
                                                 AF.Identity, bias=bias),
                                  waits=[mmm, guard] if mi == 0 else (),
                                  inc=sACT if mi == nb_ - 1 else None)
                            psW_free[bank] = sACT.cur()
                            if kind == "q" and h0 == 4:
                                qm_mark[tau] = sACT.cur()
                            elif kind == "rq":
                                rqm_mark[tau] = sACT.cur()
                        return go

                    def vbatch():
                        for n in (range(2) if sub > -3 else []):
                            for kc in range(KC):
                                w = [yfm_m, wldA, psSC_free[0]] \
                                    if (n == 0 and kc == 0) else ()
                                E("tensor", MM(
                                    psSC0[:, 512 * n:512 * n + 512],
                                    yv[:, 128 * kc:128 * kc + 128],
                                    wqkv[:, 3072 * kc + 2048 + 512 * n:
                                         3072 * kc + 2048 + 512 * n + 512],
                                    kc == 0, False), waits=w)
                            E("tensor", MM(psSC0[:, 512 * n:512 * n + 512],
                                           ones_bf[0:1, :],
                                           qkvb_vrow[0:1, 512 * n:512 * n + 512],
                                           False, True))
                        ctx.retro_inc("tensor", sPE)
                        vmm = sPE.cur()
                        vmm_mark[tau] = vmm
                        if sub > -3:
                            E("scalar", ACTV(vr[:, s3 * D:(s3 + 1) * D],
                                             psSC0[:, :], AF.Copy),
                              waits=[vmm, av3.get(tau - 2)], inc=sACT)
                            psSC_free[0] = sACT.cur()
                        kv_rdy[tau] = sACT.cur()
                    out = []
                    if sub > -4:
                        out.append(wbatch("k", 0, 8))
                        out.append(wbatch("k", 4, 8))
                        if tau >= 0:
                            out.append(wbatch("q", 0, 8))
                            out.append(wbatch("q", 4, 8))
                            out.append(wbatch("rq", 0, 2))
                    out.append(vbatch)
                    return out

                pend = [[]]

                def set_batches(tau):
                    pend[0] = h1b_batches(tau)

                def nb():
                    if pend[0]:
                        pend[0].pop(0)()

                def flush_b():
                    while pend[0]:
                        pend[0].pop(0)()

                def emit_rkv():
                    for f in range(8):
                        w = [pbf_m, psT_free[1], wldC] if f == 0 else ()
                        E("tensor", TRP(psT1[:, 64 * f:64 * f + 64],
                                        pooled_bf[0:64, 128 * f:128 * f + 128],
                                        id_bf[0:64, 0:64]), waits=w)
                    ctx.retro_inc("tensor", sPE)
                    tm = sPE.cur()
                    E("vector", VCOPY(pooled_fm[:, :], psT1[:, 0:512]),
                      waits=[tm], inc=sDVE)
                    pfm = sDVE.cur()
                    psT_free[1] = pfm
                    for m in range(2):
                        for kc in range(KC):
                            w = [pfm, wldR, psW_free[0]] if (m == 0 and kc == 0) else ()
                            E("tensor", MM(
                                psW0[:, 64 * m:64 * m + 64],
                                wrkv[:, 512 * kc + 128 * m:
                                     512 * kc + 128 * m + 128],
                                pooled_fm[:, 64 * kc:64 * kc + 64],
                                kc == 0, kc == KC - 1), waits=w)
                    ctx.retro_inc("tensor", sPE)
                    rkm = sPE.cur()
                    for m in range(2):
                        E("scalar", ACTV(rk_fm[:, 64 * m:64 * m + 64],
                                         psW0[:, 64 * m:64 * m + 64], AF.Identity,
                                         scale=R128, bias=rkvb_col[:, m:m + 1]),
                          waits=[rkm] if m == 0 else (),
                          inc=sACT if m == 1 else None)
                    psW_free[0] = sACT.cur()
                    for h in range(LH):
                        for kc in range(KC):
                            w = [pfm, wldR, psW_free[1]] if (h == 0 and kc == 0) else ()
                            E("tensor", MM(
                                psW1[0:64, 64 * h:64 * h + 64],
                                pooled_fm[:, 64 * kc:64 * kc + 64],
                                wrkv[:, 512 * kc + 256 + 64 * h:
                                     512 * kc + 256 + 64 * h + 64],
                                kc == 0, False), waits=w)
                        E("tensor", MM(psW1[0:64, 64 * h:64 * h + 64],
                                       ones_bf[0:1, 0:64],
                                       rvb_sc[0:1, 64 * h:64 * h + 64],
                                       False, True))
                    ctx.retro_inc("tensor", sPE)
                    rvm = sPE.cur()
                    E("scalar", ACTV(rv_lm[0:64, :], psW1[0:64, 0:256], AF.Copy,
                                     scale=R128), waits=[rvm], inc=sACT)
                    psW_free[1] = sACT.cur()
                    rkrv_m[0] = sACT.cur()
                    rkv_done[0] = True

                E("sync", DMA(xr[:, 2 * D:3 * D], xh_d), inc=sBld, dn=16)
                E("sync", DMA(xr[:, 0:D], x_d[0:TOK, :]), inc=sBld, dn=16)
                ym = h1a(-1)
                b_free[-1] = ym
                set_batches(-1)
                flush_b()
                E("sync", DMA(xr[:, D:2 * D], x_d[TOK:2 * TOK, :]),
                  inc=sBld, dn=16)
                h1a(0)
                set_batches(0)
                flush_b()
                if sub >= 0:
                    emit_rkv()
                h1a(1)
                xm_sp = {}

                def emit_ln2(tt):
                    xmv2 = xmid[:, (tt % 2) * D:(tt % 2) * D + D]
                    E("vector", RSUM(nm2[:, tt:tt + 1], xmv2, neg=True),
                      waits=[xm_sp[tt]], inc=sDVE)
                    E("vector", TS(nm2[:, tt:tt + 1], nm2[:, tt:tt + 1],
                                   R1024, None, ALU.mult),
                      waits=[sDVE.cur()], inc=sDVE)
                    nm2m = sDVE.cur()
                    E("scalar", ACTV(sq_scr[:, :], xmv2, AF.Square,
                                     bias=nm2[:, tt:tt + 1], accum=ssq[:, 0:1]),
                      waits=[nm2m], inc=sACT)
                    E("scalar", ACTV(lnv[:, 0:1], ssq[:, 0:1], AF.Ln,
                                     scale=R1024, bias=eps_col[:, 0:1]),
                      waits=[sACT.cur()], inc=sACT)
                    E("scalar", ACTV(rs2[:, tt:tt + 1], lnv[:, 0:1], AF.Exp,
                                     scale=-0.5), waits=[sACT.cur()], inc=sACT)
                    rs2m = sACT.cur()
                    E("vector", TS(y2_scr[:, :], xmv2, nm2[:, tt:tt + 1],
                                   rs2[:, tt:tt + 1], ALU.add, ALU.mult),
                      waits=[rs2m], inc=sDVE)
                    E("vector", RMAXA(tmp1[:, 0:1], y2_scr[:, :]),
                      waits=[sDVE.cur()], inc=sDVE)
                    E("vector", TT(y2mx[:, :], y2mx[:, :], tmp1[:, 0:1],
                                   ALU.max), waits=[sDVE.cur()], inc=sDVE)
                    xm_free[tt] = sDVE.cur()

                for t in range(T):
                    slot = t % 2
                    xv = xr[:, (t % 3) * D:(t % 3) * D + D]

                    def h1a_next():
                        if t + 2 < T:
                            E("sync", DMA(
                                xr[:, ((t + 2) % 3) * D:((t + 2) % 3) * D + D],
                                x_d[(t + 2) * TOK:(t + 3) * TOK, :]),
                              waits=[b_free.get(t - 1)], inc=sBld, dn=16)
                            h1a(t + 2)
                    if t + 1 < T:
                        set_batches(t + 1)
                    qm = qm_mark.get(t)
                    rqm = rqm_mark.get(t)
                    if sub < 1:
                        h1a_next()

                    if sub >= 1:
                        mask = mask0 if t == 0 else (maskE if t % 2 == 0 else maskO)
                        loc_m = None
                        scm_g = {}
                        expm_g = {}
                        mskm_g = {}
                        trm_g = {}
                        atTm_g = {}
                        avm_g = {}
                        SH = {g: ((0, 4 * g), (1, 4 * g + 2),
                                  (2, 4 * g + 1), (3, 4 * g + 3))
                              for g in range(4)}

                        def e_sc(g):
                            gb = g % 2
                            ps = psSC0 if gb == 0 else psSC1
                            for (j, h) in SH[g]:
                                pt = h // 2
                                pr = 64 * (h % 2)
                                lhq = q_fm[pr:pr + 64,
                                           slot * D + 128 * pt:
                                           slot * D + 128 * pt + 128]
                                for s_i, ktau in enumerate((t - 1, t)):
                                    ks = ktau % 3
                                    rhk = kr[pr:pr + 64,
                                             (pt * 3 + ks) * 128:
                                             (pt * 3 + ks) * 128 + 128]
                                    w = [qm, kv_rdy[t - 1], kv_rdy[t],
                                         psSC_free[gb]] if (j == 0 and s_i == 0) \
                                        else ()
                                    E("tensor", MM(
                                        ps[:, 256 * j + 128 * s_i:
                                           256 * j + 128 * s_i + 128],
                                        lhq, rhk, True, True), waits=w)
                            ctx.retro_inc("tensor", sPE)
                            scm_g[g] = sPE.cur()
                            if g == 3:
                                sc3[t] = scm_g[g]

                        def e_exp(g):
                            gb = g % 2
                            ps = psSC0 if gb == 0 else psSC1
                            atv = att[:, gb * D:(gb + 1) * D]
                            E("scalar", ACTV(atv, ps[:, :], AF.Exp, scale=0.125),
                              waits=[scm_g[g], att_free[0][gb]], inc=sACT)
                            psSC_free[gb] = sACT.cur()
                            expm_g[g] = sACT.cur()

                        def e_mask(g):
                            gb = g % 2
                            atv = att[:, gb * D:(gb + 1) * D]
                            E("vector", TT(atv, atv, mask[:, :], ALU.mult),
                              waits=[expm_g[g], wldM], inc=sDVE)
                            mskm_g[g] = sDVE.cur()

                        def e_trp(g):
                            gb = g % 2
                            psT = psT1 if gb == 0 else psT0
                            atv = att[:, gb * D:(gb + 1) * D]
                            for jj in range(8):
                                w = [mskm_g[g], psT_free[1 - gb]] \
                                    if jj == 0 else ()
                                E("tensor", TRP(
                                    psT[:, 128 * jj:128 * jj + 128],
                                    atv[:, 128 * jj:128 * jj + 128], id_bf[:, :]),
                                  waits=w)
                            ctx.retro_inc("tensor", sPE)
                            trm_g[g] = sPE.cur()
                            att_free[0][gb] = trm_g[g]

                        def e_cpy(g):
                            gb = g % 2
                            psT = psT1 if gb == 0 else psT0
                            atT = attT[:, gb * D:(gb + 1) * D]
                            E("vector", VCOPY(atT, psT[:, :]),
                              waits=[trm_g[g], attT_free[0][gb]], inc=sDVE)
                            psT_free[1 - gb] = sDVE.cur()
                            atTm_g[g] = sDVE.cur()

                        def e_av(g):
                            gb = g % 2
                            psw = psW0 if gb == 0 else psW1
                            atT = attT[:, gb * D:(gb + 1) * D]
                            for (j, h) in SH[g]:
                                for s_i, ktau in enumerate((t - 1, t)):
                                    w = [atTm_g[g], psW_free[gb]] \
                                        if (j == 0 and s_i == 0) else ()
                                    E("tensor", MM(
                                        psw[:, 256 + 64 * j:256 + 64 * j + 64],
                                        atT[:, 256 * j + 128 * s_i:
                                            256 * j + 128 * s_i + 128],
                                        vr[:, 1024 * (ktau % 3) + 64 * h:
                                           1024 * (ktau % 3) + 64 * h + 64],
                                        s_i == 0, s_i == 1), waits=w)
                                for s_i in range(2):
                                    E("tensor", MM(
                                        psw[:, 2 * j:2 * j + 1],
                                        atT[:, 256 * j + 128 * s_i:
                                            256 * j + 128 * s_i + 128],
                                        ones_bf[:, 0:1], s_i == 0,
                                        (s_i == 1) and t != 0))
                                if t == 0:
                                    E("tensor", MM(psw[:, 2 * j:2 * j + 1],
                                                   z_row[0:1, :],
                                                   ones_bf[0:1, 0:1],
                                                   False, True))
                            ctx.retro_inc("tensor", sPE)
                            avm_g[g] = sPE.cur()
                            if g == 3:
                                av3[t] = avm_g[g]
                            attT_free[0][gb] = avm_g[g]

                        def e_loc(g):
                            gb = g % 2
                            psw = psW0 if gb == 0 else psW1
                            for (j, h) in SH[g]:
                                w = [avm_g[g]] if j == 0 else ()
                                E("vector", RECIP(recip[:, h:h + 1],
                                                  psw[:, 2 * j:2 * j + 1]),
                                  waits=w, inc=sDVE if j == 3 else None)
                            rcp_m = sDVE.cur()
                            for (j, h) in SH[g]:
                                w = [loc_free[0], rcp_m] if j == 0 else ()
                                E("vector", TS(loc[:, 64 * h:64 * h + 64],
                                               psw[:, 256 + 64 * j:
                                                   256 + 64 * j + 64],
                                               recip[:, h:h + 1], None,
                                               ALU.mult),
                                  waits=w, inc=sDVE if j == 3 else None)
                            psW_free[gb] = sDVE.cur()
                            return sDVE.cur()

                        if lsub >= 3:
                            e_sc(0)
                            e_sc(1)
                            e_exp(0)
                            e_mask(0)
                            e_exp(1)
                            e_mask(1)
                            nb()
                            e_trp(0)
                            e_cpy(0)
                            h1a_next()
                            nb()
                            e_trp(1)
                            e_cpy(1)
                            e_av(0)
                            e_sc(2)
                            e_exp(2)
                            e_mask(2)
                            e_av(1)
                            e_sc(3)
                            e_exp(3)
                            e_mask(3)
                            loc_m = e_loc(0)
                            e_trp(2)
                            e_cpy(2)
                            e_av(2)
                            loc_m = e_loc(1)
                            e_trp(3)
                            e_cpy(3)
                            e_av(3)
                            loc_m = e_loc(2)
                            loc_m = e_loc(3)
                        else:
                            h1a_next()
                            for g in range(4):
                                gb = g % 2
                                e_sc(g)
                                e_exp(g)
                                psSC_free[gb] = expm_g[g]
                                att_free[0][gb] = expm_g[g]
                                attT_free[0][gb] = expm_g[g]
                                loc_m = expm_g[g]
                                if lsub >= 1:
                                    e_mask(g)
                                    e_trp(g)
                                    e_cpy(g)
                                    attT_free[0][gb] = atTm_g[g]
                                    loc_m = atTm_g[g]
                                if lsub >= 2:
                                    e_av(g)
                                    psW_free[gb] = avm_g[g]
                                    loc_m = avm_g[g]
                        if dbg and t == 0 and lsub >= 3:
                            E("vector", VCOPY(snap_recip[:, :],
                                              recip[:, :]), waits=[loc_m],
                              inc=sDVE)
                            E("vector", VCOPY(snap_loc[:, :], loc[:, :]),
                              inc=sDVE)
                            E("sync", DMA(dbg_d["d_recip"],
                                          snap_recip[:, :]),
                              waits=[sDVE.cur()], inc=sOUT, dn=16)
                            E("sync", DMA(dbg_d["d_loc"], snap_loc[:, :]),
                              waits=[sDVE.cur()], inc=sOUT, dn=16)

                    if sub >= 2:
                        for h in range(LH):
                            psL = psW0 if h % 2 == 0 else psW1
                            lcol = 66 * (h // 2)
                            pb = 64 * (h % 2)
                            w = [rqm, rkrv_m[0], psW_free[0], psW_free[1],
                                 wldM] if h == 0 else ()
                            E("tensor", MM(psL[:, lcol:lcol + 65],
                                           ones_bf[pb:pb + 1, :],
                                           lmask[pb:pb + 1, 65 * t:65 * t + 65],
                                           True, False), waits=w)
                            E("tensor", MM(
                                psL[:, lcol:lcol + 64],
                                rq_fm[pb:pb + 64,
                                      slot * 256 + 128 * (h // 2):
                                      slot * 256 + 128 * (h // 2) + 128],
                                rk_fm[pb:pb + 64,
                                      64 * (h // 2):64 * (h // 2) + 64],
                                False, True))
                        ctx.retro_inc("tensor", sPE)
                        lsm = sPE.cur()
                        lsm_mark[t] = lsm
                        for h in range(LH):
                            psL = psW0 if h % 2 == 0 else psW1
                            lcol = 66 * (h // 2)
                            w = [lsm, latt_free[0]] if h == 0 else ()
                            E("scalar", ACTV(latt[:, 66 * h:66 * h + 65],
                                             psL[:, lcol:lcol + 65], AF.Exp,
                                             scale=0.125, accum=lsum[:, h:h + 1]),
                              waits=w, inc=sACT if h == LH - 1 else None)
                        psW_free[0] = sACT.cur()
                        psW_free[1] = sACT.cur()
                        lexm = sACT.cur()
                        nb()
                        E("vector", RECIP(lrecip[:, :], lsum[:, :]),
                          waits=[lexm], inc=sDVE)
                        lrc_m = sDVE.cur()
                        for h in range(LH):
                            w = [lattn_free[0], lrc_m] if h == 0 else ()
                            E("vector", TS(lattn[:, 64 * h:64 * h + 64],
                                           latt[:, 66 * h:66 * h + 64],
                                           lrecip[:, h:h + 1], None, ALU.mult),
                              waits=w, inc=sDVE if h == LH - 1 else None)
                        latm = sDVE.cur()
                        latt_free[0] = latm
                        for h in range(LH):
                            w = [latm, psT_free[1]] if h == 0 else ()
                            E("tensor", TRP(psT1[0:64, 128 * h:128 * h + 128],
                                            lattn[:, 64 * h:64 * h + 64],
                                            id_bf[:, :]), waits=w)
                        ctx.retro_inc("tensor", sPE)
                        ltrm = sPE.cur()
                        lattn_free[0] = ltrm
                        nb()
                        E("vector", VCOPY(lattnT[0:64, :], psT1[0:64, 0:512]),
                          waits=[ltrm, lattnT_free[0]], inc=sDVE)
                        psT_free[1] = sDVE.cur()
                        latTm = sDVE.cur()
                        for h in range(LH):
                            w = [latTm, psW_free[1]] if h == 0 else ()
                            E("tensor", MM(
                                psW1[64 * (h % 2):64 * (h % 2) + 64,
                                     128 * (h // 2):128 * (h // 2) + 128],
                                rv_lm[0:64, 64 * h:64 * h + 64],
                                lattnT[0:64, 128 * h:128 * h + 128],
                                True, True), waits=w)
                        ctx.retro_inc("tensor", sPE)
                        lavm = sPE.cur()
                        lattnT_free[0] = lavm
                        E("scalar", ACTV(lat_fm[:, :], psW1[:, 0:256], AF.Copy),
                          waits=[lavm, latfm_free[0]], inc=sACT)
                        psW_free[1] = sACT.cur()
                        latfm_m = sACT.cur()
                        nb()
                        for n in range(2):
                            for c in range(2):
                                w = [latfm_m, wldRO, psSC_free[1]] \
                                    if (n == 0 and c == 0) else ()
                                E("tensor", MM(
                                    psSC1[:, 512 * n:512 * n + 512],
                                    lat_fm[:, 128 * c:128 * c + 128],
                                    wrout[:, 1024 * c + 512 * n:
                                          1024 * c + 512 * n + 512],
                                    c == 0, False), waits=w)
                            E("tensor", MM(psSC1[:, 512 * n:512 * n + 512],
                                           ones_bf[0:1, :],
                                           routb_row[0:1, 512 * n:512 * n + 512],
                                           False, True))
                        ctx.retro_inc("tensor", sPE)
                        remm = sPE.cur()
                        latfm_free[0] = remm
                        E("scalar", ACTV(rem[:, :], psSC1[:, :], AF.Copy),
                          waits=[remm, rem_free[0]], inc=sACT)
                        psSC_free[1] = sACT.cur()
                        rem_m = sACT.cur()
                        nb()

                    if sub >= 3:
                        E("vector", TT(gd[:, :], loc[:, :], glb[:, :], ALU.mult),
                          waits=[loc_m, wldM], inc=sDVE)
                        E("vector", RED3(g1[:, :],
                                         gd[:, :].rearrange("p (h d) -> p h d", d=64)),
                          waits=[sDVE.cur()], inc=sDVE)
                        E("vector", TT(gdB[:, :], rem[:, :], grb[:, :], ALU.mult),
                          waits=[rem_m], inc=sDVE)
                        E("vector", RED3(g2[:, :],
                                         gdB[:, :].rearrange("p (h d) -> p h d", d=64)),
                          waits=[sDVE.cur()], inc=sDVE)
                        E("vector", TT(g1[:, :], g1[:, :], g2[:, :], ALU.add),
                          waits=[gex_free[0], sDVE.cur()], inc=sDVE)
                        glm = sDVE.cur()
                        E("scalar", ACTV(gex[:, :], g1[:, :], AF.Exp, scale=-1.0),
                          waits=[glm], inc=sACT)
                        E("scalar", ACTV(gex[:, :], gex[:, :], AF.Identity,
                                         bias=1.0), waits=[sACT.cur()],
                          inc=sACT)
                        E("vector", RECIP(g_tm[:, :], gex[:, :]),
                          waits=[sACT.cur()], inc=sDVE)
                        gex_free[0] = sDVE.cur()
                        grc_m = sDVE.cur()
                        E("gpsimd", TT(dif[:, :], loc[:, :], rem[:, :], ALU.subtract),
                          waits=[loc_m, rem_m, mix_free[0]], inc=sPL)
                        difm = sPL.cur()
                        loc_free[0] = difm
                        for h in range(NH):
                            w = [difm, grc_m] if h == 0 else ()
                            E("vector", STT(mixed[:, 64 * h:64 * h + 64],
                                            dif[:, 64 * h:64 * h + 64],
                                            g_tm[:, h:h + 1],
                                            rem[:, 64 * h:64 * h + 64],
                                            ALU.mult, ALU.add),
                              waits=w, inc=sDVE if h == NH - 1 else None)
                        mixm = sDVE.cur()
                        rem_free[0] = mixm
                        tm, mfm_m = fm_transposes(mixed_fm, mixed,
                                                  [mixm, psT_free[0]],
                                                  [mfm_free[0]])
                        mix_free[0] = tm
                        for n in range(2):
                            for kc in range(KC):
                                w = [mfm_m, wldO, psSC_free[1]] \
                                    if (n == 0 and kc == 0) else ()
                                E("tensor", MM(
                                    psSC1[:, 512 * n:512 * n + 512],
                                    mixed_fm[:, 128 * kc:128 * kc + 128],
                                    wout[:, 1024 * kc + 512 * n:
                                         1024 * kc + 512 * n + 512],
                                    kc == 0, False), waits=w)
                            E("tensor", MM(psSC1[:, 512 * n:512 * n + 512],
                                           ones_bf[0:1, :],
                                           outb_row[0:1, 512 * n:512 * n + 512],
                                           False, True))
                        ctx.retro_inc("tensor", sPE)
                        opm = sPE.cur()
                        mfm_free[0] = opm
                        xmv = xmid[:, slot * D:(slot + 1) * D]
                        E("vector", TT(xmv, psSC1[:, :], xv, ALU.add),
                          waits=[opm, xm_free.get(t - 2), sXMs.at(16 * (t - 1))],
                          inc=sDVE)
                        psSC_free[1] = sDVE.cur()
                        xm_m = sDVE.cur()
                        b_free[t] = xm_m
                        E("scalar", DMA(xmid_d[t * TOK:(t + 1) * TOK, :], xmv),
                          waits=[xm_m], inc=sXMs, dn=16)
                        if dbg and t == 0:
                            for nm, buf, wt in [
                                    ("d_kr", kr[:, :], kv_rdy[0]),
                                    ("d_vr", vr[:, :], kv_rdy[0]),
                                    ("d_qfm", q_fm[:, :], qm),
                                    ("d_ybf", y_bf[:, :], kv_rdy[0]),
                                    ("d_att", att[:, :], loc_m),
                                    ("d_rem", rem[:, :], rem_m),
                                    ("d_gtm", g_tm[:, :], mixm),
                                    ("d_mixed", mixed[:, :], mixm),
                                    ("d_lattn", lattn[:, :], latm),
                                    ("d_rkfm", rk_fm[:, :], rkrv_m[0]),
                                    ("d_rvlm", rv_lm[0:64, :], rkrv_m[0]),
                                    ("d_latfm", lat_fm[:, :], latfm_m)]:
                                E("sync", DMA(dbg_d[nm], buf), waits=[wt],
                                  inc=sOUT, dn=16)
                        xm_sp[t] = xm_m

                    else:
                        xmv = xmid[:, slot * D:(slot + 1) * D]
                        E("vector", VCOPY(xmv, xv), waits=[xm_free.get(t - 2), sXMs.at(16 * (t - 1))], inc=sDVE)
                        xm_m = sDVE.cur()
                        b_free[t] = xm_m
                        E("scalar", DMA(xmid_d[t * TOK:(t + 1) * TOK, :], xmv), waits=[xm_m], inc=sXMs, dn=16)
                        xm_free[t] = xm_m
                    flush_b()
                    if sub >= 3 and t >= 1:
                        emit_ln2(t - 1)
                if sub >= 3:
                    emit_ln2(T - 1)
                y2m = sDVE.cur()
                E("tensor", TRP(psW0[0:1, 0:128], y2mx[:, 0:1], id_f32[:, :]),
                  waits=[y2m, psW_free[0], wldC], inc=sPE)
                E("scalar", SCOPY(m128[0:1, :], psW0[0:1, 0:128]),
                  waits=[sPE.cur()], inc=sACT)
                E("sync", DMA(ar1_in, m128[0:1, :]), waits=[sACT.cur()],
                  inc=sPC, dn=16)
                E("gpsimd", lambda e: e.collective_compute(
                    "AllReduce", ALU.max, replica_groups=G8,
                    ins=[ar1_in.opt()], outs=[ar1_out.opt()]),
                  waits=[sPC.cur()], inc=sAR1)

            if stage == 0:
                E("sync", DMA(out_d, x_d), waits=[sAG.cur()], inc=sOUT, dn=16)
                E("sync", NOP(), waits=[sOUT.cur()])
        ctx.run_block()

        wdT = psb("wdT", [128, NIT * 1024], BF16)
        swd_bc = psb("swd_bc", [128, D], F32)
        swd_cols = psb("swd_cols", [128, 8], F32)
        swd_row = psb("swd_row", [1, D], F32)
        rsw_wd = psb("rsw_wd", [128, 1], F32)

        if stage >= 2:
            with ExitStack() as B:
                def bsb(name, shape, dt):
                    return B.enter_context(nc.sbuf_tensor(name, shape, dt))

                xq_fm = bsb("xq_fm", [128, KC * 2048], BF16)
                xm_t = bsb("xm_t", [128, 2 * D], F32)
                y2 = bsb("y2", [128, D], F32)
                wd_st = bsb("wd_st", [128, DI], F32)
                wdq_bf = bsb("wdq_bf", [128, DI], BF16)
                xq_i32 = bsb("xq_i32", [128, D], I32)
                xq_bf = bsb("xq_bf", [128, D], BF16)
                wg_st = bsb("wg_st", [128, 2 * D], F32)
                wu_st = bsb("wu_st", [128, 2 * D], F32)
                w_scr = bsb("w_scr", [128, D], F32)
                w_i32 = bsb("w_i32", [128, D], I32)
                rsw2 = bsb("rsw2", [128, 2], F32)
                wqg_bf = bsb("wqg_bf", [128, 2 * D], BF16)
                wqu_bf = bsb("wqu_bf", [128, 2 * D], BF16)
                wgT = bsb("wgT", [128, 2 * D], BF16)
                wuT = bsb("wuT", [128, 2 * D], BF16)
                amx = bsb("amx", [128, 4], F32)
                w_act1 = bsb("w_act1", [128, 2], F32)
                amx2 = bsb("amx2", [128, 2], F32)
                swsx_g3 = bsb("swsx_g3", [128, 3], F32)
                swsx_u3 = bsb("swsx_u3", [128, 3], F32)
                rsw = bsb("rsw", [128, 2], F32)
                silu_sb = bsb("silu_sb", [128, 1024], F32)
                h_sb = bsb("h_sb", [128, 2 * 2048], F32)
                htmp = bsb("htmp", [128, 1], F32)
                htmp2 = bsb("htmp2", [128, 1], F32)

                psG = [B.enter_context(nc.psum_tensor(f"psG{i}", [128, 512], F32))
                       for i in range(4)]
                psTf = B.enter_context(nc.psum_tensor("psTf", [128, 1024], BF16))
                psTg = B.enter_context(nc.psum_tensor("psTg", [128, 1024], BF16))
                psTu = B.enter_context(nc.psum_tensor("psTu", [128, 1024], BF16))
                psM = B.enter_context(nc.psum_tensor("psM", [128, 512], F32))

                psG_free = [None, None, None, None]
                silu_free = [None, None, None, None]
                hm_marks = {}
                wqm_marks = {}
                gum_marks = {}
                swsx_wm = {}
                qm_g = {}
                qm_u = {}
                qmark = {}
                trpg_marks = {}
                trp_marks = {}
                ev_g = {}
                ev_u = {}
                silm_last = {}
                mul_prev = [None]

                def emit_wq_q(i):
                    r = i % 2
                    E("sync", DMA(wg_st[:, r * D:r * D + D],
                                  wg_d[128 * i:128 * i + 128, :]),
                      waits=[qmark.get(i - 2)], inc=sFW, dn=16)
                    gld = sFW.cur()
                    E("scalar", DMA(wu_st[:, r * D:r * D + D],
                                    wu_d[128 * i:128 * i + 128, :]),
                      inc=sFW, dn=16)
                    uld = sFW.cur()
                    wsm = [None, None]
                    for c, (wv, ldm) in enumerate((
                            (wg_st[:, r * D:r * D + D], gld),
                            (wu_st[:, r * D:r * D + D], uld))):
                        ac2 = 2 * (i % 2) + c
                        E("vector", RMAXA(amx[:, ac2:ac2 + 1], wv),
                          waits=[ldm, mul_prev[0]], inc=sDVE)
                        E("vector", VCOPY(amx2[:, c:c + 1], amx[:, ac2:ac2 + 1]),
                          waits=[sDVE.cur()], inc=sDVE)
                        am = sDVE.cur()

                        def mk_mul(c):
                            return lambda e: e.mul(out=w_act1[:, c:c + 1],
                                                   in_=amx2[:, c:c + 1],
                                                   mul=R127)
                        E("scalar", mk_mul(c), waits=[am], inc=sACT)
                        wsm[c] = sACT.cur()
                    mul_prev[0] = wsm[1]
                    for c, (wv, ldm2, swsx, wq2) in enumerate((
                            (wg_st[:, r * D:r * D + D], gld, swsx_g3, wqg_bf),
                            (wu_st[:, r * D:r * D + D], uld, swsx_u3, wqu_bf))):
                        E("vector", RECIP(rsw[:, c:c + 1], w_act1[:, c:c + 1]),
                          waits=[wsm[c]], inc=sDVE)
                        E("vector", VCOPY(rsw2[:, c:c + 1], rsw[:, c:c + 1]),
                          waits=[sDVE.cur()], inc=sDVE)
                        rm = sDVE.cur()

                        wiv = w_i32[:, :] if c == 0 \
                            else w_scr[:, :].bitcast(I32)
                        E("scalar", ACTV(wiv, wv, AF.Identity,
                                         scale=rsw2[:, c:c + 1]),
                          waits=[rm, ldm2,
                                 (qm_g if c == 0 else qm_u).get(i - 1)],
                          inc=sACT)
                        qact = sACT.cur()
                        E("vector", VCOPY(wq2[:, r * D:r * D + D], wiv),
                          waits=[qact, trp_marks.get(i - 2)], inc=sDVE)
                        (qm_g if c == 0 else qm_u)[i] = sDVE.cur()
                    qmark[i] = qm_u[i]

                def emit_trp(i):
                    r = i % 2
                    for c, (wq2, psT, qm, evp) in enumerate((
                            (wqg_bf, psTg, qm_g[i], ev_g.get(i - 1)),
                            (wqu_bf, psTu, qm_u[i], ev_u.get(i - 1)))):
                        for f in range(8):
                            w = [qm, evp] if f == 0 else ()
                            E("tensor", TRP(psT[:, 128 * f:128 * f + 128],
                                            wq2[:, r * D + 128 * f:
                                                r * D + 128 * f + 128],
                                            id_bf[:, :]), waits=w)
                        ctx.retro_inc("tensor", sPE)
                        (trpg_marks if c == 0 else trp_marks)[i] = sPE.cur()

                def emit_swsx(i):
                    s3 = i % 3
                    for c, swsx in ((0, swsx_g3), (1, swsx_u3)):
                        ac2 = 2 * (i % 2) + c
                        E("vector", TS(swsx[:, s3:s3 + 1], amx[:, ac2:ac2 + 1],
                                       R127, sxb[:, 0:1], ALU.mult, ALU.mult),
                          waits=[sxbm, silm_last.get(i - 3)], inc=sDVE)
                    swsx_wm[i] = sDVE.cur()

                def emit_ev(i):
                    r = i % 2
                    E("vector", VCOPY(wgT[:, r * D:r * D + D], psTg[:, :]),
                      waits=[trpg_marks[i], gum_marks.get(i - 2)], inc=sDVE)
                    ev_g[i] = sDVE.cur()
                    E("vector", VCOPY(wuT[:, r * D:r * D + D], psTu[:, :]),
                      waits=[trp_marks[i]], inc=sDVE)
                    ev_u[i] = sDVE.cur()
                    wqm_marks[i] = ev_u[i]
                emit_wq_q(0)
                emit_trp(0)
                emit_ev(0)
                emit_wq_q(1)
                emit_trp(1)
                emit_ev(1)

                E("sync", DMA(m128[0:1, :], ar1_out), waits=[sAR1.cur()],
                  inc=sCCR, dn=16)
                E("vector", RMAX(sc[0:1, 8:9], m128[0:1, :]),
                  waits=[sCCR.cur()], inc=sDVE)
                E("vector", VCOPY(sc[0:1, 0:1], sc[0:1, 8:9]),
                  waits=[sDVE.cur()], inc=sDVE)
                gmm = sDVE.cur()
                E("scalar", ACTV(sc[0:1, 1:2], clipx[0:1, 0:1], AF.Sigmoid),
                  inc=sACT)
                E("scalar", ACTV(sc[0:1, 2:3], sc[0:1, 1:2], AF.Identity,
                                 scale=sc[0:1, 0:1]),
                  waits=[gmm, sACT.cur()], inc=sACT)
                E("scalar", lambda e: e.mul(out=sc[0:1, 2:3],
                                            in_=sc[0:1, 2:3], mul=R127),
                  waits=[sACT.cur()], inc=sACT)
                sxactm = sACT.cur()
                E("vector", RECIP(sc[0:1, 3:4], sc[0:1, 2:3]),
                  waits=[sxactm], inc=sDVE)
                E("vector", VCOPY(sc[0:1, 9:10], sc[0:1, 3:4]),
                  waits=[sDVE.cur()], inc=sDVE)
                sxm = sDVE.cur()
                E("tensor", MM(psM[:, 0:1], ones_f32[0:1, :], sc[0:1, 2:3],
                               True, True), waits=[sxm])
                E("tensor", MM(psM[:, 2:3], ones_f32[0:1, :], sc[0:1, 9:10],
                               True, True), inc=sPE)
                E("vector", VCOPY(sxb[:, 0:1], psM[:, 0:1]), waits=[sPE.cur()])
                E("vector", VCOPY(rsxb[:, 0:1], psM[:, 2:3]), inc=sDVE)
                sxbm = sDVE.cur()
                psM_free = [sxbm]
                emit_swsx(0)
                emit_swsx(1)

                psTf_free = [None]
                xq_marks = {}
                trm_prev = [None]
                for t in range(T):
                    r = t % 2
                    xmv = xm_t[:, r * D:(r + 1) * D]
                    E("sync", DMA(xmv, xmid_d[t * TOK:(t + 1) * TOK, :]),
                      waits=[sXMs.at(16 * (t + 1)), xq_marks.get(t - 2)],
                      inc=sXMl, dn=16)
                    ldm = sXMl.cur()
                    E("vector", TS(y2[:, :], xmv, nm2[:, t:t + 1],
                                   rs2[:, t:t + 1], ALU.add, ALU.mult),
                      waits=[ldm, sxbm])
                    E("vector", TS(y2[:, :], y2[:, :], rsxb[:, 0:1], QHI,
                                   ALU.mult, ALU.min))
                    E("vector", TS(xq_i32[:, :], y2[:, :], QLO, None, ALU.max))
                    E("vector", VCOPY(xq_bf[:, :], xq_i32[:, :]),
                      waits=[trm_prev[0]], inc=sDVE)
                    xqbm = sDVE.cur()
                    if dbg and t == 0:
                        E("vector", VCOPY(silu_sb[:, 0:1024], y2[:, :]),
                          inc=sDVE)
                        E("sync", DMA(dbg_d["d_y2"], silu_sb[:, 0:1024]),
                          waits=[sDVE.cur()], inc=sOUT, dn=16)
                        E("sync", DMA(dbg_d["d_nm2"], nm2[:, :]),
                          waits=[sDVE.cur()], inc=sOUT, dn=16)
                        E("sync", DMA(dbg_d["d_rs2"], rs2[:, :]),
                          waits=[sDVE.cur()], inc=sOUT, dn=16)
                    for f in range(8):
                        w = [xqbm, psTf_free[0]] if f == 0 else ()
                        E("tensor", TRP(psTf[:, 128 * f:128 * f + 128],
                                        xq_bf[:, 128 * f:128 * f + 128],
                                        id_bf[:, :]), waits=w)
                    ctx.retro_inc("tensor", sPE)
                    trm = sPE.cur()
                    trm_prev[0] = trm
                    E("vector", VCOPY(
                        xq_fm[:, :].rearrange("p (f n) -> p f n", n=2048)[
                            :, :, 128 * t:128 * t + 128],
                        psTf[:, :].rearrange("p (f n) -> p f n", n=128)),
                      waits=[trm], inc=sDVE)
                    psTf_free[0] = sDVE.cur()
                    xq_marks[t] = sDVE.cur()
                xq_all = sDVE.cur()

                wd_j = [0]
                wdT_marks = []
                wd_pe_prev = [None]

                def emit_wd_prep(j):
                    E("sync", DMA(wd_st[:, :], wd_d[128 * j:128 * j + 128, :]),
                      waits=[wdT_marks[j - 1] if j > 0 else None], inc=sFW, dn=16)
                    ldm = sFW.cur()
                    E("vector", RMAXA(swd_cols[:, j:j + 1], wd_st[:, :]),
                      waits=[ldm], inc=sDVE)
                    E("vector", VCOPY(amx2[:, 0:1], swd_cols[:, j:j + 1]),
                      waits=[sDVE.cur()], inc=sDVE)
                    E("scalar", lambda e: e.mul(out=w_act1[:, 0:1],
                                                in_=amx2[:, 0:1], mul=R127),
                      waits=[sDVE.cur()], inc=sACT)
                    E("vector", RECIP(rsw_wd[:, 0:1], w_act1[:, 0:1]),
                      waits=[sACT.cur()], inc=sDVE)
                    rwm = sDVE.cur()
                    for cc in range(3):
                        wdt = 1024 if cc < 2 else DI - 2048
                        E("vector", TS(y2[:, 0:wdt],
                                       wd_st[:, 1024 * cc:1024 * cc + wdt],
                                       rsw_wd[:, 0:1], QHI, ALU.mult, ALU.min),
                          waits=[wd_pe_prev[0], rwm] if cc == 0 else ())
                        E("vector", TS(xq_i32[:, 0:wdt], y2[:, 0:wdt], QLO,
                                       None, ALU.max))
                        E("vector", VCOPY(wdq_bf[:, 1024 * cc:1024 * cc + wdt],
                                          xq_i32[:, 0:wdt]))
                    ctx.retro_inc("vector", sDVE)
                    qm = sDVE.cur()
                    for i in range(NIT):
                        half = i % 8
                        w = [qm, psTf_free[0]] if half == 0 else ()
                        E("tensor", TRP(psTf[:, 128 * half:128 * half + 128],
                                        wdq_bf[:, 128 * i:128 * i + 128],
                                        id_bf[:, :]), waits=w)
                        if half == 7 or i == NIT - 1:
                            ctx.retro_inc("tensor", sPE)
                            tm = sPE.cur()
                            lo = i - half
                            E("vector", VCOPY(
                                wdT[:, :].rearrange("p (i n) -> p i n", n=1024)[
                                    :, lo:lo + half + 1, 128 * j:128 * j + 128],
                                psTf[:, 0:(half + 1) * 128].rearrange(
                                    "p (i n) -> p i n", n=128)),
                              waits=[tm], inc=sDVE)
                            psTf_free[0] = sDVE.cur()
                    wd_pe_prev[0] = sPE.cur()
                    wdT_marks.append(sDVE.cur())

                for i in range(NIT):
                    r = i % 2
                    if i + 2 < NIT:
                        emit_wq_q(i + 2)
                        emit_swsx(i + 2)
                    wq_m = wqm_marks[i]
                    hv = h_sb[:, r * 2048:r * 2048 + 2048]
                    h_first = [True]
                    for n in range(4):
                        pg = psG[(2 * n) % 4]
                        pu = psG[(2 * n + 1) % 4]
                        for kc in range(KC):
                            w = [wq_m, xq_marks[min(4 * n + 3, T - 1)],
                                 psG_free[(2 * n) % 4]] \
                                if kc == 0 else ()
                            E("tensor", MM(
                                pg[:, :],
                                wgT[:, r * D + 128 * kc:r * D + 128 * kc + 128],
                                xq_fm[:, 2048 * kc + 512 * n:
                                      2048 * kc + 512 * n + 512],
                                kc == 0, kc == KC - 1), waits=w)
                        ctx.retro_inc("tensor", sPE)
                        gmm2 = sPE.cur()
                        for kc in range(KC):
                            w = [psG_free[(2 * n + 1) % 4]] if kc == 0 else ()
                            E("tensor", MM(
                                pu[:, :],
                                wuT[:, r * D + 128 * kc:r * D + 128 * kc + 128],
                                xq_fm[:, 2048 * kc + 512 * n:
                                      2048 * kc + 512 * n + 512],
                                kc == 0, kc == KC - 1), waits=w)
                        ctx.retro_inc("tensor", sPE)
                        umm = sPE.cur()
                        s3 = i % 3
                        sl = n % 2
                        E("scalar", ACTV(silu_sb[:, 512 * sl:512 * sl + 512],
                                         pg[:, :], AF.Silu,
                                         scale=swsx_g3[:, s3:s3 + 1]),
                          waits=[gmm2, silu_free[sl], swsx_wm[i]], inc=sACT)
                        psG_free[(2 * n) % 4] = sACT.cur()
                        silm = sACT.cur()
                        silm_last[i] = silm
                        w_extra = [sHw.at(16 * (i - 1))] if h_first[0] else []
                        h_first[0] = False
                        E("vector", STT(hv[:, 512 * n:512 * n + 512], pu[:, :],
                                        swsx_u3[:, s3:s3 + 1],
                                        silu_sb[:, 512 * sl:512 * sl + 512],
                                        ALU.mult, ALU.mult),
                          waits=[umm, silm] + w_extra, inc=sDVE)
                        psG_free[(2 * n + 1) % 4] = sDVE.cur()
                        silu_free[sl] = sDVE.cur()
                    gum_marks[i] = sPE.cur()
                    hm = sDVE.cur()
                    if i + 2 < NIT:
                        emit_trp(i + 2)
                        emit_ev(i + 2)
                    E("vector", RMAXA(htmp[:, 0:1], hv), waits=[hm],
                      inc=sDVE)
                    E("vector", TT(hmx[:, :], hmx[:, :], htmp[:, 0:1],
                                   ALU.max), waits=[sDVE.cur()], inc=sDVE)
                    hm_marks[i] = sDVE.cur()
                    E("scalar", DMA(h_d[128 * i:128 * i + 128, :], hv),
                      waits=[hm], inc=sHw, dn=16)
                    if i % 3 == 2 and wd_j[0] < 8:
                        emit_wd_prep(wd_j[0])
                        wd_j[0] += 1
                while wd_j[0] < 8:
                    emit_wd_prep(wd_j[0])
                    wd_j[0] += 1
                wdT_done[0] = wdT_marks[-1]
                if dbg:
                    E("sync", DMA(dbg_d["d_hsb"],
                                  h_sb[:, ((NIT - 1) % 2) * 2048:
                                       ((NIT - 1) % 2) * 2048 + 2048]),
                      waits=[hm_marks[NIT - 1]], inc=sOUT, dn=16)

                if dbg:
                    E("sync", DMA(dbg_d["d_xq"], xq_fm[:, 0:2048]),
                      waits=[xq_all], inc=sOUT, dn=16)
                    E("sync", DMA(dbg_d["d_swsxg"], swsx_g3[:, 0:1]),
                      waits=[sDVE.cur()], inc=sOUT, dn=16)
                    E("sync", DMA(dbg_d["d_hmx"], hmx[:, 0:1]),
                      waits=[sDVE.cur()], inc=sOUT, dn=16)
                hmxm = sDVE.cur()
                E("tensor", TRP(psM[0:1, 0:128], hmx[:, 0:1], id_f32[:, :]),
                  waits=[hmxm, psM_free[0]], inc=sPE)
                E("scalar", SCOPY(m128[0:1, :], psM[0:1, 0:128]),
                  waits=[sPE.cur()], inc=sACT)
                E("sync", DMA(ar2_in, m128[0:1, :]), waits=[sACT.cur()],
                  inc=sPC, dn=16)
                E("gpsimd", lambda e: e.collective_compute(
                    "AllReduce", ALU.max, replica_groups=G8,
                    ins=[ar2_in.opt()], outs=[ar2_out.opt()]),
                  waits=[sPC.cur()], inc=sAR2)

        if stage in (1, 2):
            E("sync", DMA(out_d, xmid_d), waits=[sXMs.at(16 * T)], inc=sOUT, dn=16)
            E("sync", NOP(), waits=[sOUT.cur()])
        ctx.run_block()

        if stage >= 3:
            with ExitStack() as C:
                def csb(name, shape, dt):
                    return C.enter_context(nc.sbuf_tensor(name, shape, dt))

                hrb = csb("hrb", [128, 2 * DI], F32)
                hq_scr = csb("hq_scr", [128, DI], F32)
                hq_i32 = csb("hq_i32", [128, DI], I32)
                hq_bf = csb("hq_bf", [128, 2 * DI], BF16)
                xm_t2 = csb("xm_t2", [128, 2 * D], F32)
                o1 = csb("o1", [128, D], F32)
                out_sb = csb("out_sb", [128, 2 * D], F32)

                psD0 = C.enter_context(nc.psum_tensor("psD0", [128, 1024], F32))
                psD1 = C.enter_context(nc.psum_tensor("psD1", [128, 1024], F32))
                psS = C.enter_context(nc.psum_tensor("psS", [128, 512], F32))
                psS2 = C.enter_context(nc.psum_tensor("psS2", [128, 512], F32))

                E("sync", DMA(m128[0:1, :], ar2_out), waits=[sAR2.cur()],
                  inc=sCCR, dn=16)
                E("vector", RMAX(sc[0:1, 12:13], m128[0:1, :]),
                  waits=[sCCR.cur()], inc=sDVE)
                E("vector", VCOPY(sc[0:1, 4:5], sc[0:1, 12:13]),
                  waits=[sDVE.cur()], inc=sDVE)
                gm2 = sDVE.cur()
                E("scalar", ACTV(sc[0:1, 5:6], clipd[0:1, 0:1], AF.Sigmoid),
                  inc=sACT)
                sig2 = sACT.cur()
                E("scalar", ACTV(sc[0:1, 6:7], sc[0:1, 5:6], AF.Identity,
                                 scale=sc[0:1, 4:5]), waits=[gm2, sig2],
                  inc=sACT)
                E("scalar", lambda e: e.mul(out=sc[0:1, 6:7],
                                            in_=sc[0:1, 6:7], mul=R127),
                  waits=[sACT.cur()], inc=sACT)
                E("vector", RECIP(sc[0:1, 7:8], sc[0:1, 6:7]),
                  waits=[sACT.cur()], inc=sDVE)
                E("vector", VCOPY(sc[0:1, 13:14], sc[0:1, 7:8]),
                  waits=[sDVE.cur()], inc=sDVE)
                shm = sDVE.cur()
                E("tensor", MM(psS[:, 0:1], ones_f32[0:1, :], sc[0:1, 13:14],
                               True, True), waits=[shm], inc=sPE)
                E("vector", VCOPY(rshb[:, 0:1], psS[:, 0:1]), waits=[sPE.cur()],
                  inc=sDVE)
                rshm = sDVE.cur()
                acm = [None]
                for j in range(8):
                    w = [sPE.cur()] if j == 0 else ([acm[0]] if j == 4 else ())
                    E("tensor", TRP(psS2[0:1, 128 * (j % 4):128 * (j % 4) + 128],
                                    swd_cols[:, j:j + 1], id_f32[:, :]), waits=w)
                    if j % 4 == 3:
                        ctx.retro_inc("tensor", sPE)
                        tm = sPE.cur()
                        lo = (j - 3) * 128
                        E("scalar", SCOPY(swd_row[0:1, lo:lo + 512],
                                          psS2[0:1, 0:512]), waits=[tm], inc=sACT)
                        acm[0] = sACT.cur()
                swdr = sACT.cur()
                E("vector", TS(swd_row[0:1, :], swd_row[0:1, :], R127,
                               sc[0:1, 6:7], ALU.mult, ALU.mult),
                  waits=[swdr, shm], inc=sDVE)
                swdm = sDVE.cur()
                E("tensor", MM(psS[:, 0:512], ones_f32[0:1, :],
                               swd_row[0:1, 0:512], True, True),
                  waits=[swdm, sDVE.cur()])
                E("tensor", MM(psS2[:, 0:512], ones_f32[0:1, :],
                               swd_row[0:1, 512:1024], True, True), inc=sPE)
                bcm = sPE.cur()
                E("vector", VCOPY(swd_bc[:, 0:512], psS[:, 0:512]), waits=[bcm])
                E("vector", VCOPY(swd_bc[:, 512:1024], psS2[:, 0:512]), inc=sDVE)
                swdbcm = sDVE.cur()

                if dbg:
                    E("sync", DMA(dbg_d["d_sc"], sc[0:1, :]),
                      waits=[shm], inc=sOUT, dn=16)
                    E("sync", DMA(dbg_d["d_swdrow"], swd_row[0:1, :]),
                      waits=[swdm], inc=sOUT, dn=16)
                    E("sync", DMA(dbg_d["d_rshb"], rshb[:, 0:1]),
                      waits=[rshm], inc=sOUT, dn=16)
                hq_marks = {}
                out_done = {}
                dmm_marks = {}
                hview = h_d.rearrange("(i p) q -> p i q", p=128)
                E("sync", DMA(hrb[:, 0:DI].rearrange("p (i n) -> p i n", n=128),
                              hview[:, :, 0:128]),
                  waits=[sHw.at(16 * NIT)], inc=sHr, dn=16)

                def emit_hq(t):
                    r = t % 2
                    if t + 1 < T:
                        nr = (t + 1) % 2
                        E("sync", DMA(
                            hrb[:, nr * DI:nr * DI + DI].rearrange(
                                "p (i n) -> p i n", n=128),
                            hview[:, :, 128 * (t + 1):128 * (t + 1) + 128]),
                          waits=[hq_marks.get(t - 1)], inc=sHr, dn=16)
                    hld = sHr.at(16 * (t + 1))
                    hv = hrb[:, r * DI:r * DI + DI]
                    hqv = hq_bf[:, r * DI:r * DI + DI]
                    E("vector", TS(hq_scr[:, :], hv, rshb[:, 0:1], QHI,
                                   ALU.mult, ALU.min), waits=[hld, rshm])
                    E("vector", TS(hq_i32[:, :], hq_scr[:, :], QLO, None,
                                   ALU.max))
                    E("vector", VCOPY(hqv, hq_i32[:, :]),
                      waits=[dmm_marks.get(t - 2)], inc=sDVE)
                    hq_marks[t] = sDVE.cur()

                emit_hq(0)
                for t in range(T):
                    r = t % 2
                    if t + 1 < T:
                        emit_hq(t + 1)
                    hqm = hq_marks[t]
                    hqv = hq_bf[:, r * DI:r * DI + DI]
                    if dbg and t == 0:
                        E("sync", DMA(dbg_d["d_hq"], hqv), waits=[hqm],
                          inc=sOUT, dn=16)
                    xmv = xm_t2[:, r * D:(r + 1) * D]
                    E("scalar", DMA(xmv, xmid_d[t * TOK:(t + 1) * TOK, :]),
                      waits=[out_done.get(t - 2)], inc=sXMl, dn=16)
                    xml = sXMl.cur()
                    psD = psD0 if r == 0 else psD1
                    for half in range(2):
                        for kc in range(NIT):
                            w = [hqm, wdT_done[0], out_done.get(t - 2)] \
                                if (half == 0 and kc == 0) else ()
                            E("tensor", MM(
                                psD[:, 512 * half:512 * half + 512],
                                hqv[:, 128 * kc:128 * kc + 128],
                                wdT[:, 1024 * kc + 512 * half:
                                    1024 * kc + 512 * half + 512],
                                kc == 0, kc == NIT - 1), waits=w)
                    ctx.retro_inc("tensor", sPE)
                    dmm_marks[t] = sPE.cur()
                    dmm = sPE.cur()
                    E("vector", TT(o1[:, :], psD[:, :], swd_bc[:, :], ALU.mult),
                      waits=[dmm, swdbcm, xml])
                    ov = out_sb[:, r * D:(r + 1) * D]
                    E("vector", TT(ov, o1[:, :], xmv, ALU.add),
                      waits=[sOUT.at(16 * (t - 1))], inc=sDVE)
                    out_done[t] = sDVE.cur()
                    E("sync", DMA(out_d[t * TOK:(t + 1) * TOK, :], ov),
                      waits=[out_done[t]], inc=sOUT, dn=16)
                E("sync", NOP(), waits=[sOUT.cur()])

        ctx.run_block()

    return nc


def kernel(L0_attn_x, L0_attn_qkv_w, L0_attn_qkv_b, L0_attn_rq_w, L0_attn_rq_b,
           L0_attn_rkv_w, L0_attn_rkv_b, L0_attn_rout_w, L0_attn_rout_b,
           L0_attn_out_w, L0_attn_out_b, L0_attn_gl_w, L0_attn_gr_w,
           L0_ffn_clip_x, L0_ffn_clip_down, L0_ffn_w_gate, L0_ffn_w_up,
           L0_ffn_w_down):
    X = np.ascontiguousarray(np.asarray(L0_attn_x, dtype=np.float32))
    assert X.shape == (2, 8192, 1024)

    def f32(a):
        return np.ascontiguousarray(np.asarray(a, dtype=np.float32))

    qkv_b = f32(L0_attn_qkv_b)
    rq_b = f32(L0_attn_rq_b)
    rkv_b = f32(L0_attn_rkv_b)
    gl_w = f32(L0_attn_gl_w)
    gr_w = f32(L0_attn_gr_w)

    qi = np.arange(128)[:, None]
    kj = np.arange(256)[None, :]
    band = (qi <= kj) & (kj < qi + 128)
    bandE = band.copy()
    bandE[0, 0] = False
    band0 = band & (kj >= 128)

    def mk_mask(b):
        return np.ascontiguousarray(
            np.tile(b.astype(np.float32), (1, 4)).astype(ml_dtypes.bfloat16))

    maskE_np = mk_mask(bandE)
    maskO_np = mk_mask(band)
    mask0_np = mk_mask(band0)

    glb_np = np.ascontiguousarray(
        np.broadcast_to(np.tile(gl_w[:, 0], 16)[None, :], (128, 1024))
        .astype(np.float32))
    grb_np = np.ascontiguousarray(
        np.broadcast_to(np.tile(gr_w[:, 0], 16)[None, :], (128, 1024))
        .astype(np.float32))

    shared = {
        "wqkv": f32(L0_attn_qkv_w), "wrq": f32(L0_attn_rq_w),
        "wrkv": f32(L0_attn_rkv_w), "wrout": f32(L0_attn_rout_w),
        "wout": f32(L0_attn_out_w),
        "qkvb_col": np.ascontiguousarray(qkv_b[:2048].reshape(16, 128).T),
        "qkvb_vrow": np.ascontiguousarray(qkv_b[2048:].reshape(1, 1024)),
        "rqb_col": np.ascontiguousarray(rq_b.reshape(2, 128).T),
        "rkvb_col": np.ascontiguousarray(rkv_b[:256].reshape(2, 128).T),
        "rvb_row": np.ascontiguousarray(rkv_b[256:].reshape(1, 256)),
        "routb_row": np.ascontiguousarray(f32(L0_attn_rout_b).reshape(1, 1024)),
        "outb_row": np.ascontiguousarray(f32(L0_attn_out_b).reshape(1, 1024)),
        "glb": glb_np, "grb": grb_np,
        "maskE": maskE_np, "maskO": maskO_np,
        "clipx": f32(L0_ffn_clip_x).reshape(1, 1),
        "clipd": f32(L0_ffn_clip_down).reshape(1, 1),
        "wg": f32(L0_ffn_w_gate), "wu": f32(L0_ffn_w_up),
        "wd": f32(L0_ffn_w_down),
    }

    in_maps = []
    for c in range(NCORES):
        b, j = divmod(c, 4)
        lo = j * 2048
        xs = np.ascontiguousarray(X[b, lo:lo + 2048, :])
        xh = (np.ascontiguousarray(X[b, lo - 128:lo, :]) if j > 0
              else np.zeros((128, 1024), np.float32))
        z = np.zeros((128, 1), np.float32)
        if j == 0:
            z[0, 0] = 1.0
        lm = np.full((T, 65), NEG, np.float32)
        g0 = 16 * j
        for t in range(T):
            cg = g0 + t
            lm[t, :min(cg, 64)] = 0.0
            lm[t, 64] = 0.0 if cg == 0 else NEG
        m = dict(shared)
        m["x"] = xs
        m["xh"] = xh
        m["z_col"] = z
        m["z_row"] = np.ascontiguousarray(z.reshape(1, 128))
        lmrow = lm.reshape(T * 65)
        lm65 = np.zeros((65, T * 65), np.float32)
        lm65[0, :] = lmrow
        lm65[64, :] = lmrow
        m["lmask"] = np.ascontiguousarray(lm65.astype(ml_dtypes.bfloat16))
        m["mask0"] = mask0_np if j == 0 else maskE_np
        in_maps.append(m)

    nc = build_nc()
    res = run_bass_kernel_spmd(nc, in_maps, core_ids=list(range(NCORES)))
    parts = [np.asarray(res.results[c]["out"]) for c in range(NCORES)]
    return np.concatenate(parts, axis=0).reshape(2, 8192, 1024).astype(
        np.float32)


# version 25
# speedup vs baseline: 1.0628x; 1.0628x over previous
import numpy as np
import ml_dtypes
import concourse.bass as bass
import concourse.mybir as mybir
from concourse.bass_utils import run_bass_kernel_spmd
from contextlib import ExitStack

F32 = mybir.dt.float32
BF16 = mybir.dt.bfloat16
I32 = mybir.dt.int32
AF = mybir.ActivationFunctionType
ALU = mybir.AluOpType
AX = mybir.AxisListType.X

NCORES = 8
T = 16
TOK = 128
D = 1024
NH = 16
HD = 64
LH = 4
DI = 2816
NIT = 22
KC = 8
NEG = -1.0e9
EPS = 1e-5
QHI = 127.49
QLO = -128.49
R1024 = 1.0 / 1024.0
R127 = 1.0 / 127.0
R128 = 1.0 / 128.0


def MM(o, l, r, s, st):
    return lambda e, o=o, l=l, r=r, s=s, st=st: e.matmul(o, l, r, start=s,
                                                         stop=st)


def TRP(o, i, ident):
    return lambda e, o=o, i=i, d=ident: e.transpose(o, i, d)


def ACTV(o, i, f, bias=0.0, scale=1.0, accum=None):
    return lambda e, o=o, i=i, f=f, b=bias, s=scale, a=accum: e.activation(
        out=o, in_=i, func=f, bias=b, scale=s, accum_out=a)


def SCOPY(o, i):
    return lambda e, o=o, i=i: e.copy(out=o, in_=i)


def VCOPY(o, i):
    return lambda e, o=o, i=i: e.tensor_copy(out=o, in_=i)


def TT(o, a, b, op):
    return lambda e, o=o, a=a, b=b, op=op: e.tensor_tensor(out=o, in0=a,
                                                           in1=b, op=op)


def TS(o, a, s1, s2, o0, o1=ALU.bypass):
    if s2 is None:
        return lambda e, o=o, a=a, s1=s1, o0=o0: e.tensor_scalar(
            out=o, in0=a, scalar1=s1, scalar2=None, op0=o0)
    return lambda e, o=o, a=a, s1=s1, s2=s2, o0=o0, o1=o1: e.tensor_scalar(
        out=o, in0=a, scalar1=s1, scalar2=s2, op0=o0, op1=o1)


def STT(o, a, s, b, o0, o1):
    return lambda e, o=o, a=a, s=s, b=b, o0=o0, o1=o1: e.scalar_tensor_tensor(
        out=o, in0=a, scalar=s, in1=b, op0=o0, op1=o1)


def RSUM(o, i, neg=False):
    return lambda e, o=o, i=i, n=neg: e.reduce_sum(out=o, in_=i, axis=AX,
                                                   negate=n)


def RMAXA(o, i):
    return lambda e, o=o, i=i: e.reduce_max(out=o, in_=i, axis=AX,
                                            apply_absolute_value=True)


def RMAX(o, i):
    return lambda e, o=o, i=i: e.reduce_max(out=o, in_=i, axis=AX)


def RED3(o, i):
    return lambda e, o=o, i=i: e.tensor_reduce(out=o, in_=i, axis=AX,
                                               op=ALU.add)


def RECIP(o, i):
    return lambda e, o=o, i=i: e.reciprocal(out=o, in_=i)


def DMA(o, i):
    return lambda e, o=o, i=i: e.dma_start(out=o, in_=i)


def MEMSET(o, v):
    return lambda e, o=o, v=v: e.memset(o, v)


def NOP():
    return lambda e: e.nop()


class Sem:
    def __init__(self, h):
        self.h = h
        self.n = 0

    def cur(self):
        return (self, self.n)

    def at(self, v):
        return (self, v)


class Ctx:
    def __init__(self, nc):
        self.nc = nc
        self.ops = []

    def retro_inc(self, eng, sem, dn=1):
        for i in range(len(self.ops) - 1, -1, -1):
            if self.ops[i][0] == eng:
                self.ops[i][3].append((sem, dn))
                sem.n += dn
                return
        raise RuntimeError("no prior op on " + eng)

    def emit(self, eng, fn, waits=(), inc=None, dn=1):
        incs = [(inc, dn)] if inc is not None else []
        self.ops.append((eng, fn, [w for w in waits if w is not None], incs))
        if inc is not None:
            inc.n += dn

    def run_block(self):
        ops = self.ops
        with self.nc.Block() as block:
            def mk(name):
                def f(eng):
                    waited = {}
                    for (e, fn, waits, incs) in ops:
                        if e != name:
                            continue
                        for (sem, val) in waits:
                            if val <= 0 or waited.get(id(sem), -1) >= val:
                                continue
                            eng.wait_ge(sem.h, val)
                            waited[id(sem)] = val
                        inst = fn(eng)
                        for (sem, v) in incs:
                            inst = inst.then_inc(sem.h, v)
                return f
            block.sync(mk("sync"))
            block.tensor(mk("tensor"))
            block.vector(mk("vector"))
            block.scalar(mk("scalar"))
            block.gpsimd(mk("gpsimd"))
        self.ops = []


def build_nc(stage=3, sub=3, lsub=3, dbg=False):
    nc = bass.Bass("TRN2", target_bir_lowering=False)

    def din(name, shape, dt=F32):
        return nc.dram_tensor(name, shape, dt, kind="ExternalInput").ap()

    x_d = din("x", [T * TOK, D])
    xh_d = din("xh", [TOK, D])
    wqkv_d = din("wqkv", [D, 3 * D])
    wrq_d = din("wrq", [D, 256])
    wrkv_d = din("wrkv", [D, 512])
    wrout_d = din("wrout", [256, D])
    wout_d = din("wout", [D, D])
    qkvb_col_d = din("qkvb_col", [128, 16])
    qkvb_vrow_d = din("qkvb_vrow", [1, D])
    rqb_col_d = din("rqb_col", [128, 2])
    rkvb_col_d = din("rkvb_col", [128, 2])
    rvb_row_d = din("rvb_row", [1, 256])
    routb_row_d = din("routb_row", [1, D])
    outb_row_d = din("outb_row", [1, D])
    glb_d = din("glb", [128, D])
    grb_d = din("grb", [128, D])
    mask0_d = din("mask0", [128, 1024], BF16)
    maskE_d = din("maskE", [128, 1024], BF16)
    maskO_d = din("maskO", [128, 1024], BF16)
    z_col_d = din("z_col", [128, 1])
    zrow_d = din("z_row", [1, 128])
    lmask_d = din("lmask", [65, T * 65], BF16)
    clipx_d = din("clipx", [1, 1])
    clipd_d = din("clipd", [1, 1])
    wg_d = din("wg", [DI, D])
    wu_d = din("wu", [DI, D])
    wd_d = din("wd", [D, DI])

    out_d = nc.dram_tensor("out", [T * TOK, D], F32, kind="ExternalOutput").ap()
    dbg_d = {}
    if dbg:
        for nm, shape, dt in [("d_kr", [128, 2048], BF16), ("d_vr", [128, 2048], BF16),
                              ("d_qfm", [128, 1024], BF16), ("d_att", [128, 2048], BF16),
                              ("d_loc", [128, 1024], F32), ("d_recip", [128, 16], F32),
                              ("d_rem", [128, 1024], F32), ("d_gtm", [128, 16], F32),
                              ("d_mixed", [128, 1024], BF16), ("d_ybf", [128, 1024], BF16),
                              ("d_lattn", [128, 256], BF16), ("d_rkfm", [128, 128], BF16),
                              ("d_rvlm", [64, 256], BF16), ("d_latfm", [128, 256], BF16),
                              ("d_psw", [128, 512], F32), ("d_den", [128, 16], F32),
                              ("d_sc", [1, 16], F32), ("d_swdrow", [1, 1024], F32),
                              ("d_rshb", [128, 1], F32), ("d_hq", [128, 2816], BF16),
                              ("d_hsb", [128, 2048], F32), ("d_xq", [128, 2048], BF16),
                              ("d_swsxg", [128, 1], F32), ("d_hmx", [128, 1], F32),
                              ("d_y2", [128, 1024], F32), ("d_nm2", [128, 16], F32),
                              ("d_rs2", [128, 16], F32)]:
            dbg_d[nm] = nc.dram_tensor(nm, shape, dt, kind="ExternalOutput").ap()

    xmid_d = nc.dram_tensor("xmid_dram", [T * TOK, D], F32).ap()
    h_d = nc.dram_tensor("h_dram", [DI, T * TOK], F32).ap()
    ag_in = nc.dram_tensor("ag_in", [16, D], F32).ap()
    ag_out = nc.dram_tensor("ag_out", [64, D], F32).ap()
    ar1_in = nc.dram_tensor("ar1_in", [1, 128], F32).ap()
    ar1_out = nc.dram_tensor("ar1_out", [1, 128], F32).ap()
    ar2_in = nc.dram_tensor("ar2_in", [1, 128], F32).ap()
    ar2_out = nc.dram_tensor("ar2_out", [1, 128], F32).ap()

    id_bf_d = nc.inline_tensor(np.eye(128).astype(ml_dtypes.bfloat16),
                               "id_bf").ap()
    id_f32_d = nc.inline_tensor(np.eye(128, dtype=np.float32), "id_f32").ap()

    G2 = [[0, 1, 2, 3], [4, 5, 6, 7]]
    G8 = [list(range(8))]

    with ExitStack() as P:
        def psb(name, shape, dt):
            return P.enter_context(nc.sbuf_tensor(name, shape, dt))

        def psem(name):
            return Sem(P.enter_context(nc.semaphore(name)))

        nm1 = psb("nm1", [128, T + 1], F32)
        rs1 = psb("rs1", [128, T + 1], F32)
        nm2 = psb("nm2", [128, T], F32)
        rs2 = psb("rs2", [128, T], F32)
        y2mx = psb("y2mx", [128, 1], F32)
        hmx = psb("hmx", [128, 1], F32)
        ones_bf = psb("ones_bf", [128, 128], BF16)
        ones_f32 = psb("ones_f32", [1, 128], F32)
        id_bf = psb("id_bf_s", [128, 128], BF16)
        id_f32 = psb("id_f32_s", [128, 128], F32)
        qkvb_col = psb("qkvb_col_s", [128, 16], F32)
        qkvb_vrow = psb("qkvb_vrow_s", [1, D], BF16)
        rqb_col = psb("rqb_col_s", [128, 2], F32)
        rkvb_col = psb("rkvb_col_s", [128, 2], F32)
        rvb_row = psb("rvb_row_s", [1, 256], BF16)
        rvb_sc = psb("rvb_sc", [1, 256], BF16)
        routb_row = psb("routb_row_s", [1, D], BF16)
        outb_row = psb("outb_row_s", [1, D], BF16)
        glb = psb("glb_s", [128, D], F32)
        grb = psb("grb_s", [128, D], F32)
        mask0 = psb("mask0_s", [128, 1024], BF16)
        maskE = psb("maskE_s", [128, 1024], BF16)
        maskO = psb("maskO_s", [128, 1024], BF16)
        z_col = psb("z_col_s", [128, 1], F32)
        z_row = psb("z_row_s", [1, 128], BF16)
        lmask = psb("lmask_s", [65, T * 65], BF16)
        clipx = psb("clipx_s", [1, 1], F32)
        clipd = psb("clipd_s", [1, 1], F32)
        sc = psb("sc", [1, 16], F32)
        sxb = psb("sxb", [128, 1], F32)
        rsxb = psb("rsxb", [128, 1], F32)
        rshb = psb("rshb", [128, 1], F32)
        m128 = psb("m128", [1, 128], F32)
        eps_col = psb("eps_col", [128, 1], F32)

        sAld = psem("sAld")
        sAld2 = psem("sAld2")
        sBld = psem("sBld")
        sWld = psem("sWld")
        sPC = psem("sPC")
        sAG = psem("sAG")
        sAR1 = psem("sAR1")
        sAR2 = psem("sAR2")
        sCCR = psem("sCCR")
        sXMs = psem("sXMs")
        sXMl = psem("sXMl")
        sHw = psem("sHw")
        sHr = psem("sHr")
        sOUT = psem("sOUT")
        sFW = psem("sFW")
        sPE = psem("sPE")
        sDVE = psem("sDVE")
        sACT = psem("sACT")
        sPL = psem("sPL")

        ctx = Ctx(nc)
        E = ctx.emit

        wdT_done = [None]

        with ExitStack() as A:
            def asb(name, shape, dt):
                return A.enter_context(nc.sbuf_tensor(name, shape, dt))

            wqkv = asb("wqkv_s", [128, KC * 3 * D], BF16)
            wrq = asb("wrq_s", [128, KC * 256], BF16)
            wrkv = asb("wrkv_s", [128, KC * 512], BF16)
            wrout = asb("wrout_s", [128, 2 * D], BF16)
            wout = asb("wout_s", [128, KC * D], BF16)
            xa = asb("xa", [128, 2 * D], F32)
            xr = asb("xr", [128, 3 * D], F32)
            sq_scr = asb("sq_scr", [128, D], BF16)
            y2_scr = asb("y2_scr", [128, D], F32)
            gd = y2_scr
            gdB = asb("gdB", [128, D], F32)
            ssq = asb("ssq", [128, 1], F32)
            lnv = asb("lnv", [128, 1], F32)
            y_bf = asb("y_bf", [128, D], BF16)
            y_fm = asb("y_fm", [128, 2 * D], BF16)
            pp = asb("pp", [1, D], F32)
            kr = asb("kr", [128, KC * 3 * 128], BF16)
            vr = asb("vr", [128, 3 * D], BF16)
            q_fm = asb("q_fm", [128, 2 * D], BF16)
            rq_fm = asb("rq_fm", [128, 2 * 256], BF16)
            att = asb("att", [128, 2 * D], BF16)
            attT = asb("attT", [128, 2 * D], BF16)
            den = asb("den", [128, 16], F32)
            recip = asb("recip", [128, 16], F32)
            loc = asb("loc", [128, D], F32)
            pooled_all = asb("pooled_all", [64, D], F32)
            pooled_bf = asb("pooled_bf", [64, D], BF16)
            pooled_fm = asb("pooled_fm", [128, 512], BF16)
            rk_fm = asb("rk_fm", [128, 128], BF16)
            rv_lm = asb("rv_lm", [64, 256], BF16)
            latt = asb("latt", [128, 4 * 66], BF16)
            lattn = asb("lattn", [128, 256], BF16)
            lattnT = asb("lattnT", [64, 512], BF16)
            lsum = asb("lsum", [128, 4], F32)
            lrecip = asb("lrecip", [128, 4], F32)
            lat_fm = asb("lat_fm", [128, 256], BF16)
            rem = asb("rem", [128, D], F32)
            
            g1 = asb("g1", [128, 16], F32)
            g2 = asb("g2", [128, 16], F32)
            gex = asb("gex", [128, 16], F32)
            g_tm = asb("g_tm", [128, 16], F32)
            dif = asb("dif", [128, D], F32)
            mixed = asb("mixed", [128, D], BF16)
            mixed_fm = asb("mixed_fm", [128, D], BF16)
            xmid = asb("xmid", [128, 2 * D], F32)
            tmp1 = asb("tmp1", [128, 1], F32)
            if dbg:
                snap_recip = asb("snap_recip", [128, 16], F32)
                snap_loc = asb("snap_loc", [128, D], F32)
                snap_rem = asb("snap_rem", [128, D], F32)
                snap_den = asb("snap_den", [128, 512], F32)

            psSC0 = A.enter_context(nc.psum_tensor("psSC0", [128, 1024], F32))
            psSC1 = A.enter_context(nc.psum_tensor("psSC1", [128, 1024], F32))
            psW0 = A.enter_context(nc.psum_tensor("psW0", [128, 512], F32))
            psW1 = A.enter_context(nc.psum_tensor("psW1", [128, 512], F32))
            psT0 = A.enter_context(nc.psum_tensor("psT0", [128, 1024], BF16))
            psT1 = A.enter_context(nc.psum_tensor("psT1", [128, 1024], BF16))

            E("gpsimd", MEMSET(ones_bf[:, :], 1.0), inc=sPL)
            E("gpsimd", MEMSET(ones_f32[:, :], 1.0), inc=sPL)
            ones_rdy = sPL.cur()
            for (dst, src) in [
                (id_bf[:, :], id_bf_d), (id_f32[:, :], id_f32_d),
                (qkvb_col[:, :], qkvb_col_d), (qkvb_vrow[:, :], qkvb_vrow_d),
                (rqb_col[:, :], rqb_col_d), (rkvb_col[:, :], rkvb_col_d),
                (rvb_row[:, :], rvb_row_d), (routb_row[:, :], routb_row_d),
                (outb_row[:, :], outb_row_d),
                (z_col[:, :], z_col_d), (z_row[:, :], zrow_d),
                (clipx[:, :], clipx_d), (clipd[:, :], clipd_d),
            ]:
                E("gpsimd", DMA(dst, src), inc=sWld, dn=16)
            wldC = sWld.cur()
            for (dst, src) in [
                (wqkv[:, :].rearrange("p (k n) -> p k n", n=3 * D),
                 wqkv_d.rearrange("(k p) n -> p k n", p=128)),
                (wrq[:, :].rearrange("p (k n) -> p k n", n=256),
                 wrq_d.rearrange("(k p) n -> p k n", p=128)),
            ]:
                E("gpsimd", DMA(dst, src), inc=sWld, dn=16)
            wldA = sWld.cur()
            for (dst, src) in [
                (glb[:, :], glb_d), (grb[:, :], grb_d),
                (mask0[:, :], mask0_d), (maskE[:, :], maskE_d),
                (maskO[:, :], maskO_d), (lmask[:, :], lmask_d),
            ]:
                E("gpsimd", DMA(dst, src), inc=sWld, dn=16)
            wldM = sWld.cur()
            E("gpsimd", DMA(wrkv[:, :].rearrange("p (k n) -> p k n", n=512),
                            wrkv_d.rearrange("(k p) n -> p k n", p=128)),
              inc=sWld, dn=16)
            wldR = sWld.cur()
            E("gpsimd", DMA(wrout[:, :].rearrange("p (k n) -> p k n", n=D),
                            wrout_d.rearrange("(k p) n -> p k n", p=128)),
              inc=sWld, dn=16)
            wldRO = sWld.cur()
            E("gpsimd", DMA(wout[:, :].rearrange("p (k n) -> p k n", n=D),
                            wout_d.rearrange("(k p) n -> p k n", p=128)),
              inc=sWld, dn=16)
            wldO = sWld.cur()
            wld = wldO

            E("vector", MEMSET(y2mx[:, :], 0.0))
            E("vector", MEMSET(eps_col[:, :], EPS))
            E("vector", MEMSET(hmx[:, :], 0.0), inc=sDVE)
            E("scalar", SCOPY(rvb_sc[:, :], rvb_row[:, :]), waits=[wldC])
            E("scalar", lambda e: e.mul(out=rvb_sc[:, :], in_=rvb_sc[:, :],
                                        mul=128.0), inc=sACT)

            a_free = {}
            pp_free = {}
            evA = {}
            rsA = {}
            pool_free = {}
            aldm = {}

            def emit_pooled(pa):
                pv = att[:, (pa % 2) * D:(pa % 2) * D + D]
                E("vector", TS(pv, xa[:, (pa % 2) * D:(pa % 2) * D + D],
                               nm1[:, pa:pa + 1], rs1[:, pa:pa + 1],
                               ALU.add, ALU.mult),
                  waits=[rsA[pa], pool_free.get(pa - 2)], inc=sDVE)
                ym2 = sDVE.cur()
                a_free[pa] = ym2
                E("tensor", MM(psW0[0:1, 0:512], ones_bf[:, 0:1],
                               pv[:, 0:512], True, True),
                  waits=[ym2, ones_rdy, evA.get(pa - 1)])
                E("tensor", MM(psW1[0:1, 0:512], ones_bf[:, 0:1],
                               pv[:, 512:1024], True, True), inc=sPE)
                pem = sPE.cur()
                pool_free[pa] = pem
                ppv = pp[0:1, 0:D]
                E("scalar", SCOPY(ppv[:, 0:512], psW0[0:1, 0:512]),
                  waits=[pem, pp_free.get(pa - 1)])
                E("scalar", SCOPY(ppv[:, 512:1024], psW1[0:1, 0:512]),
                  inc=sACT)
                evA[pa] = sACT.cur()
                E("sync", DMA(ag_in[pa:pa + 1, :], ppv),
                  waits=[evA[pa]], inc=sPC, dn=16)
                pp_free[pa] = sPC.cur()

            for seq in range(T + 1):
                tau = seq if seq < T else -1
                col = tau if tau >= 0 else T
                r = seq % 2
                src = x_d[tau * TOK:(tau + 1) * TOK, :] if tau >= 0 else xh_d
                xav = xa[:, r * D:(r + 1) * D]
                if seq % 2 == 0:
                    E("sync", DMA(xav, src), waits=[a_free.get(seq - 2)],
                      inc=sAld, dn=16)
                    aldm[seq] = sAld.cur()
                else:
                    E("scalar", DMA(xav, src), waits=[a_free.get(seq - 2)],
                      inc=sAld2, dn=16)
                    aldm[seq] = sAld2.cur()
                nmc = nm1[:, col:col + 1]
                rsc = rs1[:, col:col + 1]
                E("vector", RSUM(nmc, xav, neg=True), waits=[aldm[seq]],
                  inc=sDVE)
                E("vector", TS(nmc, nmc, R1024, None, ALU.mult),
                  waits=[sDVE.cur()], inc=sDVE)
                nm_m = sDVE.cur()
                E("scalar", ACTV(sq_scr[:, :], xav, AF.Square, bias=nmc,
                                 accum=ssq[:, 0:1]), waits=[nm_m], inc=sACT)
                E("scalar", ACTV(lnv[:, 0:1], ssq[:, 0:1], AF.Ln,
                                 scale=R1024, bias=eps_col[:, 0:1]),
                  waits=[sACT.cur()], inc=sACT)
                E("scalar", ACTV(rsc, lnv[:, 0:1], AF.Exp, scale=-0.5),
                  waits=[sACT.cur()], inc=sACT)
                rsA[col] = sACT.cur()
                if tau < 0:
                    a_free[seq] = rsA[col]
                if seq >= 1:
                    emit_pooled(seq - 1)

            E("gpsimd", lambda e: e.collective_compute(
                "AllGather", ALU.bypass, replica_groups=G2,
                ins=[ag_in.opt()], outs=[ag_out.opt()]),
              waits=[sPC.cur()], inc=sAG)
            E("sync", DMA(pooled_all[:, :], ag_out), waits=[sAG.cur()],
              inc=sCCR, dn=16)
            E("scalar", SCOPY(pooled_bf[:, :], pooled_all[:, :]),
              waits=[sCCR.cur()], inc=sACT)
            pbf_m = sACT.cur()

            if stage >= 1:
                psW_free = [evA.get(T - 1), evA.get(T - 1)]
                psSC_free = [None, None]
                psT_free = [None, None]
                kv_rdy = {}
                sc3 = {}
                av3 = {}
                b_free = {}
                xm_free = {}
                ybf_free = [None]
                yfm_mark = {}
                vmm_mark = {}
                qm_mark = {}
                rqm_mark = {}
                lsm_mark = {}
                att_free = [[pool_free.get(T - 2), pool_free.get(T - 1)]]
                attT_free = [[None, None]]
                loc_free = [None]
                rem_free = [None]
                gex_free = [None]
                mix_free = [None]
                mfm_free = [None]
                latt_free = [None]
                lattn_free = [None]
                lattnT_free = [None]
                latfm_free = [None]
                rkrv_m = [None]
                rkv_done = [False]

                def xld(tau):
                    return sBld.at(16 * (tau + 2))

                def ln_apply(col, xv, extra):
                    E("vector", NOP(), waits=[w for w in extra if w])
                    E("vector", TS(y_bf[:, :], xv, nm1[:, col:col + 1],
                                   rs1[:, col:col + 1], ALU.add, ALU.mult),
                      waits=[rsA[col]], inc=sDVE)
                    return sDVE.cur()

                def fm_transposes(dst, src_bf, first_waits, evac_waits):
                    for f in range(8):
                        w = first_waits if f == 0 else ()
                        E("tensor", TRP(psT0[:, 128 * f:128 * f + 128],
                                        src_bf[:, 128 * f:128 * f + 128],
                                        id_bf[:, :]), waits=w)
                    ctx.retro_inc("tensor", sPE)
                    tm = sPE.cur()
                    E("vector", VCOPY(dst[:, 0:1024], psT0[:, 0:1024]),
                      waits=[tm] + [w for w in evac_waits if w], inc=sDVE)
                    psT_free[0] = sDVE.cur()
                    return tm, sDVE.cur()

                def h1a(tau):
                    slot = tau % 2
                    sx = tau % 3
                    xv = xr[:, sx * D:(sx + 1) * D]
                    col = tau if tau >= 0 else T
                    ym = ln_apply(col, xv, [xld(tau), ybf_free[0]])
                    tm, yfm_m = fm_transposes(
                        y_fm[:, slot * D:(slot + 1) * D], y_bf,
                        [ym, psT_free[0], wldC], [vmm_mark.get(tau - 2)])
                    ybf_free[0] = tm
                    yfm_mark[tau] = yfm_m
                    return ym

                def h1b_batches(tau):
                    slot = tau % 2
                    s3 = tau % 3
                    yv = y_fm[:, slot * D:(slot + 1) * D]
                    yfm_m = yfm_mark[tau]

                    def wbatch(kind, h0, nch):
                        def go():
                            nb_ = min(4, nch - h0)
                            bank = (h0 // 4) % 2
                            ps = psW0 if bank == 0 else psW1
                            for mi in range(nb_):
                                m = h0 + mi
                                ccol = 128 * mi
                                for kc in range(KC):
                                    if kind == "q":
                                        lh = wqkv[:, 3072 * kc + 128 * m:
                                                  3072 * kc + 128 * m + 128]
                                    elif kind == "k":
                                        lh = wqkv[:, 3072 * kc + 1024 + 128 * m:
                                                  3072 * kc + 1024 + 128 * m + 128]
                                    else:
                                        lh = wrq[:, 256 * kc + 128 * m:
                                                 256 * kc + 128 * m + 128]
                                    w = [yfm_m, wldA, psW_free[bank]] \
                                        if (mi == 0 and kc == 0) else ()
                                    E("tensor", MM(ps[:, ccol:ccol + 128], lh,
                                                   yv[:, 128 * kc:128 * kc + 128],
                                                   kc == 0, kc == KC - 1), waits=w)
                            ctx.retro_inc("tensor", sPE)
                            mmm = sPE.cur()
                            for mi in range(nb_):
                                m = h0 + mi
                                ccol = 128 * mi
                                if kind == "q":
                                    dst = q_fm[:, slot * D + 128 * m:
                                               slot * D + 128 * m + 128]
                                    bias = qkvb_col[:, m:m + 1]
                                    guard = sc3.get(tau - 2)
                                elif kind == "k":
                                    dst = kr[:, (m * 3 + s3) * 128:
                                             (m * 3 + s3) * 128 + 128]
                                    bias = qkvb_col[:, 8 + m:9 + m]
                                    guard = sc3.get(tau - 2)
                                else:
                                    dst = rq_fm[:, slot * 256 + 128 * m:
                                                slot * 256 + 128 * m + 128]
                                    bias = rqb_col[:, m:m + 1]
                                    guard = lsm_mark.get(tau - 2)
                                E("scalar", ACTV(dst, ps[:, ccol:ccol + 128],
                                                 AF.Identity, bias=bias),
                                  waits=[mmm, guard] if mi == 0 else (),
                                  inc=sACT if mi == nb_ - 1 else None)
                            psW_free[bank] = sACT.cur()
                            if kind == "q" and h0 == 4:
                                qm_mark[tau] = sACT.cur()
                            elif kind == "rq":
                                rqm_mark[tau] = sACT.cur()
                        return go

                    def vbatch():
                        for n in (range(2) if sub > -3 else []):
                            for kc in range(KC):
                                w = [yfm_m, wldA, psSC_free[0]] \
                                    if (n == 0 and kc == 0) else ()
                                E("tensor", MM(
                                    psSC0[:, 512 * n:512 * n + 512],
                                    yv[:, 128 * kc:128 * kc + 128],
                                    wqkv[:, 3072 * kc + 2048 + 512 * n:
                                         3072 * kc + 2048 + 512 * n + 512],
                                    kc == 0, False), waits=w)
                            E("tensor", MM(psSC0[:, 512 * n:512 * n + 512],
                                           ones_bf[0:1, :],
                                           qkvb_vrow[0:1, 512 * n:512 * n + 512],
                                           False, True))
                        ctx.retro_inc("tensor", sPE)
                        vmm = sPE.cur()
                        vmm_mark[tau] = vmm
                        if sub > -3:
                            E("scalar", ACTV(vr[:, s3 * D:(s3 + 1) * D],
                                             psSC0[:, :], AF.Copy),
                              waits=[vmm, av3.get(tau - 2)], inc=sACT)
                            psSC_free[0] = sACT.cur()
                        kv_rdy[tau] = sACT.cur()
                    out = []
                    if sub > -4:
                        out.append(wbatch("k", 0, 8))
                        out.append(wbatch("k", 4, 8))
                        if tau >= 0:
                            out.append(wbatch("q", 0, 8))
                            out.append(wbatch("q", 4, 8))
                            out.append(wbatch("rq", 0, 2))
                    out.append(vbatch)
                    return out

                pend = [[]]

                def set_batches(tau):
                    pend[0] = h1b_batches(tau)

                def nb():
                    if pend[0]:
                        pend[0].pop(0)()

                def flush_b():
                    while pend[0]:
                        pend[0].pop(0)()

                def emit_rkv():
                    for f in range(8):
                        w = [pbf_m, psT_free[1], wldC] if f == 0 else ()
                        E("tensor", TRP(psT1[:, 64 * f:64 * f + 64],
                                        pooled_bf[0:64, 128 * f:128 * f + 128],
                                        id_bf[0:64, 0:64]), waits=w)
                    ctx.retro_inc("tensor", sPE)
                    tm = sPE.cur()
                    E("vector", VCOPY(pooled_fm[:, :], psT1[:, 0:512]),
                      waits=[tm], inc=sDVE)
                    pfm = sDVE.cur()
                    psT_free[1] = pfm
                    for m in range(2):
                        for kc in range(KC):
                            w = [pfm, wldR, psW_free[0]] if (m == 0 and kc == 0) else ()
                            E("tensor", MM(
                                psW0[:, 64 * m:64 * m + 64],
                                wrkv[:, 512 * kc + 128 * m:
                                     512 * kc + 128 * m + 128],
                                pooled_fm[:, 64 * kc:64 * kc + 64],
                                kc == 0, kc == KC - 1), waits=w)
                    ctx.retro_inc("tensor", sPE)
                    rkm = sPE.cur()
                    for m in range(2):
                        E("scalar", ACTV(rk_fm[:, 64 * m:64 * m + 64],
                                         psW0[:, 64 * m:64 * m + 64], AF.Identity,
                                         scale=R128, bias=rkvb_col[:, m:m + 1]),
                          waits=[rkm] if m == 0 else (),
                          inc=sACT if m == 1 else None)
                    psW_free[0] = sACT.cur()
                    for h in range(LH):
                        for kc in range(KC):
                            w = [pfm, wldR, psW_free[1]] if (h == 0 and kc == 0) else ()
                            E("tensor", MM(
                                psW1[0:64, 64 * h:64 * h + 64],
                                pooled_fm[:, 64 * kc:64 * kc + 64],
                                wrkv[:, 512 * kc + 256 + 64 * h:
                                     512 * kc + 256 + 64 * h + 64],
                                kc == 0, False), waits=w)
                        E("tensor", MM(psW1[0:64, 64 * h:64 * h + 64],
                                       ones_bf[0:1, 0:64],
                                       rvb_sc[0:1, 64 * h:64 * h + 64],
                                       False, True))
                    ctx.retro_inc("tensor", sPE)
                    rvm = sPE.cur()
                    E("scalar", ACTV(rv_lm[0:64, :], psW1[0:64, 0:256], AF.Copy,
                                     scale=R128), waits=[rvm], inc=sACT)
                    psW_free[1] = sACT.cur()
                    rkrv_m[0] = sACT.cur()
                    rkv_done[0] = True

                E("sync", DMA(xr[:, 2 * D:3 * D], xh_d), inc=sBld, dn=16)
                E("sync", DMA(xr[:, 0:D], x_d[0:TOK, :]), inc=sBld, dn=16)
                ym = h1a(-1)
                b_free[-1] = ym
                set_batches(-1)
                flush_b()
                E("sync", DMA(xr[:, D:2 * D], x_d[TOK:2 * TOK, :]),
                  inc=sBld, dn=16)
                h1a(0)
                set_batches(0)
                flush_b()
                if sub >= 0:
                    emit_rkv()
                h1a(1)
                xm_sp = {}

                def emit_ln2(tt):
                    xmv2 = xmid[:, (tt % 2) * D:(tt % 2) * D + D]
                    E("vector", RSUM(nm2[:, tt:tt + 1], xmv2, neg=True),
                      waits=[xm_sp[tt]], inc=sDVE)
                    E("vector", TS(nm2[:, tt:tt + 1], nm2[:, tt:tt + 1],
                                   R1024, None, ALU.mult),
                      waits=[sDVE.cur()], inc=sDVE)
                    nm2m = sDVE.cur()
                    E("scalar", ACTV(sq_scr[:, :], xmv2, AF.Square,
                                     bias=nm2[:, tt:tt + 1], accum=ssq[:, 0:1]),
                      waits=[nm2m], inc=sACT)
                    E("scalar", ACTV(lnv[:, 0:1], ssq[:, 0:1], AF.Ln,
                                     scale=R1024, bias=eps_col[:, 0:1]),
                      waits=[sACT.cur()], inc=sACT)
                    E("scalar", ACTV(rs2[:, tt:tt + 1], lnv[:, 0:1], AF.Exp,
                                     scale=-0.5), waits=[sACT.cur()], inc=sACT)
                    rs2m = sACT.cur()
                    E("vector", TS(y2_scr[:, :], xmv2, nm2[:, tt:tt + 1],
                                   rs2[:, tt:tt + 1], ALU.add, ALU.mult),
                      waits=[rs2m], inc=sDVE)
                    E("vector", RMAXA(tmp1[:, 0:1], y2_scr[:, :]),
                      waits=[sDVE.cur()], inc=sDVE)
                    E("vector", TT(y2mx[:, :], y2mx[:, :], tmp1[:, 0:1],
                                   ALU.max), waits=[sDVE.cur()], inc=sDVE)
                    xm_free[tt] = sDVE.cur()

                for t in range(T):
                    slot = t % 2
                    xv = xr[:, (t % 3) * D:(t % 3) * D + D]

                    def h1a_next():
                        if t + 2 < T:
                            E("sync", DMA(
                                xr[:, ((t + 2) % 3) * D:((t + 2) % 3) * D + D],
                                x_d[(t + 2) * TOK:(t + 3) * TOK, :]),
                              waits=[b_free.get(t - 1)], inc=sBld, dn=16)
                            h1a(t + 2)
                    if t + 1 < T:
                        set_batches(t + 1)
                    qm = qm_mark.get(t)
                    rqm = rqm_mark.get(t)
                    if sub < 1:
                        h1a_next()

                    if sub >= 1:
                        mask = mask0 if t == 0 else (maskE if t % 2 == 0 else maskO)
                        loc_m = None
                        scm_g = {}
                        expm_g = {}
                        mskm_g = {}
                        trm_g = {}
                        atTm_g = {}
                        avm_g = {}
                        SH = {g: ((0, 4 * g), (1, 4 * g + 2),
                                  (2, 4 * g + 1), (3, 4 * g + 3))
                              for g in range(4)}

                        def e_sc(g):
                            gb = g % 2
                            ps = psSC0 if gb == 0 else psSC1
                            for (j, h) in SH[g]:
                                pt = h // 2
                                pr = 64 * (h % 2)
                                lhq = q_fm[pr:pr + 64,
                                           slot * D + 128 * pt:
                                           slot * D + 128 * pt + 128]
                                for s_i, ktau in enumerate((t - 1, t)):
                                    ks = ktau % 3
                                    rhk = kr[pr:pr + 64,
                                             (pt * 3 + ks) * 128:
                                             (pt * 3 + ks) * 128 + 128]
                                    w = [qm, kv_rdy[t - 1], kv_rdy[t],
                                         psSC_free[gb]] if (j == 0 and s_i == 0) \
                                        else ()
                                    E("tensor", MM(
                                        ps[:, 256 * j + 128 * s_i:
                                           256 * j + 128 * s_i + 128],
                                        rhk, lhq, True, True), waits=w)
                            ctx.retro_inc("tensor", sPE)
                            scm_g[g] = sPE.cur()
                            if g == 3:
                                sc3[t] = scm_g[g]

                        def e_exp(g):
                            gb = g % 2
                            ps = psSC0 if gb == 0 else psSC1
                            atv = att[:, gb * D:(gb + 1) * D]
                            E("scalar", ACTV(atv, ps[:, :], AF.Exp, scale=0.125),
                              waits=[scm_g[g], att_free[0][gb]], inc=sACT)
                            psSC_free[gb] = sACT.cur()
                            expm_g[g] = sACT.cur()

                        def e_mask(g):
                            gb = g % 2
                            atv = att[:, gb * D:(gb + 1) * D]
                            E("vector", TT(atv, atv, mask[:, :], ALU.mult),
                              waits=[expm_g[g], wldM], inc=sDVE)
                            mskm_g[g] = sDVE.cur()

                        def e_av(g):
                            gb = g % 2
                            psw = psW0 if gb == 0 else psW1
                            atT = att[:, gb * D:(gb + 1) * D]
                            for (j, h) in SH[g]:
                                for s_i, ktau in enumerate((t - 1, t)):
                                    w = [mskm_g[g], psW_free[gb]] \
                                        if (j == 0 and s_i == 0) else ()
                                    E("tensor", MM(
                                        psw[:, 256 + 64 * j:256 + 64 * j + 64],
                                        atT[:, 256 * j + 128 * s_i:
                                            256 * j + 128 * s_i + 128],
                                        vr[:, 1024 * (ktau % 3) + 64 * h:
                                           1024 * (ktau % 3) + 64 * h + 64],
                                        s_i == 0, s_i == 1), waits=w)
                                for s_i in range(2):
                                    E("tensor", MM(
                                        psw[:, 2 * j:2 * j + 1],
                                        atT[:, 256 * j + 128 * s_i:
                                            256 * j + 128 * s_i + 128],
                                        ones_bf[:, 0:1], s_i == 0,
                                        (s_i == 1) and t != 0))
                                if t == 0:
                                    E("tensor", MM(psw[:, 2 * j:2 * j + 1],
                                                   z_row[0:1, :],
                                                   ones_bf[0:1, 0:1],
                                                   False, True))
                            ctx.retro_inc("tensor", sPE)
                            avm_g[g] = sPE.cur()
                            if g == 3:
                                av3[t] = avm_g[g]
                            att_free[0][gb] = avm_g[g]

                        def e_loc(g):
                            gb = g % 2
                            psw = psW0 if gb == 0 else psW1
                            for (j, h) in SH[g]:
                                w = [avm_g[g]] if j == 0 else ()
                                E("vector", RECIP(recip[:, h:h + 1],
                                                  psw[:, 2 * j:2 * j + 1]),
                                  waits=w, inc=sDVE if j == 3 else None)
                            rcp_m = sDVE.cur()
                            for (j, h) in SH[g]:
                                w = [loc_free[0], rcp_m] if j == 0 else ()
                                E("vector", TS(loc[:, 64 * h:64 * h + 64],
                                               psw[:, 256 + 64 * j:
                                                   256 + 64 * j + 64],
                                               recip[:, h:h + 1], None,
                                               ALU.mult),
                                  waits=w, inc=sDVE if j == 3 else None)
                            psW_free[gb] = sDVE.cur()
                            return sDVE.cur()

                        if lsub >= 3:
                            e_sc(0)
                            e_sc(1)
                            e_exp(0)
                            e_mask(0)
                            e_exp(1)
                            e_mask(1)
                            nb()
                            e_av(0)
                            e_sc(2)
                            e_exp(2)
                            e_mask(2)
                            h1a_next()
                            nb()
                            e_av(1)
                            e_sc(3)
                            e_exp(3)
                            e_mask(3)
                            loc_m = e_loc(0)
                            e_av(2)
                            loc_m = e_loc(1)
                            e_av(3)
                            loc_m = e_loc(2)
                            loc_m = e_loc(3)
                        else:
                            h1a_next()
                            for g in range(4):
                                gb = g % 2
                                e_sc(g)
                                e_exp(g)
                                psSC_free[gb] = expm_g[g]
                                att_free[0][gb] = expm_g[g]
                                loc_m = expm_g[g]
                                if lsub >= 1:
                                    e_mask(g)
                                    loc_m = mskm_g[g]
                                if lsub >= 2:
                                    e_av(g)
                                    psW_free[gb] = avm_g[g]
                                    loc_m = avm_g[g]
                        if dbg and t == 0 and lsub >= 3:
                            E("vector", VCOPY(snap_recip[:, :],
                                              recip[:, :]), waits=[loc_m],
                              inc=sDVE)
                            E("vector", VCOPY(snap_loc[:, :], loc[:, :]),
                              inc=sDVE)
                            E("sync", DMA(dbg_d["d_recip"],
                                          snap_recip[:, :]),
                              waits=[sDVE.cur()], inc=sOUT, dn=16)
                            E("sync", DMA(dbg_d["d_loc"], snap_loc[:, :]),
                              waits=[sDVE.cur()], inc=sOUT, dn=16)

                    if sub >= 2:
                        for h in range(LH):
                            psL = psW0 if h % 2 == 0 else psW1
                            lcol = 66 * (h // 2)
                            pb = 64 * (h % 2)
                            w = [rqm, rkrv_m[0], psW_free[0], psW_free[1],
                                 wldM] if h == 0 else ()
                            E("tensor", MM(psL[:, lcol:lcol + 65],
                                           ones_bf[pb:pb + 1, :],
                                           lmask[pb:pb + 1, 65 * t:65 * t + 65],
                                           True, False), waits=w)
                            E("tensor", MM(
                                psL[:, lcol:lcol + 64],
                                rq_fm[pb:pb + 64,
                                      slot * 256 + 128 * (h // 2):
                                      slot * 256 + 128 * (h // 2) + 128],
                                rk_fm[pb:pb + 64,
                                      64 * (h // 2):64 * (h // 2) + 64],
                                False, True))
                        ctx.retro_inc("tensor", sPE)
                        lsm = sPE.cur()
                        lsm_mark[t] = lsm
                        for h in range(LH):
                            psL = psW0 if h % 2 == 0 else psW1
                            lcol = 66 * (h // 2)
                            w = [lsm, latt_free[0]] if h == 0 else ()
                            E("scalar", ACTV(latt[:, 66 * h:66 * h + 65],
                                             psL[:, lcol:lcol + 65], AF.Exp,
                                             scale=0.125, accum=lsum[:, h:h + 1]),
                              waits=w, inc=sACT if h == LH - 1 else None)
                        psW_free[0] = sACT.cur()
                        psW_free[1] = sACT.cur()
                        lexm = sACT.cur()
                        nb()
                        E("vector", RECIP(lrecip[:, :], lsum[:, :]),
                          waits=[lexm], inc=sDVE)
                        lrc_m = sDVE.cur()
                        for h in range(LH):
                            w = [lattn_free[0], lrc_m] if h == 0 else ()
                            E("vector", TS(lattn[:, 64 * h:64 * h + 64],
                                           latt[:, 66 * h:66 * h + 64],
                                           lrecip[:, h:h + 1], None, ALU.mult),
                              waits=w, inc=sDVE if h == LH - 1 else None)
                        latm = sDVE.cur()
                        latt_free[0] = latm
                        for h in range(LH):
                            w = [latm, psT_free[1]] if h == 0 else ()
                            E("tensor", TRP(psT1[0:64, 128 * h:128 * h + 128],
                                            lattn[:, 64 * h:64 * h + 64],
                                            id_bf[:, :]), waits=w)
                        ctx.retro_inc("tensor", sPE)
                        ltrm = sPE.cur()
                        lattn_free[0] = ltrm
                        nb()
                        E("vector", VCOPY(lattnT[0:64, :], psT1[0:64, 0:512]),
                          waits=[ltrm, lattnT_free[0]], inc=sDVE)
                        psT_free[1] = sDVE.cur()
                        latTm = sDVE.cur()
                        for h in range(LH):
                            w = [latTm, psW_free[1]] if h == 0 else ()
                            E("tensor", MM(
                                psW1[64 * (h % 2):64 * (h % 2) + 64,
                                     128 * (h // 2):128 * (h // 2) + 128],
                                rv_lm[0:64, 64 * h:64 * h + 64],
                                lattnT[0:64, 128 * h:128 * h + 128],
                                True, True), waits=w)
                        ctx.retro_inc("tensor", sPE)
                        lavm = sPE.cur()
                        lattnT_free[0] = lavm
                        E("scalar", ACTV(lat_fm[:, :], psW1[:, 0:256], AF.Copy),
                          waits=[lavm, latfm_free[0]], inc=sACT)
                        psW_free[1] = sACT.cur()
                        latfm_m = sACT.cur()
                        nb()
                        for n in range(2):
                            for c in range(2):
                                w = [latfm_m, wldRO, psSC_free[1]] \
                                    if (n == 0 and c == 0) else ()
                                E("tensor", MM(
                                    psSC1[:, 512 * n:512 * n + 512],
                                    lat_fm[:, 128 * c:128 * c + 128],
                                    wrout[:, 1024 * c + 512 * n:
                                          1024 * c + 512 * n + 512],
                                    c == 0, False), waits=w)
                            E("tensor", MM(psSC1[:, 512 * n:512 * n + 512],
                                           ones_bf[0:1, :],
                                           routb_row[0:1, 512 * n:512 * n + 512],
                                           False, True))
                        ctx.retro_inc("tensor", sPE)
                        remm = sPE.cur()
                        latfm_free[0] = remm
                        E("scalar", ACTV(rem[:, :], psSC1[:, :], AF.Copy),
                          waits=[remm, rem_free[0]], inc=sACT)
                        psSC_free[1] = sACT.cur()
                        rem_m = sACT.cur()
                        nb()

                    if sub >= 3:
                        E("vector", TT(gd[:, :], loc[:, :], glb[:, :], ALU.mult),
                          waits=[loc_m, wldM], inc=sDVE)
                        E("vector", RED3(g1[:, :],
                                         gd[:, :].rearrange("p (h d) -> p h d", d=64)),
                          waits=[sDVE.cur()], inc=sDVE)
                        E("vector", TT(gdB[:, :], rem[:, :], grb[:, :], ALU.mult),
                          waits=[rem_m], inc=sDVE)
                        E("vector", RED3(g2[:, :],
                                         gdB[:, :].rearrange("p (h d) -> p h d", d=64)),
                          waits=[sDVE.cur()], inc=sDVE)
                        E("vector", TT(g1[:, :], g1[:, :], g2[:, :], ALU.add),
                          waits=[gex_free[0], sDVE.cur()], inc=sDVE)
                        glm = sDVE.cur()
                        E("scalar", ACTV(gex[:, :], g1[:, :], AF.Exp, scale=-1.0),
                          waits=[glm], inc=sACT)
                        E("scalar", ACTV(gex[:, :], gex[:, :], AF.Identity,
                                         bias=1.0), waits=[sACT.cur()],
                          inc=sACT)
                        E("vector", RECIP(g_tm[:, :], gex[:, :]),
                          waits=[sACT.cur()], inc=sDVE)
                        gex_free[0] = sDVE.cur()
                        grc_m = sDVE.cur()
                        E("gpsimd", TT(dif[:, :], loc[:, :], rem[:, :], ALU.subtract),
                          waits=[loc_m, rem_m, mix_free[0]], inc=sPL)
                        difm = sPL.cur()
                        loc_free[0] = difm
                        for h in range(NH):
                            w = [difm, grc_m] if h == 0 else ()
                            E("vector", STT(mixed[:, 64 * h:64 * h + 64],
                                            dif[:, 64 * h:64 * h + 64],
                                            g_tm[:, h:h + 1],
                                            rem[:, 64 * h:64 * h + 64],
                                            ALU.mult, ALU.add),
                              waits=w, inc=sDVE if h == NH - 1 else None)
                        mixm = sDVE.cur()
                        rem_free[0] = mixm
                        tm, mfm_m = fm_transposes(mixed_fm, mixed,
                                                  [mixm, psT_free[0]],
                                                  [mfm_free[0]])
                        mix_free[0] = tm
                        for n in range(2):
                            for kc in range(KC):
                                w = [mfm_m, wldO, psSC_free[1]] \
                                    if (n == 0 and kc == 0) else ()
                                E("tensor", MM(
                                    psSC1[:, 512 * n:512 * n + 512],
                                    mixed_fm[:, 128 * kc:128 * kc + 128],
                                    wout[:, 1024 * kc + 512 * n:
                                         1024 * kc + 512 * n + 512],
                                    kc == 0, False), waits=w)
                            E("tensor", MM(psSC1[:, 512 * n:512 * n + 512],
                                           ones_bf[0:1, :],
                                           outb_row[0:1, 512 * n:512 * n + 512],
                                           False, True))
                        ctx.retro_inc("tensor", sPE)
                        opm = sPE.cur()
                        mfm_free[0] = opm
                        xmv = xmid[:, slot * D:(slot + 1) * D]
                        E("vector", TT(xmv, psSC1[:, :], xv, ALU.add),
                          waits=[opm, xm_free.get(t - 2), sXMs.at(16 * (t - 1))],
                          inc=sDVE)
                        psSC_free[1] = sDVE.cur()
                        xm_m = sDVE.cur()
                        b_free[t] = xm_m
                        E("scalar", DMA(xmid_d[t * TOK:(t + 1) * TOK, :], xmv),
                          waits=[xm_m], inc=sXMs, dn=16)
                        if dbg and t == 0:
                            for nm, buf, wt in [
                                    ("d_kr", kr[:, :], kv_rdy[0]),
                                    ("d_vr", vr[:, :], kv_rdy[0]),
                                    ("d_qfm", q_fm[:, :], qm),
                                    ("d_ybf", y_bf[:, :], kv_rdy[0]),
                                    ("d_att", att[:, :], loc_m),
                                    ("d_rem", rem[:, :], rem_m),
                                    ("d_gtm", g_tm[:, :], mixm),
                                    ("d_mixed", mixed[:, :], mixm),
                                    ("d_lattn", lattn[:, :], latm),
                                    ("d_rkfm", rk_fm[:, :], rkrv_m[0]),
                                    ("d_rvlm", rv_lm[0:64, :], rkrv_m[0]),
                                    ("d_latfm", lat_fm[:, :], latfm_m)]:
                                E("sync", DMA(dbg_d[nm], buf), waits=[wt],
                                  inc=sOUT, dn=16)
                        xm_sp[t] = xm_m

                    else:
                        xmv = xmid[:, slot * D:(slot + 1) * D]
                        E("vector", VCOPY(xmv, xv), waits=[xm_free.get(t - 2), sXMs.at(16 * (t - 1))], inc=sDVE)
                        xm_m = sDVE.cur()
                        b_free[t] = xm_m
                        E("scalar", DMA(xmid_d[t * TOK:(t + 1) * TOK, :], xmv), waits=[xm_m], inc=sXMs, dn=16)
                        xm_free[t] = xm_m
                    flush_b()
                    if sub >= 3 and t >= 1:
                        emit_ln2(t - 1)
                if sub >= 3:
                    emit_ln2(T - 1)
                y2m = sDVE.cur()
                E("tensor", TRP(psW0[0:1, 0:128], y2mx[:, 0:1], id_f32[:, :]),
                  waits=[y2m, psW_free[0], wldC], inc=sPE)
                E("scalar", SCOPY(m128[0:1, :], psW0[0:1, 0:128]),
                  waits=[sPE.cur()], inc=sACT)
                E("sync", DMA(ar1_in, m128[0:1, :]), waits=[sACT.cur()],
                  inc=sPC, dn=16)
                E("gpsimd", lambda e: e.collective_compute(
                    "AllReduce", ALU.max, replica_groups=G8,
                    ins=[ar1_in.opt()], outs=[ar1_out.opt()]),
                  waits=[sPC.cur()], inc=sAR1)

            if stage == 0:
                E("sync", DMA(out_d, x_d), waits=[sAG.cur()], inc=sOUT, dn=16)
                E("sync", NOP(), waits=[sOUT.cur()])
        ctx.run_block()

        wdT = psb("wdT", [128, NIT * 1024], BF16)
        swd_bc = psb("swd_bc", [128, D], F32)
        swd_cols = psb("swd_cols", [128, 8], F32)
        swd_row = psb("swd_row", [1, D], F32)
        rsw_wd = psb("rsw_wd", [128, 1], F32)

        if stage >= 2:
            with ExitStack() as B:
                def bsb(name, shape, dt):
                    return B.enter_context(nc.sbuf_tensor(name, shape, dt))

                xq_fm = bsb("xq_fm", [128, KC * 2048], BF16)
                xm_t = bsb("xm_t", [128, 2 * D], F32)
                y2 = bsb("y2", [128, D], F32)
                wd_st = bsb("wd_st", [128, DI], F32)
                wdq_bf = bsb("wdq_bf", [128, DI], BF16)
                xq_i32 = bsb("xq_i32", [128, D], I32)
                xq_bf = bsb("xq_bf", [128, D], BF16)
                wg_st = bsb("wg_st", [128, 2 * D], F32)
                wu_st = bsb("wu_st", [128, 2 * D], F32)
                w_scr = bsb("w_scr", [128, D], F32)
                w_i32 = bsb("w_i32", [128, D], I32)
                wqg_bf = bsb("wqg_bf", [128, 2 * D], BF16)
                wqu_bf = bsb("wqu_bf", [128, 2 * D], BF16)
                wgT = bsb("wgT", [128, 2 * D], BF16)
                wuT = bsb("wuT", [128, 2 * D], BF16)
                amx = bsb("amx", [128, 4], F32)
                w_act1 = bsb("w_act1", [128, 2], F32)
                amx2 = bsb("amx2", [128, 2], F32)
                swsx_g3 = bsb("swsx_g3", [128, 3], F32)
                swsx_u3 = bsb("swsx_u3", [128, 3], F32)
                rsw = bsb("rsw", [128, 2], F32)
                silu_sb = bsb("silu_sb", [128, 1024], F32)
                h_sb = bsb("h_sb", [128, 2 * 2048], F32)
                htmp = bsb("htmp", [128, 1], F32)
                htmp2 = bsb("htmp2", [128, 1], F32)

                psG = [B.enter_context(nc.psum_tensor(f"psG{i}", [128, 512], F32))
                       for i in range(4)]
                psTf = B.enter_context(nc.psum_tensor("psTf", [128, 1024], BF16))
                psTg = B.enter_context(nc.psum_tensor("psTg", [128, 1024], BF16))
                psTu = B.enter_context(nc.psum_tensor("psTu", [128, 1024], BF16))
                psM = B.enter_context(nc.psum_tensor("psM", [128, 512], F32))

                psG_free = [None, None, None, None]
                silu_free = [None, None, None, None]
                hm_marks = {}
                wqm_marks = {}
                gum_marks = {}
                swsx_wm = {}
                qm_g = {}
                qm_u = {}
                qmark = {}
                trpg_marks = {}
                trp_marks = {}
                ev_g = {}
                ev_u = {}
                silm_last = {}
                mul_prev = [None]

                def emit_wq_q(i):
                    r = i % 2
                    E("sync", DMA(wg_st[:, r * D:r * D + D],
                                  wg_d[128 * i:128 * i + 128, :]),
                      waits=[qmark.get(i - 2)], inc=sFW, dn=16)
                    gld = sFW.cur()
                    E("scalar", DMA(wu_st[:, r * D:r * D + D],
                                    wu_d[128 * i:128 * i + 128, :]),
                      inc=sFW, dn=16)
                    uld = sFW.cur()
                    wsm = [None, None]
                    for c, (wv, ldm) in enumerate((
                            (wg_st[:, r * D:r * D + D], gld),
                            (wu_st[:, r * D:r * D + D], uld))):
                        ac2 = 2 * (i % 2) + c
                        E("vector", RMAXA(amx[:, ac2:ac2 + 1], wv),
                          waits=[ldm, mul_prev[0]], inc=sDVE)
                        E("vector", VCOPY(amx2[:, c:c + 1], amx[:, ac2:ac2 + 1]),
                          waits=[sDVE.cur()], inc=sDVE)
                        am = sDVE.cur()

                        def mk_mul(c):
                            return lambda e: e.mul(out=w_act1[:, c:c + 1],
                                                   in_=amx2[:, c:c + 1],
                                                   mul=R127)
                        E("scalar", mk_mul(c), waits=[am], inc=sACT)
                        wsm[c] = sACT.cur()
                    mul_prev[0] = wsm[1]
                    for c, (wv, swsx, wq2) in enumerate((
                            (wg_st[:, r * D:r * D + D], swsx_g3, wqg_bf),
                            (wu_st[:, r * D:r * D + D], swsx_u3, wqu_bf))):
                        E("vector", RECIP(rsw[:, c:c + 1], w_act1[:, c:c + 1]),
                          waits=[wsm[c]], inc=sDVE)
                        rm = sDVE.cur()
                        E("vector", TS(w_scr[:, :], wv, rsw[:, c:c + 1], QHI,
                                       ALU.mult, ALU.min), waits=[rm])
                        E("vector", TS(w_i32[:, :], w_scr[:, :], QLO, None,
                                       ALU.max))
                        E("vector", VCOPY(wq2[:, r * D:r * D + D],
                                          w_i32[:, :]),
                          waits=[trp_marks.get(i - 2)], inc=sDVE)
                        (qm_g if c == 0 else qm_u)[i] = sDVE.cur()
                    qmark[i] = qm_u[i]

                def emit_trp(i):
                    r = i % 2
                    for c, (wq2, psT, qm, evp) in enumerate((
                            (wqg_bf, psTg, qm_g[i], ev_g.get(i - 1)),
                            (wqu_bf, psTu, qm_u[i], ev_u.get(i - 1)))):
                        for f in range(8):
                            w = [qm, evp] if f == 0 else ()
                            E("tensor", TRP(psT[:, 128 * f:128 * f + 128],
                                            wq2[:, r * D + 128 * f:
                                                r * D + 128 * f + 128],
                                            id_bf[:, :]), waits=w)
                        ctx.retro_inc("tensor", sPE)
                        (trpg_marks if c == 0 else trp_marks)[i] = sPE.cur()

                def emit_swsx(i):
                    s3 = i % 3
                    for c, swsx in ((0, swsx_g3), (1, swsx_u3)):
                        ac2 = 2 * (i % 2) + c
                        E("vector", TS(swsx[:, s3:s3 + 1], amx[:, ac2:ac2 + 1],
                                       R127, sxb[:, 0:1], ALU.mult, ALU.mult),
                          waits=[sxbm, silm_last.get(i - 3)], inc=sDVE)
                    swsx_wm[i] = sDVE.cur()

                def emit_ev(i):
                    r = i % 2
                    E("vector", VCOPY(wgT[:, r * D:r * D + D], psTg[:, :]),
                      waits=[trpg_marks[i], gum_marks.get(i - 2)], inc=sDVE)
                    ev_g[i] = sDVE.cur()
                    E("vector", VCOPY(wuT[:, r * D:r * D + D], psTu[:, :]),
                      waits=[trp_marks[i]], inc=sDVE)
                    ev_u[i] = sDVE.cur()
                    wqm_marks[i] = ev_u[i]
                emit_wq_q(0)
                emit_trp(0)
                emit_ev(0)
                emit_wq_q(1)
                emit_trp(1)
                emit_ev(1)

                E("sync", DMA(m128[0:1, :], ar1_out), waits=[sAR1.cur()],
                  inc=sCCR, dn=16)
                E("vector", RMAX(sc[0:1, 8:9], m128[0:1, :]),
                  waits=[sCCR.cur()], inc=sDVE)
                E("vector", VCOPY(sc[0:1, 0:1], sc[0:1, 8:9]),
                  waits=[sDVE.cur()], inc=sDVE)
                gmm = sDVE.cur()
                E("scalar", ACTV(sc[0:1, 1:2], clipx[0:1, 0:1], AF.Sigmoid),
                  inc=sACT)
                E("scalar", ACTV(sc[0:1, 2:3], sc[0:1, 1:2], AF.Identity,
                                 scale=sc[0:1, 0:1]),
                  waits=[gmm, sACT.cur()], inc=sACT)
                E("scalar", lambda e: e.mul(out=sc[0:1, 2:3],
                                            in_=sc[0:1, 2:3], mul=R127),
                  waits=[sACT.cur()], inc=sACT)
                sxactm = sACT.cur()
                E("vector", RECIP(sc[0:1, 3:4], sc[0:1, 2:3]),
                  waits=[sxactm], inc=sDVE)
                E("vector", VCOPY(sc[0:1, 9:10], sc[0:1, 3:4]),
                  waits=[sDVE.cur()], inc=sDVE)
                sxm = sDVE.cur()
                E("tensor", MM(psM[:, 0:1], ones_f32[0:1, :], sc[0:1, 2:3],
                               True, True), waits=[sxm])
                E("tensor", MM(psM[:, 2:3], ones_f32[0:1, :], sc[0:1, 9:10],
                               True, True), inc=sPE)
                E("vector", VCOPY(sxb[:, 0:1], psM[:, 0:1]), waits=[sPE.cur()])
                E("vector", VCOPY(rsxb[:, 0:1], psM[:, 2:3]), inc=sDVE)
                sxbm = sDVE.cur()
                psM_free = [sxbm]
                emit_swsx(0)
                emit_swsx(1)

                psTf_free = [None]
                xq_marks = {}
                trm_prev = [None]
                for t in range(T):
                    r = t % 2
                    xmv = xm_t[:, r * D:(r + 1) * D]
                    E("sync", DMA(xmv, xmid_d[t * TOK:(t + 1) * TOK, :]),
                      waits=[sXMs.at(16 * (t + 1)), xq_marks.get(t - 2)],
                      inc=sXMl, dn=16)
                    ldm = sXMl.cur()
                    E("vector", TS(y2[:, :], xmv, nm2[:, t:t + 1],
                                   rs2[:, t:t + 1], ALU.add, ALU.mult),
                      waits=[ldm, sxbm])
                    E("vector", TS(y2[:, :], y2[:, :], rsxb[:, 0:1], QHI,
                                   ALU.mult, ALU.min))
                    E("vector", TS(xq_i32[:, :], y2[:, :], QLO, None, ALU.max))
                    E("vector", VCOPY(xq_bf[:, :], xq_i32[:, :]),
                      waits=[trm_prev[0]], inc=sDVE)
                    xqbm = sDVE.cur()
                    if dbg and t == 0:
                        E("vector", VCOPY(silu_sb[:, 0:1024], y2[:, :]),
                          inc=sDVE)
                        E("sync", DMA(dbg_d["d_y2"], silu_sb[:, 0:1024]),
                          waits=[sDVE.cur()], inc=sOUT, dn=16)
                        E("sync", DMA(dbg_d["d_nm2"], nm2[:, :]),
                          waits=[sDVE.cur()], inc=sOUT, dn=16)
                        E("sync", DMA(dbg_d["d_rs2"], rs2[:, :]),
                          waits=[sDVE.cur()], inc=sOUT, dn=16)
                    for f in range(8):
                        w = [xqbm, psTf_free[0]] if f == 0 else ()
                        E("tensor", TRP(psTf[:, 128 * f:128 * f + 128],
                                        xq_bf[:, 128 * f:128 * f + 128],
                                        id_bf[:, :]), waits=w)
                    ctx.retro_inc("tensor", sPE)
                    trm = sPE.cur()
                    trm_prev[0] = trm
                    E("vector", VCOPY(
                        xq_fm[:, :].rearrange("p (f n) -> p f n", n=2048)[
                            :, :, 128 * t:128 * t + 128],
                        psTf[:, :].rearrange("p (f n) -> p f n", n=128)),
                      waits=[trm], inc=sDVE)
                    psTf_free[0] = sDVE.cur()
                    xq_marks[t] = sDVE.cur()
                xq_all = sDVE.cur()

                wd_j = [0]
                wdT_marks = []
                wd_pe_prev = [None]

                def emit_wd_prep(j):
                    E("sync", DMA(wd_st[:, :], wd_d[128 * j:128 * j + 128, :]),
                      waits=[wdT_marks[j - 1] if j > 0 else None], inc=sFW, dn=16)
                    ldm = sFW.cur()
                    E("vector", RMAXA(swd_cols[:, j:j + 1], wd_st[:, :]),
                      waits=[ldm], inc=sDVE)
                    E("vector", VCOPY(amx2[:, 0:1], swd_cols[:, j:j + 1]),
                      waits=[sDVE.cur()], inc=sDVE)
                    E("scalar", lambda e: e.mul(out=w_act1[:, 0:1],
                                                in_=amx2[:, 0:1], mul=R127),
                      waits=[sDVE.cur()], inc=sACT)
                    E("vector", RECIP(rsw_wd[:, 0:1], w_act1[:, 0:1]),
                      waits=[sACT.cur()], inc=sDVE)
                    rwm = sDVE.cur()
                    for cc in range(3):
                        wdt = 1024 if cc < 2 else DI - 2048
                        E("vector", TS(w_scr[:, 0:wdt],
                                       wd_st[:, 1024 * cc:1024 * cc + wdt],
                                       rsw_wd[:, 0:1], QHI, ALU.mult, ALU.min),
                          waits=[wd_pe_prev[0], rwm] if cc == 0 else ())
                        E("vector", TS(w_i32[:, 0:wdt], w_scr[:, 0:wdt], QLO,
                                       None, ALU.max))
                        E("vector", VCOPY(wdq_bf[:, 1024 * cc:1024 * cc + wdt],
                                          w_i32[:, 0:wdt]))
                    ctx.retro_inc("vector", sDVE)
                    qm = sDVE.cur()
                    for i in range(NIT):
                        half = i % 8
                        w = [qm, psTf_free[0]] if half == 0 else ()
                        E("tensor", TRP(psTf[:, 128 * half:128 * half + 128],
                                        wdq_bf[:, 128 * i:128 * i + 128],
                                        id_bf[:, :]), waits=w)
                        if half == 7 or i == NIT - 1:
                            ctx.retro_inc("tensor", sPE)
                            tm = sPE.cur()
                            lo = i - half
                            E("vector", VCOPY(
                                wdT[:, :].rearrange("p (i n) -> p i n", n=1024)[
                                    :, lo:lo + half + 1, 128 * j:128 * j + 128],
                                psTf[:, 0:(half + 1) * 128].rearrange(
                                    "p (i n) -> p i n", n=128)),
                              waits=[tm], inc=sDVE)
                            psTf_free[0] = sDVE.cur()
                    wd_pe_prev[0] = sPE.cur()
                    wdT_marks.append(sDVE.cur())

                for i in range(NIT):
                    r = i % 2
                    if i + 2 < NIT:
                        emit_wq_q(i + 2)
                        emit_swsx(i + 2)
                    wq_m = wqm_marks[i]
                    hv = h_sb[:, r * 2048:r * 2048 + 2048]
                    h_first = [True]
                    for n in range(4):
                        pg = psG[(2 * n) % 4]
                        pu = psG[(2 * n + 1) % 4]
                        for kc in range(KC):
                            w = [wq_m, xq_marks[min(4 * n + 3, T - 1)],
                                 psG_free[(2 * n) % 4]] \
                                if kc == 0 else ()
                            E("tensor", MM(
                                pg[:, :],
                                wgT[:, r * D + 128 * kc:r * D + 128 * kc + 128],
                                xq_fm[:, 2048 * kc + 512 * n:
                                      2048 * kc + 512 * n + 512],
                                kc == 0, kc == KC - 1), waits=w)
                        ctx.retro_inc("tensor", sPE)
                        gmm2 = sPE.cur()
                        for kc in range(KC):
                            w = [psG_free[(2 * n + 1) % 4]] if kc == 0 else ()
                            E("tensor", MM(
                                pu[:, :],
                                wuT[:, r * D + 128 * kc:r * D + 128 * kc + 128],
                                xq_fm[:, 2048 * kc + 512 * n:
                                      2048 * kc + 512 * n + 512],
                                kc == 0, kc == KC - 1), waits=w)
                        ctx.retro_inc("tensor", sPE)
                        umm = sPE.cur()
                        s3 = i % 3
                        sl = n % 2
                        E("scalar", ACTV(silu_sb[:, 512 * sl:512 * sl + 512],
                                         pg[:, :], AF.Silu,
                                         scale=swsx_g3[:, s3:s3 + 1]),
                          waits=[gmm2, silu_free[sl], swsx_wm[i]], inc=sACT)
                        psG_free[(2 * n) % 4] = sACT.cur()
                        silm = sACT.cur()
                        silm_last[i] = silm
                        w_extra = [sHw.at(16 * (i - 1))] if h_first[0] else []
                        h_first[0] = False
                        E("vector", STT(hv[:, 512 * n:512 * n + 512], pu[:, :],
                                        swsx_u3[:, s3:s3 + 1],
                                        silu_sb[:, 512 * sl:512 * sl + 512],
                                        ALU.mult, ALU.mult),
                          waits=[umm, silm] + w_extra, inc=sDVE)
                        psG_free[(2 * n + 1) % 4] = sDVE.cur()
                        silu_free[sl] = sDVE.cur()
                    gum_marks[i] = sPE.cur()
                    hm = sDVE.cur()
                    if i + 2 < NIT:
                        emit_trp(i + 2)
                        emit_ev(i + 2)
                    E("vector", RMAXA(htmp[:, 0:1], hv), waits=[hm],
                      inc=sDVE)
                    E("vector", TT(hmx[:, :], hmx[:, :], htmp[:, 0:1],
                                   ALU.max), waits=[sDVE.cur()], inc=sDVE)
                    hm_marks[i] = sDVE.cur()
                    E("scalar", DMA(h_d[128 * i:128 * i + 128, :], hv),
                      waits=[hm], inc=sHw, dn=16)
                    if i % 3 == 2 and wd_j[0] < 8:
                        emit_wd_prep(wd_j[0])
                        wd_j[0] += 1
                while wd_j[0] < 8:
                    emit_wd_prep(wd_j[0])
                    wd_j[0] += 1
                wdT_done[0] = wdT_marks[-1]
                if dbg:
                    E("sync", DMA(dbg_d["d_hsb"],
                                  h_sb[:, ((NIT - 1) % 2) * 2048:
                                       ((NIT - 1) % 2) * 2048 + 2048]),
                      waits=[hm_marks[NIT - 1]], inc=sOUT, dn=16)

                if dbg:
                    E("sync", DMA(dbg_d["d_xq"], xq_fm[:, 0:2048]),
                      waits=[xq_all], inc=sOUT, dn=16)
                    E("sync", DMA(dbg_d["d_swsxg"], swsx_g3[:, 0:1]),
                      waits=[sDVE.cur()], inc=sOUT, dn=16)
                    E("sync", DMA(dbg_d["d_hmx"], hmx[:, 0:1]),
                      waits=[sDVE.cur()], inc=sOUT, dn=16)
                hmxm = sDVE.cur()
                E("tensor", TRP(psM[0:1, 0:128], hmx[:, 0:1], id_f32[:, :]),
                  waits=[hmxm, psM_free[0]], inc=sPE)
                E("scalar", SCOPY(m128[0:1, :], psM[0:1, 0:128]),
                  waits=[sPE.cur()], inc=sACT)
                E("sync", DMA(ar2_in, m128[0:1, :]), waits=[sACT.cur()],
                  inc=sPC, dn=16)
                E("gpsimd", lambda e: e.collective_compute(
                    "AllReduce", ALU.max, replica_groups=G8,
                    ins=[ar2_in.opt()], outs=[ar2_out.opt()]),
                  waits=[sPC.cur()], inc=sAR2)

        if stage in (1, 2):
            E("sync", DMA(out_d, xmid_d), waits=[sXMs.at(16 * T)], inc=sOUT, dn=16)
            E("sync", NOP(), waits=[sOUT.cur()])
        ctx.run_block()

        if stage >= 3:
            with ExitStack() as C:
                def csb(name, shape, dt):
                    return C.enter_context(nc.sbuf_tensor(name, shape, dt))

                hrb = csb("hrb", [128, 2 * DI], F32)
                hq_scr = csb("hq_scr", [128, DI], F32)
                hq_i32 = csb("hq_i32", [128, DI], I32)
                hq_bf = csb("hq_bf", [128, 2 * DI], BF16)
                xm_t2 = csb("xm_t2", [128, 2 * D], F32)
                o1 = csb("o1", [128, D], F32)
                out_sb = csb("out_sb", [128, 2 * D], F32)

                psD0 = C.enter_context(nc.psum_tensor("psD0", [128, 1024], F32))
                psD1 = C.enter_context(nc.psum_tensor("psD1", [128, 1024], F32))
                psS = C.enter_context(nc.psum_tensor("psS", [128, 512], F32))
                psS2 = C.enter_context(nc.psum_tensor("psS2", [128, 512], F32))

                E("sync", DMA(m128[0:1, :], ar2_out), waits=[sAR2.cur()],
                  inc=sCCR, dn=16)
                E("vector", RMAX(sc[0:1, 12:13], m128[0:1, :]),
                  waits=[sCCR.cur()], inc=sDVE)
                E("vector", VCOPY(sc[0:1, 4:5], sc[0:1, 12:13]),
                  waits=[sDVE.cur()], inc=sDVE)
                gm2 = sDVE.cur()
                E("scalar", ACTV(sc[0:1, 5:6], clipd[0:1, 0:1], AF.Sigmoid),
                  inc=sACT)
                sig2 = sACT.cur()
                E("scalar", ACTV(sc[0:1, 6:7], sc[0:1, 5:6], AF.Identity,
                                 scale=sc[0:1, 4:5]), waits=[gm2, sig2],
                  inc=sACT)
                E("scalar", lambda e: e.mul(out=sc[0:1, 6:7],
                                            in_=sc[0:1, 6:7], mul=R127),
                  waits=[sACT.cur()], inc=sACT)
                E("vector", RECIP(sc[0:1, 7:8], sc[0:1, 6:7]),
                  waits=[sACT.cur()], inc=sDVE)
                E("vector", VCOPY(sc[0:1, 13:14], sc[0:1, 7:8]),
                  waits=[sDVE.cur()], inc=sDVE)
                shm = sDVE.cur()
                E("tensor", MM(psS[:, 0:1], ones_f32[0:1, :], sc[0:1, 13:14],
                               True, True), waits=[shm], inc=sPE)
                E("vector", VCOPY(rshb[:, 0:1], psS[:, 0:1]), waits=[sPE.cur()],
                  inc=sDVE)
                rshm = sDVE.cur()
                acm = [None]
                for j in range(8):
                    w = [sPE.cur()] if j == 0 else ([acm[0]] if j == 4 else ())
                    E("tensor", TRP(psS2[0:1, 128 * (j % 4):128 * (j % 4) + 128],
                                    swd_cols[:, j:j + 1], id_f32[:, :]), waits=w)
                    if j % 4 == 3:
                        ctx.retro_inc("tensor", sPE)
                        tm = sPE.cur()
                        lo = (j - 3) * 128
                        E("scalar", SCOPY(swd_row[0:1, lo:lo + 512],
                                          psS2[0:1, 0:512]), waits=[tm], inc=sACT)
                        acm[0] = sACT.cur()
                swdr = sACT.cur()
                E("vector", TS(swd_row[0:1, :], swd_row[0:1, :], R127,
                               sc[0:1, 6:7], ALU.mult, ALU.mult),
                  waits=[swdr, shm], inc=sDVE)
                swdm = sDVE.cur()
                E("tensor", MM(psS[:, 0:512], ones_f32[0:1, :],
                               swd_row[0:1, 0:512], True, True),
                  waits=[swdm, sDVE.cur()])
                E("tensor", MM(psS2[:, 0:512], ones_f32[0:1, :],
                               swd_row[0:1, 512:1024], True, True), inc=sPE)
                bcm = sPE.cur()
                E("vector", VCOPY(swd_bc[:, 0:512], psS[:, 0:512]), waits=[bcm])
                E("vector", VCOPY(swd_bc[:, 512:1024], psS2[:, 0:512]), inc=sDVE)
                swdbcm = sDVE.cur()

                if dbg:
                    E("sync", DMA(dbg_d["d_sc"], sc[0:1, :]),
                      waits=[shm], inc=sOUT, dn=16)
                    E("sync", DMA(dbg_d["d_swdrow"], swd_row[0:1, :]),
                      waits=[swdm], inc=sOUT, dn=16)
                    E("sync", DMA(dbg_d["d_rshb"], rshb[:, 0:1]),
                      waits=[rshm], inc=sOUT, dn=16)
                hq_marks = {}
                out_done = {}
                dmm_marks = {}
                hview = h_d.rearrange("(i p) q -> p i q", p=128)
                E("sync", DMA(hrb[:, 0:DI].rearrange("p (i n) -> p i n", n=128),
                              hview[:, :, 0:128]),
                  waits=[sHw.at(16 * NIT)], inc=sHr, dn=16)

                def emit_hq(t):
                    r = t % 2
                    if t + 1 < T:
                        nr = (t + 1) % 2
                        E("sync", DMA(
                            hrb[:, nr * DI:nr * DI + DI].rearrange(
                                "p (i n) -> p i n", n=128),
                            hview[:, :, 128 * (t + 1):128 * (t + 1) + 128]),
                          waits=[hq_marks.get(t - 1)], inc=sHr, dn=16)
                    hld = sHr.at(16 * (t + 1))
                    hv = hrb[:, r * DI:r * DI + DI]
                    hqv = hq_bf[:, r * DI:r * DI + DI]
                    E("vector", TS(hq_scr[:, :], hv, rshb[:, 0:1], QHI,
                                   ALU.mult, ALU.min), waits=[hld, rshm])
                    E("vector", TS(hq_i32[:, :], hq_scr[:, :], QLO, None,
                                   ALU.max))
                    E("vector", VCOPY(hqv, hq_i32[:, :]),
                      waits=[dmm_marks.get(t - 2)], inc=sDVE)
                    hq_marks[t] = sDVE.cur()

                emit_hq(0)
                for t in range(T):
                    r = t % 2
                    if t + 1 < T:
                        emit_hq(t + 1)
                    hqm = hq_marks[t]
                    hqv = hq_bf[:, r * DI:r * DI + DI]
                    if dbg and t == 0:
                        E("sync", DMA(dbg_d["d_hq"], hqv), waits=[hqm],
                          inc=sOUT, dn=16)
                    xmv = xm_t2[:, r * D:(r + 1) * D]
                    E("scalar", DMA(xmv, xmid_d[t * TOK:(t + 1) * TOK, :]),
                      waits=[out_done.get(t - 2)], inc=sXMl, dn=16)
                    xml = sXMl.cur()
                    psD = psD0 if r == 0 else psD1
                    for half in range(2):
                        for kc in range(NIT):
                            w = [hqm, wdT_done[0], out_done.get(t - 2)] \
                                if (half == 0 and kc == 0) else ()
                            E("tensor", MM(
                                psD[:, 512 * half:512 * half + 512],
                                hqv[:, 128 * kc:128 * kc + 128],
                                wdT[:, 1024 * kc + 512 * half:
                                    1024 * kc + 512 * half + 512],
                                kc == 0, kc == NIT - 1), waits=w)
                    ctx.retro_inc("tensor", sPE)
                    dmm_marks[t] = sPE.cur()
                    dmm = sPE.cur()
                    E("vector", TT(o1[:, :], psD[:, :], swd_bc[:, :], ALU.mult),
                      waits=[dmm, swdbcm, xml])
                    ov = out_sb[:, r * D:(r + 1) * D]
                    E("vector", TT(ov, o1[:, :], xmv, ALU.add),
                      waits=[sOUT.at(16 * (t - 1))], inc=sDVE)
                    out_done[t] = sDVE.cur()
                    E("sync", DMA(out_d[t * TOK:(t + 1) * TOK, :], ov),
                      waits=[out_done[t]], inc=sOUT, dn=16)
                E("sync", NOP(), waits=[sOUT.cur()])

        ctx.run_block()

    return nc


def kernel(L0_attn_x, L0_attn_qkv_w, L0_attn_qkv_b, L0_attn_rq_w, L0_attn_rq_b,
           L0_attn_rkv_w, L0_attn_rkv_b, L0_attn_rout_w, L0_attn_rout_b,
           L0_attn_out_w, L0_attn_out_b, L0_attn_gl_w, L0_attn_gr_w,
           L0_ffn_clip_x, L0_ffn_clip_down, L0_ffn_w_gate, L0_ffn_w_up,
           L0_ffn_w_down):
    X = np.ascontiguousarray(np.asarray(L0_attn_x, dtype=np.float32))
    assert X.shape == (2, 8192, 1024)

    def f32(a):
        return np.ascontiguousarray(np.asarray(a, dtype=np.float32))

    qkv_b = f32(L0_attn_qkv_b)
    rq_b = f32(L0_attn_rq_b)
    rkv_b = f32(L0_attn_rkv_b)
    gl_w = f32(L0_attn_gl_w)
    gr_w = f32(L0_attn_gr_w)

    qi = np.arange(128)[:, None]
    kj = np.arange(256)[None, :]
    band = (qi <= kj) & (kj < qi + 128)
    bandE = band.copy()
    bandE[0, 0] = False
    band0 = band & (kj >= 128)

    def mk_mask(b):
        bT = np.concatenate([b[:, 0:128].T, b[:, 128:256].T], axis=1)
        return np.ascontiguousarray(
            np.tile(bT.astype(np.float32), (1, 4)).astype(ml_dtypes.bfloat16))

    maskE_np = mk_mask(bandE)
    maskO_np = mk_mask(band)
    mask0_np = mk_mask(band0)

    glb_np = np.ascontiguousarray(
        np.broadcast_to(np.tile(gl_w[:, 0], 16)[None, :], (128, 1024))
        .astype(np.float32))
    grb_np = np.ascontiguousarray(
        np.broadcast_to(np.tile(gr_w[:, 0], 16)[None, :], (128, 1024))
        .astype(np.float32))

    shared = {
        "wqkv": f32(L0_attn_qkv_w), "wrq": f32(L0_attn_rq_w),
        "wrkv": f32(L0_attn_rkv_w), "wrout": f32(L0_attn_rout_w),
        "wout": f32(L0_attn_out_w),
        "qkvb_col": np.ascontiguousarray(qkv_b[:2048].reshape(16, 128).T),
        "qkvb_vrow": np.ascontiguousarray(qkv_b[2048:].reshape(1, 1024)),
        "rqb_col": np.ascontiguousarray(rq_b.reshape(2, 128).T),
        "rkvb_col": np.ascontiguousarray(rkv_b[:256].reshape(2, 128).T),
        "rvb_row": np.ascontiguousarray(rkv_b[256:].reshape(1, 256)),
        "routb_row": np.ascontiguousarray(f32(L0_attn_rout_b).reshape(1, 1024)),
        "outb_row": np.ascontiguousarray(f32(L0_attn_out_b).reshape(1, 1024)),
        "glb": glb_np, "grb": grb_np,
        "maskE": maskE_np, "maskO": maskO_np,
        "clipx": f32(L0_ffn_clip_x).reshape(1, 1),
        "clipd": f32(L0_ffn_clip_down).reshape(1, 1),
        "wg": f32(L0_ffn_w_gate), "wu": f32(L0_ffn_w_up),
        "wd": f32(L0_ffn_w_down),
    }

    in_maps = []
    for c in range(NCORES):
        b, j = divmod(c, 4)
        lo = j * 2048
        xs = np.ascontiguousarray(X[b, lo:lo + 2048, :])
        xh = (np.ascontiguousarray(X[b, lo - 128:lo, :]) if j > 0
              else np.zeros((128, 1024), np.float32))
        z = np.zeros((128, 1), np.float32)
        if j == 0:
            z[0, 0] = 1.0
        lm = np.full((T, 65), NEG, np.float32)
        g0 = 16 * j
        for t in range(T):
            cg = g0 + t
            lm[t, :min(cg, 64)] = 0.0
            lm[t, 64] = 0.0 if cg == 0 else NEG
        m = dict(shared)
        m["x"] = xs
        m["xh"] = xh
        m["z_col"] = z
        m["z_row"] = np.ascontiguousarray(z.reshape(1, 128))
        lmrow = lm.reshape(T * 65)
        lm65 = np.zeros((65, T * 65), np.float32)
        lm65[0, :] = lmrow
        lm65[64, :] = lmrow
        m["lmask"] = np.ascontiguousarray(lm65.astype(ml_dtypes.bfloat16))
        m["mask0"] = mask0_np if j == 0 else maskE_np
        in_maps.append(m)

    nc = build_nc()
    res = run_bass_kernel_spmd(nc, in_maps, core_ids=list(range(NCORES)))
    parts = [np.asarray(res.results[c]["out"]) for c in range(NCORES)]
    return np.concatenate(parts, axis=0).reshape(2, 8192, 1024).astype(
        np.float32)


# version 26
# speedup vs baseline: 1.1277x; 1.0611x over previous
import numpy as np
import ml_dtypes
import concourse.bass as bass
import concourse.mybir as mybir
from concourse.bass_utils import run_bass_kernel_spmd
from contextlib import ExitStack

F32 = mybir.dt.float32
BF16 = mybir.dt.bfloat16
I32 = mybir.dt.int32
AF = mybir.ActivationFunctionType
ALU = mybir.AluOpType
AX = mybir.AxisListType.X

NCORES = 8
T = 16
TOK = 128
D = 1024
NH = 16
HD = 64
LH = 4
DI = 2816
NIT = 22
KC = 8
NEG = -1.0e9
EPS = 1e-5
QHI = 127.49
QLO = -128.49
R1024 = 1.0 / 1024.0
R127 = 1.0 / 127.0
R128 = 1.0 / 128.0


def MM(o, l, r, s, st):
    return lambda e, o=o, l=l, r=r, s=s, st=st: e.matmul(o, l, r, start=s,
                                                         stop=st)


def TRP(o, i, ident):
    return lambda e, o=o, i=i, d=ident: e.transpose(o, i, d)


def ACTV(o, i, f, bias=0.0, scale=1.0, accum=None):
    return lambda e, o=o, i=i, f=f, b=bias, s=scale, a=accum: e.activation(
        out=o, in_=i, func=f, bias=b, scale=s, accum_out=a)


def SCOPY(o, i):
    return lambda e, o=o, i=i: e.copy(out=o, in_=i)


def VCOPY(o, i):
    return lambda e, o=o, i=i: e.tensor_copy(out=o, in_=i)


def TT(o, a, b, op):
    return lambda e, o=o, a=a, b=b, op=op: e.tensor_tensor(out=o, in0=a,
                                                           in1=b, op=op)


def TS(o, a, s1, s2, o0, o1=ALU.bypass):
    if s2 is None:
        return lambda e, o=o, a=a, s1=s1, o0=o0: e.tensor_scalar(
            out=o, in0=a, scalar1=s1, scalar2=None, op0=o0)
    return lambda e, o=o, a=a, s1=s1, s2=s2, o0=o0, o1=o1: e.tensor_scalar(
        out=o, in0=a, scalar1=s1, scalar2=s2, op0=o0, op1=o1)


def STT(o, a, s, b, o0, o1):
    return lambda e, o=o, a=a, s=s, b=b, o0=o0, o1=o1: e.scalar_tensor_tensor(
        out=o, in0=a, scalar=s, in1=b, op0=o0, op1=o1)


def RSUM(o, i, neg=False):
    return lambda e, o=o, i=i, n=neg: e.reduce_sum(out=o, in_=i, axis=AX,
                                                   negate=n)


def RMAXA(o, i):
    return lambda e, o=o, i=i: e.reduce_max(out=o, in_=i, axis=AX,
                                            apply_absolute_value=True)


def RMAX(o, i):
    return lambda e, o=o, i=i: e.reduce_max(out=o, in_=i, axis=AX)


def RED3(o, i):
    return lambda e, o=o, i=i: e.tensor_reduce(out=o, in_=i, axis=AX,
                                               op=ALU.add)


def RECIP(o, i):
    return lambda e, o=o, i=i: e.reciprocal(out=o, in_=i)


def DMA(o, i):
    return lambda e, o=o, i=i: e.dma_start(out=o, in_=i)


def MEMSET(o, v):
    return lambda e, o=o, v=v: e.memset(o, v)


def NOP():
    return lambda e: e.nop()


class Sem:
    def __init__(self, h):
        self.h = h
        self.n = 0

    def cur(self):
        return (self, self.n)

    def at(self, v):
        return (self, v)


class Ctx:
    def __init__(self, nc):
        self.nc = nc
        self.ops = []

    def retro_inc(self, eng, sem, dn=1):
        for i in range(len(self.ops) - 1, -1, -1):
            if self.ops[i][0] == eng:
                self.ops[i][3].append((sem, dn))
                sem.n += dn
                return
        raise RuntimeError("no prior op on " + eng)

    def emit(self, eng, fn, waits=(), inc=None, dn=1):
        incs = [(inc, dn)] if inc is not None else []
        self.ops.append((eng, fn, [w for w in waits if w is not None], incs))
        if inc is not None:
            inc.n += dn

    def run_block(self):
        ops = self.ops
        with self.nc.Block() as block:
            def mk(name):
                def f(eng):
                    waited = {}
                    for (e, fn, waits, incs) in ops:
                        if e != name:
                            continue
                        for (sem, val) in waits:
                            if val <= 0 or waited.get(id(sem), -1) >= val:
                                continue
                            eng.wait_ge(sem.h, val)
                            waited[id(sem)] = val
                        inst = fn(eng)
                        for (sem, v) in incs:
                            inst = inst.then_inc(sem.h, v)
                return f
            block.sync(mk("sync"))
            block.tensor(mk("tensor"))
            block.vector(mk("vector"))
            block.scalar(mk("scalar"))
            block.gpsimd(mk("gpsimd"))
        self.ops = []


def build_nc(stage=3, sub=3, lsub=3, dbg=False):
    nc = bass.Bass("TRN2", target_bir_lowering=False)

    def din(name, shape, dt=F32):
        return nc.dram_tensor(name, shape, dt, kind="ExternalInput").ap()

    x_d = din("x", [T * TOK, D])
    xh_d = din("xh", [TOK, D])
    wqkv_d = din("wqkv", [D, 3 * D])
    wrq_d = din("wrq", [D, 256])
    wrkv_d = din("wrkv", [D, 512])
    wrout_d = din("wrout", [256, D])
    wout_d = din("wout", [D, D])
    qkvb_col_d = din("qkvb_col", [128, 16])
    qkvb_vrow_d = din("qkvb_vrow", [1, D])
    rqb_col_d = din("rqb_col", [128, 2])
    rkvb_col_d = din("rkvb_col", [128, 2])
    rvb_row_d = din("rvb_row", [1, 256])
    routb_row_d = din("routb_row", [1, D])
    outb_row_d = din("outb_row", [1, D])
    glb_d = din("glb", [128, D])
    grb_d = din("grb", [128, D])
    mask0_d = din("mask0", [128, 1024], BF16)
    maskE_d = din("maskE", [128, 1024], BF16)
    maskO_d = din("maskO", [128, 1024], BF16)
    z_col_d = din("z_col", [128, 1])
    zrow_d = din("z_row", [1, 128])
    lmask_d = din("lmask", [65, T * 65], BF16)
    clipx_d = din("clipx", [1, 1])
    clipd_d = din("clipd", [1, 1])
    wg_d = din("wg", [DI, D])
    wu_d = din("wu", [DI, D])
    wd_d = din("wd", [D, DI])

    out_d = nc.dram_tensor("out", [T * TOK, D], F32, kind="ExternalOutput").ap()
    dbg_d = {}
    if dbg:
        for nm, shape, dt in [("d_kr", [128, 2048], BF16), ("d_vr", [128, 2048], BF16),
                              ("d_qfm", [128, 1024], BF16), ("d_att", [128, 2048], BF16),
                              ("d_loc", [128, 1024], F32), ("d_recip", [128, 16], F32),
                              ("d_rem", [128, 1024], F32), ("d_gtm", [128, 16], F32),
                              ("d_mixed", [128, 1024], BF16), ("d_ybf", [128, 1024], BF16),
                              ("d_lattn", [128, 256], BF16), ("d_rkfm", [128, 128], BF16),
                              ("d_rvlm", [64, 256], BF16), ("d_latfm", [128, 256], BF16),
                              ("d_psw", [128, 512], F32), ("d_den", [128, 16], F32),
                              ("d_sc", [1, 16], F32), ("d_swdrow", [1, 1024], F32),
                              ("d_rshb", [128, 1], F32), ("d_hq", [128, 2816], BF16),
                              ("d_hsb", [128, 2048], F32), ("d_xq", [128, 2048], BF16),
                              ("d_swsxg", [128, 1], F32), ("d_hmx", [128, 1], F32),
                              ("d_y2", [128, 1024], F32), ("d_nm2", [128, 16], F32),
                              ("d_rs2", [128, 16], F32)]:
            dbg_d[nm] = nc.dram_tensor(nm, shape, dt, kind="ExternalOutput").ap()

    xmid_d = nc.dram_tensor("xmid_dram", [T * TOK, D], F32).ap()
    h_d = nc.dram_tensor("h_dram", [DI, T * TOK], F32).ap()
    ag_in = nc.dram_tensor("ag_in", [16, D], F32).ap()
    ag_out = nc.dram_tensor("ag_out", [64, D], F32).ap()
    ar1_in = nc.dram_tensor("ar1_in", [1, 128], F32).ap()
    ar1_out = nc.dram_tensor("ar1_out", [1, 128], F32).ap()
    ar2_in = nc.dram_tensor("ar2_in", [1, 128], F32).ap()
    ar2_out = nc.dram_tensor("ar2_out", [1, 128], F32).ap()

    id_bf_d = nc.inline_tensor(np.eye(128).astype(ml_dtypes.bfloat16),
                               "id_bf").ap()
    id_f32_d = nc.inline_tensor(np.eye(128, dtype=np.float32), "id_f32").ap()

    G2 = [[0, 1, 2, 3], [4, 5, 6, 7]]
    G8 = [list(range(8))]

    with ExitStack() as P:
        def psb(name, shape, dt):
            return P.enter_context(nc.sbuf_tensor(name, shape, dt))

        def psem(name):
            return Sem(P.enter_context(nc.semaphore(name)))

        nm1 = psb("nm1", [128, T + 1], F32)
        rs1 = psb("rs1", [128, T + 1], F32)
        nm2 = psb("nm2", [128, T], F32)
        rs2 = psb("rs2", [128, T], F32)
        y2mx = psb("y2mx", [128, 1], F32)
        hmx = psb("hmx", [128, 1], F32)
        ones_bf = psb("ones_bf", [128, 128], BF16)
        ones_f32 = psb("ones_f32", [1, 128], F32)
        id_bf = psb("id_bf_s", [128, 128], BF16)
        id_f32 = psb("id_f32_s", [128, 128], F32)
        qkvb_col = psb("qkvb_col_s", [128, 16], F32)
        qkvb_vrow = psb("qkvb_vrow_s", [1, D], BF16)
        rqb_col = psb("rqb_col_s", [128, 2], F32)
        rkvb_col = psb("rkvb_col_s", [128, 2], F32)
        rvb_row = psb("rvb_row_s", [1, 256], BF16)
        rvb_sc = psb("rvb_sc", [1, 256], BF16)
        routb_row = psb("routb_row_s", [1, D], BF16)
        outb_row = psb("outb_row_s", [1, D], BF16)
        glb = psb("glb_s", [128, D], F32)
        grb = psb("grb_s", [128, D], F32)
        mask0 = psb("mask0_s", [128, 1024], BF16)
        maskE = psb("maskE_s", [128, 1024], BF16)
        maskO = psb("maskO_s", [128, 1024], BF16)
        z_col = psb("z_col_s", [128, 1], F32)
        z_row = psb("z_row_s", [1, 128], BF16)
        lmask = psb("lmask_s", [65, T * 65], BF16)
        clipx = psb("clipx_s", [1, 1], F32)
        clipd = psb("clipd_s", [1, 1], F32)
        sc = psb("sc", [1, 16], F32)
        sxb = psb("sxb", [128, 1], F32)
        rsxb = psb("rsxb", [128, 1], F32)
        rshb = psb("rshb", [128, 1], F32)
        m128 = psb("m128", [1, 128], F32)
        eps_col = psb("eps_col", [128, 1], F32)

        sAld = psem("sAld")
        sAld2 = psem("sAld2")
        sBld = psem("sBld")
        sWld = psem("sWld")
        sPC = psem("sPC")
        sAG = psem("sAG")
        sAR1 = psem("sAR1")
        sAR2 = psem("sAR2")
        sCCR = psem("sCCR")
        sXMs = psem("sXMs")
        sXMl = psem("sXMl")
        sHw = psem("sHw")
        sHr = psem("sHr")
        sOUT = psem("sOUT")
        sFW = psem("sFW")
        sPE = psem("sPE")
        sDVE = psem("sDVE")
        sACT = psem("sACT")
        sPL = psem("sPL")

        ctx = Ctx(nc)
        E = ctx.emit

        wdT_done = [None]

        with ExitStack() as A:
            def asb(name, shape, dt):
                return A.enter_context(nc.sbuf_tensor(name, shape, dt))

            wqkv = asb("wqkv_s", [128, KC * 3 * D], BF16)
            wrq = asb("wrq_s", [128, KC * 256], BF16)
            wrkv = asb("wrkv_s", [128, KC * 512], BF16)
            wrout = asb("wrout_s", [128, 2 * D], BF16)
            wout = asb("wout_s", [128, KC * D], BF16)
            xa = asb("xa", [128, 2 * D], F32)
            xr = asb("xr", [128, 3 * D], F32)
            sq_scr = asb("sq_scr", [128, D], BF16)
            y2_scr = asb("y2_scr", [128, D], F32)
            gd = y2_scr
            gdB = asb("gdB", [128, D], F32)
            ssq = asb("ssq", [128, 1], F32)
            lnv = asb("lnv", [128, 1], F32)
            y_bf = asb("y_bf", [128, D], BF16)
            y_fm = asb("y_fm", [128, 2 * D], BF16)
            pp = asb("pp", [1, D], F32)
            kr = asb("kr", [128, KC * 3 * 128], BF16)
            vr = asb("vr", [128, 3 * D], BF16)
            q_fm = asb("q_fm", [128, 2 * D], BF16)
            rq_fm = asb("rq_fm", [128, 2 * 256], BF16)
            att = asb("att", [128, 2 * D], BF16)
            attT = asb("attT", [128, 2 * D], BF16)
            den = asb("den", [128, 16], F32)
            recip = asb("recip", [128, 16], F32)
            loc = asb("loc", [128, D], F32)
            pooled_all = asb("pooled_all", [64, D], F32)
            pooled_bf = asb("pooled_bf", [64, D], BF16)
            pooled_fm = asb("pooled_fm", [128, 512], BF16)
            rk_fm = asb("rk_fm", [128, 128], BF16)
            rv_lm = asb("rv_lm", [64, 256], BF16)
            latt = asb("latt", [128, 4 * 66], BF16)
            lattn = asb("lattn", [128, 256], BF16)
            lattnT = asb("lattnT", [64, 512], BF16)
            lsum = asb("lsum", [128, 4], F32)
            lrecip = asb("lrecip", [128, 4], F32)
            lat_fm = asb("lat_fm", [128, 256], BF16)
            rem = asb("rem", [128, D], F32)
            
            g1 = asb("g1", [128, 16], F32)
            g2 = asb("g2", [128, 16], F32)
            gex = asb("gex", [128, 16], F32)
            g_tm = asb("g_tm", [128, 16], F32)
            dif = asb("dif", [128, D], F32)
            mixed = asb("mixed", [128, D], BF16)
            mixed_fm = asb("mixed_fm", [128, D], BF16)
            xmid = asb("xmid", [128, 2 * D], F32)
            tmp1 = asb("tmp1", [128, 1], F32)
            if dbg:
                snap_recip = asb("snap_recip", [128, 16], F32)
                snap_loc = asb("snap_loc", [128, D], F32)
                snap_rem = asb("snap_rem", [128, D], F32)
                snap_den = asb("snap_den", [128, 512], F32)

            psSC0 = A.enter_context(nc.psum_tensor("psSC0", [128, 1024], F32))
            psSC1 = A.enter_context(nc.psum_tensor("psSC1", [128, 1024], F32))
            psW0 = A.enter_context(nc.psum_tensor("psW0", [128, 512], F32))
            psW1 = A.enter_context(nc.psum_tensor("psW1", [128, 512], F32))
            psT0 = A.enter_context(nc.psum_tensor("psT0", [128, 1024], BF16))
            psT1 = A.enter_context(nc.psum_tensor("psT1", [128, 1024], BF16))

            E("gpsimd", MEMSET(ones_bf[:, :], 1.0), inc=sPL)
            E("gpsimd", MEMSET(ones_f32[:, :], 1.0), inc=sPL)
            ones_rdy = sPL.cur()
            for (dst, src) in [
                (id_bf[:, :], id_bf_d), (id_f32[:, :], id_f32_d),
                (qkvb_col[:, :], qkvb_col_d), (qkvb_vrow[:, :], qkvb_vrow_d),
                (rqb_col[:, :], rqb_col_d), (rkvb_col[:, :], rkvb_col_d),
                (rvb_row[:, :], rvb_row_d), (routb_row[:, :], routb_row_d),
                (outb_row[:, :], outb_row_d),
                (z_col[:, :], z_col_d), (z_row[:, :], zrow_d),
                (clipx[:, :], clipx_d), (clipd[:, :], clipd_d),
            ]:
                E("gpsimd", DMA(dst, src), inc=sWld, dn=16)
            wldC = sWld.cur()
            for (dst, src) in [
                (wqkv[:, :].rearrange("p (k n) -> p k n", n=3 * D),
                 wqkv_d.rearrange("(k p) n -> p k n", p=128)),
                (wrq[:, :].rearrange("p (k n) -> p k n", n=256),
                 wrq_d.rearrange("(k p) n -> p k n", p=128)),
            ]:
                E("gpsimd", DMA(dst, src), inc=sWld, dn=16)
            wldA = sWld.cur()
            for (dst, src) in [
                (glb[:, :], glb_d), (grb[:, :], grb_d),
                (mask0[:, :], mask0_d), (maskE[:, :], maskE_d),
                (maskO[:, :], maskO_d), (lmask[:, :], lmask_d),
            ]:
                E("gpsimd", DMA(dst, src), inc=sWld, dn=16)
            wldM = sWld.cur()
            E("gpsimd", DMA(wrkv[:, :].rearrange("p (k n) -> p k n", n=512),
                            wrkv_d.rearrange("(k p) n -> p k n", p=128)),
              inc=sWld, dn=16)
            wldR = sWld.cur()
            E("gpsimd", DMA(wrout[:, :].rearrange("p (k n) -> p k n", n=D),
                            wrout_d.rearrange("(k p) n -> p k n", p=128)),
              inc=sWld, dn=16)
            wldRO = sWld.cur()
            E("gpsimd", DMA(wout[:, :].rearrange("p (k n) -> p k n", n=D),
                            wout_d.rearrange("(k p) n -> p k n", p=128)),
              inc=sWld, dn=16)
            wldO = sWld.cur()
            wld = wldO

            E("vector", MEMSET(y2mx[:, :], 0.0))
            E("vector", MEMSET(eps_col[:, :], EPS))
            E("vector", MEMSET(hmx[:, :], 0.0), inc=sDVE)
            E("scalar", SCOPY(rvb_sc[:, :], rvb_row[:, :]), waits=[wldC])
            E("scalar", lambda e: e.mul(out=rvb_sc[:, :], in_=rvb_sc[:, :],
                                        mul=128.0), inc=sACT)

            a_free = {}
            pp_free = {}
            evA = {}
            rsA = {}
            pool_free = {}
            aldm = {}

            def emit_pooled(pa):
                pv = att[:, (pa % 2) * D:(pa % 2) * D + D]
                E("vector", TS(pv, xa[:, (pa % 2) * D:(pa % 2) * D + D],
                               nm1[:, pa:pa + 1], rs1[:, pa:pa + 1],
                               ALU.add, ALU.mult),
                  waits=[rsA[pa], pool_free.get(pa - 2)], inc=sDVE)
                ym2 = sDVE.cur()
                a_free[pa] = ym2
                E("tensor", MM(psW0[0:1, 0:512], ones_bf[:, 0:1],
                               pv[:, 0:512], True, True),
                  waits=[ym2, ones_rdy, evA.get(pa - 1)])
                E("tensor", MM(psW1[0:1, 0:512], ones_bf[:, 0:1],
                               pv[:, 512:1024], True, True), inc=sPE)
                pem = sPE.cur()
                pool_free[pa] = pem
                ppv = pp[0:1, 0:D]
                E("scalar", SCOPY(ppv[:, 0:512], psW0[0:1, 0:512]),
                  waits=[pem, pp_free.get(pa - 1)])
                E("scalar", SCOPY(ppv[:, 512:1024], psW1[0:1, 0:512]),
                  inc=sACT)
                evA[pa] = sACT.cur()
                E("sync", DMA(ag_in[pa:pa + 1, :], ppv),
                  waits=[evA[pa]], inc=sPC, dn=16)
                pp_free[pa] = sPC.cur()

            for seq in range(T + 1):
                tau = seq if seq < T else -1
                col = tau if tau >= 0 else T
                r = seq % 2
                src = x_d[tau * TOK:(tau + 1) * TOK, :] if tau >= 0 else xh_d
                xav = xa[:, r * D:(r + 1) * D]
                if seq % 2 == 0:
                    E("sync", DMA(xav, src), waits=[a_free.get(seq - 2)],
                      inc=sAld, dn=16)
                    aldm[seq] = sAld.cur()
                else:
                    E("scalar", DMA(xav, src), waits=[a_free.get(seq - 2)],
                      inc=sAld2, dn=16)
                    aldm[seq] = sAld2.cur()
                nmc = nm1[:, col:col + 1]
                rsc = rs1[:, col:col + 1]
                E("vector", RSUM(nmc, xav, neg=True), waits=[aldm[seq]],
                  inc=sDVE)
                E("vector", TS(nmc, nmc, R1024, None, ALU.mult),
                  waits=[sDVE.cur()], inc=sDVE)
                nm_m = sDVE.cur()
                E("scalar", ACTV(sq_scr[:, :], xav, AF.Square, bias=nmc,
                                 accum=ssq[:, 0:1]), waits=[nm_m], inc=sACT)
                E("scalar", ACTV(lnv[:, 0:1], ssq[:, 0:1], AF.Ln,
                                 scale=R1024, bias=eps_col[:, 0:1]),
                  waits=[sACT.cur()], inc=sACT)
                E("scalar", ACTV(rsc, lnv[:, 0:1], AF.Exp, scale=-0.5),
                  waits=[sACT.cur()], inc=sACT)
                rsA[col] = sACT.cur()
                if tau < 0:
                    a_free[seq] = rsA[col]
                if seq >= 1:
                    emit_pooled(seq - 1)

            E("gpsimd", lambda e: e.collective_compute(
                "AllGather", ALU.bypass, replica_groups=G2,
                ins=[ag_in.opt()], outs=[ag_out.opt()]),
              waits=[sPC.cur()], inc=sAG)
            E("sync", DMA(pooled_all[:, :], ag_out), waits=[sAG.cur()],
              inc=sCCR, dn=16)
            E("scalar", SCOPY(pooled_bf[:, :], pooled_all[:, :]),
              waits=[sCCR.cur()], inc=sACT)
            pbf_m = sACT.cur()

            if stage >= 1:
                psW_free = [evA.get(T - 1), evA.get(T - 1)]
                psSC_free = [None, None]
                psT_free = [None, None]
                kv_rdy = {}
                sc3 = {}
                av3 = {}
                b_free = {}
                xm_free = {}
                ybf_free = [None]
                yfm_mark = {}
                vmm_mark = {}
                qm_mark = {}
                rqm_mark = {}
                lsm_mark = {}
                att_free = [[pool_free.get(T - 2), pool_free.get(T - 1)]]
                attT_free = [[None, None]]
                loc_free = [None]
                rem_free = [None]
                gex_free = [None]
                mix_free = [None]
                mfm_free = [None]
                latt_free = [None]
                lattn_free = [None]
                lattnT_free = [None]
                latfm_free = [None]
                rkrv_m = [None]
                rkv_done = [False]

                def xld(tau):
                    return sBld.at(16 * (tau + 2))

                def ln_apply(col, xv, extra):
                    E("vector", NOP(), waits=[w for w in extra if w])
                    E("vector", TS(y_bf[:, :], xv, nm1[:, col:col + 1],
                                   rs1[:, col:col + 1], ALU.add, ALU.mult),
                      waits=[rsA[col]], inc=sDVE)
                    return sDVE.cur()

                def fm_transposes(dst, src_bf, first_waits, evac_waits):
                    for f in range(8):
                        w = first_waits if f == 0 else ()
                        E("tensor", TRP(psT0[:, 128 * f:128 * f + 128],
                                        src_bf[:, 128 * f:128 * f + 128],
                                        id_bf[:, :]), waits=w)
                    ctx.retro_inc("tensor", sPE)
                    tm = sPE.cur()
                    E("vector", VCOPY(dst[:, 0:1024], psT0[:, 0:1024]),
                      waits=[tm] + [w for w in evac_waits if w], inc=sDVE)
                    psT_free[0] = sDVE.cur()
                    return tm, sDVE.cur()

                def h1a(tau):
                    slot = tau % 2
                    sx = tau % 3
                    xv = xr[:, sx * D:(sx + 1) * D]
                    col = tau if tau >= 0 else T
                    ym = ln_apply(col, xv, [xld(tau), ybf_free[0]])
                    tm, yfm_m = fm_transposes(
                        y_fm[:, slot * D:(slot + 1) * D], y_bf,
                        [ym, psT_free[0], wldC], [vmm_mark.get(tau - 2)])
                    ybf_free[0] = tm
                    yfm_mark[tau] = yfm_m
                    return ym

                def h1b_batches(tau):
                    slot = tau % 2
                    s3 = tau % 3
                    yv = y_fm[:, slot * D:(slot + 1) * D]
                    yfm_m = yfm_mark[tau]

                    def wbatch(kind, h0, nch):
                        def go():
                            nb_ = min(4, nch - h0)
                            bank = (h0 // 4) % 2
                            ps = psW0 if bank == 0 else psW1
                            for mi in range(nb_):
                                m = h0 + mi
                                ccol = 128 * mi
                                for kc in range(KC):
                                    if kind == "q":
                                        lh = wqkv[:, 3072 * kc + 128 * m:
                                                  3072 * kc + 128 * m + 128]
                                    elif kind == "k":
                                        lh = wqkv[:, 3072 * kc + 1024 + 128 * m:
                                                  3072 * kc + 1024 + 128 * m + 128]
                                    else:
                                        lh = wrq[:, 256 * kc + 128 * m:
                                                 256 * kc + 128 * m + 128]
                                    w = [yfm_m, wldA, psW_free[bank]] \
                                        if (mi == 0 and kc == 0) else ()
                                    E("tensor", MM(ps[:, ccol:ccol + 128], lh,
                                                   yv[:, 128 * kc:128 * kc + 128],
                                                   kc == 0, kc == KC - 1), waits=w)
                            ctx.retro_inc("tensor", sPE)
                            mmm = sPE.cur()
                            for mi in range(nb_):
                                m = h0 + mi
                                ccol = 128 * mi
                                if kind == "q":
                                    dst = q_fm[:, slot * D + 128 * m:
                                               slot * D + 128 * m + 128]
                                    bias = qkvb_col[:, m:m + 1]
                                    guard = sc3.get(tau - 2)
                                elif kind == "k":
                                    dst = kr[:, (m * 3 + s3) * 128:
                                             (m * 3 + s3) * 128 + 128]
                                    bias = qkvb_col[:, 8 + m:9 + m]
                                    guard = sc3.get(tau - 2)
                                else:
                                    dst = rq_fm[:, slot * 256 + 128 * m:
                                                slot * 256 + 128 * m + 128]
                                    bias = rqb_col[:, m:m + 1]
                                    guard = lsm_mark.get(tau - 2)
                                E("scalar", ACTV(dst, ps[:, ccol:ccol + 128],
                                                 AF.Identity, bias=bias),
                                  waits=[mmm, guard] if mi == 0 else (),
                                  inc=sACT if mi == nb_ - 1 else None)
                            psW_free[bank] = sACT.cur()
                            if kind == "q" and h0 == 4:
                                qm_mark[tau] = sACT.cur()
                            elif kind == "rq":
                                rqm_mark[tau] = sACT.cur()
                        return go

                    def vbatch():
                        for n in (range(2) if sub > -3 else []):
                            for kc in range(KC):
                                w = [yfm_m, wldA, psSC_free[0]] \
                                    if (n == 0 and kc == 0) else ()
                                E("tensor", MM(
                                    psSC0[:, 512 * n:512 * n + 512],
                                    yv[:, 128 * kc:128 * kc + 128],
                                    wqkv[:, 3072 * kc + 2048 + 512 * n:
                                         3072 * kc + 2048 + 512 * n + 512],
                                    kc == 0, False), waits=w)
                            E("tensor", MM(psSC0[:, 512 * n:512 * n + 512],
                                           ones_bf[0:1, :],
                                           qkvb_vrow[0:1, 512 * n:512 * n + 512],
                                           False, True))
                        ctx.retro_inc("tensor", sPE)
                        vmm = sPE.cur()
                        vmm_mark[tau] = vmm
                        if sub > -3:
                            E("scalar", ACTV(vr[:, s3 * D:(s3 + 1) * D],
                                             psSC0[:, :], AF.Copy),
                              waits=[vmm, av3.get(tau - 2)], inc=sACT)
                            psSC_free[0] = sACT.cur()
                        kv_rdy[tau] = sACT.cur()
                    out = []
                    if sub > -4:
                        out.append(wbatch("k", 0, 8))
                        out.append(wbatch("k", 4, 8))
                        if tau >= 0:
                            out.append(wbatch("q", 0, 8))
                            out.append(wbatch("q", 4, 8))
                            out.append(wbatch("rq", 0, 2))
                    out.append(vbatch)
                    return out

                pend = [[]]

                def set_batches(tau):
                    pend[0] = h1b_batches(tau)

                def nb():
                    if pend[0]:
                        pend[0].pop(0)()

                def flush_b():
                    while pend[0]:
                        pend[0].pop(0)()

                def emit_rkv():
                    for f in range(8):
                        w = [pbf_m, psT_free[1], wldC] if f == 0 else ()
                        E("tensor", TRP(psT1[:, 64 * f:64 * f + 64],
                                        pooled_bf[0:64, 128 * f:128 * f + 128],
                                        id_bf[0:64, 0:64]), waits=w)
                    ctx.retro_inc("tensor", sPE)
                    tm = sPE.cur()
                    E("vector", VCOPY(pooled_fm[:, :], psT1[:, 0:512]),
                      waits=[tm], inc=sDVE)
                    pfm = sDVE.cur()
                    psT_free[1] = pfm
                    for m in range(2):
                        for kc in range(KC):
                            w = [pfm, wldR, psW_free[0]] if (m == 0 and kc == 0) else ()
                            E("tensor", MM(
                                psW0[:, 64 * m:64 * m + 64],
                                wrkv[:, 512 * kc + 128 * m:
                                     512 * kc + 128 * m + 128],
                                pooled_fm[:, 64 * kc:64 * kc + 64],
                                kc == 0, kc == KC - 1), waits=w)
                    ctx.retro_inc("tensor", sPE)
                    rkm = sPE.cur()
                    for m in range(2):
                        E("scalar", ACTV(rk_fm[:, 64 * m:64 * m + 64],
                                         psW0[:, 64 * m:64 * m + 64], AF.Identity,
                                         scale=R128, bias=rkvb_col[:, m:m + 1]),
                          waits=[rkm] if m == 0 else (),
                          inc=sACT if m == 1 else None)
                    psW_free[0] = sACT.cur()
                    for h in range(LH):
                        for kc in range(KC):
                            w = [pfm, wldR, psW_free[1]] if (h == 0 and kc == 0) else ()
                            E("tensor", MM(
                                psW1[0:64, 64 * h:64 * h + 64],
                                pooled_fm[:, 64 * kc:64 * kc + 64],
                                wrkv[:, 512 * kc + 256 + 64 * h:
                                     512 * kc + 256 + 64 * h + 64],
                                kc == 0, False), waits=w)
                        E("tensor", MM(psW1[0:64, 64 * h:64 * h + 64],
                                       ones_bf[0:1, 0:64],
                                       rvb_sc[0:1, 64 * h:64 * h + 64],
                                       False, True))
                    ctx.retro_inc("tensor", sPE)
                    rvm = sPE.cur()
                    E("scalar", ACTV(rv_lm[0:64, :], psW1[0:64, 0:256], AF.Copy,
                                     scale=R128), waits=[rvm], inc=sACT)
                    psW_free[1] = sACT.cur()
                    rkrv_m[0] = sACT.cur()
                    rkv_done[0] = True

                E("sync", DMA(xr[:, 2 * D:3 * D], xh_d), inc=sBld, dn=16)
                E("sync", DMA(xr[:, 0:D], x_d[0:TOK, :]), inc=sBld, dn=16)
                ym = h1a(-1)
                b_free[-1] = ym
                set_batches(-1)
                flush_b()
                E("sync", DMA(xr[:, D:2 * D], x_d[TOK:2 * TOK, :]),
                  inc=sBld, dn=16)
                h1a(0)
                set_batches(0)
                flush_b()
                if sub >= 0:
                    emit_rkv()
                h1a(1)
                xm_sp = {}

                def emit_ln2(tt):
                    xmv2 = xmid[:, (tt % 2) * D:(tt % 2) * D + D]
                    E("vector", RSUM(nm2[:, tt:tt + 1], xmv2, neg=True),
                      waits=[xm_sp[tt]], inc=sDVE)
                    E("vector", TS(nm2[:, tt:tt + 1], nm2[:, tt:tt + 1],
                                   R1024, None, ALU.mult),
                      waits=[sDVE.cur()], inc=sDVE)
                    nm2m = sDVE.cur()
                    E("scalar", ACTV(sq_scr[:, :], xmv2, AF.Square,
                                     bias=nm2[:, tt:tt + 1], accum=ssq[:, 0:1]),
                      waits=[nm2m], inc=sACT)
                    E("scalar", ACTV(lnv[:, 0:1], ssq[:, 0:1], AF.Ln,
                                     scale=R1024, bias=eps_col[:, 0:1]),
                      waits=[sACT.cur()], inc=sACT)
                    E("scalar", ACTV(rs2[:, tt:tt + 1], lnv[:, 0:1], AF.Exp,
                                     scale=-0.5), waits=[sACT.cur()], inc=sACT)
                    rs2m = sACT.cur()
                    E("vector", TS(y2_scr[:, :], xmv2, nm2[:, tt:tt + 1],
                                   rs2[:, tt:tt + 1], ALU.add, ALU.mult),
                      waits=[rs2m], inc=sDVE)
                    E("vector", RMAXA(tmp1[:, 0:1], y2_scr[:, :]),
                      waits=[sDVE.cur()], inc=sDVE)
                    E("vector", TT(y2mx[:, :], y2mx[:, :], tmp1[:, 0:1],
                                   ALU.max), waits=[sDVE.cur()], inc=sDVE)
                    xm_free[tt] = sDVE.cur()

                for t in range(T):
                    slot = t % 2
                    xv = xr[:, (t % 3) * D:(t % 3) * D + D]

                    def h1a_next():
                        if t + 2 < T:
                            E("sync", DMA(
                                xr[:, ((t + 2) % 3) * D:((t + 2) % 3) * D + D],
                                x_d[(t + 2) * TOK:(t + 3) * TOK, :]),
                              waits=[b_free.get(t - 1)], inc=sBld, dn=16)
                            h1a(t + 2)
                    if t + 1 < T:
                        set_batches(t + 1)
                    qm = qm_mark.get(t)
                    rqm = rqm_mark.get(t)
                    if sub < 1:
                        h1a_next()

                    if sub >= 1:
                        mask = mask0 if t == 0 else (maskE if t % 2 == 0 else maskO)
                        loc_m = None
                        scm_g = {}
                        expm_g = {}
                        mskm_g = {}
                        trm_g = {}
                        atTm_g = {}
                        avm_g = {}
                        SH = {g: ((0, 4 * g), (1, 4 * g + 2),
                                  (2, 4 * g + 1), (3, 4 * g + 3))
                              for g in range(4)}

                        def e_sc(g):
                            gb = g % 2
                            ps = psSC0 if gb == 0 else psSC1
                            for (j, h) in SH[g]:
                                pt = h // 2
                                pr = 64 * (h % 2)
                                lhq = q_fm[pr:pr + 64,
                                           slot * D + 128 * pt:
                                           slot * D + 128 * pt + 128]
                                for s_i, ktau in enumerate((t - 1, t)):
                                    ks = ktau % 3
                                    rhk = kr[pr:pr + 64,
                                             (pt * 3 + ks) * 128:
                                             (pt * 3 + ks) * 128 + 128]
                                    w = [qm, kv_rdy[t - 1], kv_rdy[t],
                                         psSC_free[gb]] if (j == 0 and s_i == 0) \
                                        else ()
                                    E("tensor", MM(
                                        ps[:, 256 * j + 128 * s_i:
                                           256 * j + 128 * s_i + 128],
                                        rhk, lhq, True, False), waits=w)
                            for (j, h) in SH[g]:
                                for s_i in range(2):
                                    E("tensor", MM(
                                        ps[:, 256 * j + 128 * s_i:
                                           256 * j + 128 * s_i + 128],
                                        id_bf[:, :],
                                        mask[:, 256 * j + 128 * s_i:
                                             256 * j + 128 * s_i + 128],
                                        False, True),
                                      waits=[wldM] if (j == 0 and s_i == 0)
                                      else ())
                            ctx.retro_inc("tensor", sPE)
                            scm_g[g] = sPE.cur()
                            if g == 3:
                                sc3[t] = scm_g[g]

                        def e_exp(g):
                            gb = g % 2
                            ps = psSC0 if gb == 0 else psSC1
                            atv = att[:, gb * D:(gb + 1) * D]
                            E("scalar", ACTV(atv, ps[:, :], AF.Exp, scale=0.125),
                              waits=[scm_g[g], att_free[0][gb]], inc=sACT)
                            psSC_free[gb] = sACT.cur()
                            expm_g[g] = sACT.cur()

                        def e_av(g):
                            gb = g % 2
                            psw = psW0 if gb == 0 else psW1
                            atT = att[:, gb * D:(gb + 1) * D]
                            for (j, h) in SH[g]:
                                for s_i, ktau in enumerate((t - 1, t)):
                                    w = [expm_g[g], psW_free[gb]] \
                                        if (j == 0 and s_i == 0) else ()
                                    E("tensor", MM(
                                        psw[:, 256 + 64 * j:256 + 64 * j + 64],
                                        atT[:, 256 * j + 128 * s_i:
                                            256 * j + 128 * s_i + 128],
                                        vr[:, 1024 * (ktau % 3) + 64 * h:
                                           1024 * (ktau % 3) + 64 * h + 64],
                                        s_i == 0, s_i == 1), waits=w)
                                for s_i in range(2):
                                    E("tensor", MM(
                                        psw[:, 2 * j:2 * j + 1],
                                        atT[:, 256 * j + 128 * s_i:
                                            256 * j + 128 * s_i + 128],
                                        ones_bf[:, 0:1], s_i == 0,
                                        (s_i == 1) and t != 0))
                                if t == 0:
                                    E("tensor", MM(psw[:, 2 * j:2 * j + 1],
                                                   z_row[0:1, :],
                                                   ones_bf[0:1, 0:1],
                                                   False, True))
                            ctx.retro_inc("tensor", sPE)
                            avm_g[g] = sPE.cur()
                            if g == 3:
                                av3[t] = avm_g[g]
                            att_free[0][gb] = avm_g[g]

                        def e_loc(g):
                            gb = g % 2
                            psw = psW0 if gb == 0 else psW1
                            for (j, h) in SH[g]:
                                w = [avm_g[g]] if j == 0 else ()
                                E("vector", RECIP(recip[:, h:h + 1],
                                                  psw[:, 2 * j:2 * j + 1]),
                                  waits=w, inc=sDVE if j == 3 else None)
                            rcp_m = sDVE.cur()
                            for (j, h) in SH[g]:
                                w = [loc_free[0], rcp_m] if j == 0 else ()
                                E("vector", TS(loc[:, 64 * h:64 * h + 64],
                                               psw[:, 256 + 64 * j:
                                                   256 + 64 * j + 64],
                                               recip[:, h:h + 1], None,
                                               ALU.mult),
                                  waits=w, inc=sDVE if j == 3 else None)
                            psW_free[gb] = sDVE.cur()
                            return sDVE.cur()

                        if lsub >= 3:
                            e_sc(0)
                            e_sc(1)
                            e_exp(0)
                            e_exp(1)
                            nb()
                            e_av(0)
                            e_sc(2)
                            e_exp(2)
                            h1a_next()
                            nb()
                            e_av(1)
                            e_sc(3)
                            e_exp(3)
                            loc_m = e_loc(0)
                            e_av(2)
                            loc_m = e_loc(1)
                            e_av(3)
                            loc_m = e_loc(2)
                            loc_m = e_loc(3)
                        else:
                            h1a_next()
                            for g in range(4):
                                gb = g % 2
                                e_sc(g)
                                e_exp(g)
                                psSC_free[gb] = expm_g[g]
                                att_free[0][gb] = expm_g[g]
                                loc_m = expm_g[g]
                                if lsub >= 2:
                                    e_av(g)
                                    psW_free[gb] = avm_g[g]
                                    loc_m = avm_g[g]
                        if dbg and t == 0 and lsub >= 3:
                            E("vector", VCOPY(snap_recip[:, :],
                                              recip[:, :]), waits=[loc_m],
                              inc=sDVE)
                            E("vector", VCOPY(snap_loc[:, :], loc[:, :]),
                              inc=sDVE)
                            E("sync", DMA(dbg_d["d_recip"],
                                          snap_recip[:, :]),
                              waits=[sDVE.cur()], inc=sOUT, dn=16)
                            E("sync", DMA(dbg_d["d_loc"], snap_loc[:, :]),
                              waits=[sDVE.cur()], inc=sOUT, dn=16)

                    if sub >= 2:
                        for h in range(LH):
                            psL = psW0 if h % 2 == 0 else psW1
                            lcol = 66 * (h // 2)
                            pb = 64 * (h % 2)
                            w = [rqm, rkrv_m[0], psW_free[0], psW_free[1],
                                 wldM] if h == 0 else ()
                            E("tensor", MM(psL[:, lcol:lcol + 65],
                                           ones_bf[pb:pb + 1, :],
                                           lmask[pb:pb + 1, 65 * t:65 * t + 65],
                                           True, False), waits=w)
                            E("tensor", MM(
                                psL[:, lcol:lcol + 64],
                                rq_fm[pb:pb + 64,
                                      slot * 256 + 128 * (h // 2):
                                      slot * 256 + 128 * (h // 2) + 128],
                                rk_fm[pb:pb + 64,
                                      64 * (h // 2):64 * (h // 2) + 64],
                                False, True))
                        ctx.retro_inc("tensor", sPE)
                        lsm = sPE.cur()
                        lsm_mark[t] = lsm
                        for h in range(LH):
                            psL = psW0 if h % 2 == 0 else psW1
                            lcol = 66 * (h // 2)
                            w = [lsm, latt_free[0]] if h == 0 else ()
                            E("scalar", ACTV(latt[:, 66 * h:66 * h + 65],
                                             psL[:, lcol:lcol + 65], AF.Exp,
                                             scale=0.125, accum=lsum[:, h:h + 1]),
                              waits=w, inc=sACT if h == LH - 1 else None)
                        psW_free[0] = sACT.cur()
                        psW_free[1] = sACT.cur()
                        lexm = sACT.cur()
                        nb()
                        E("vector", RECIP(lrecip[:, :], lsum[:, :]),
                          waits=[lexm], inc=sDVE)
                        lrc_m = sDVE.cur()
                        for h in range(LH):
                            w = [lattn_free[0], lrc_m] if h == 0 else ()
                            E("vector", TS(lattn[:, 64 * h:64 * h + 64],
                                           latt[:, 66 * h:66 * h + 64],
                                           lrecip[:, h:h + 1], None, ALU.mult),
                              waits=w, inc=sDVE if h == LH - 1 else None)
                        latm = sDVE.cur()
                        latt_free[0] = latm
                        for h in range(LH):
                            w = [latm, psT_free[1]] if h == 0 else ()
                            E("tensor", TRP(psT1[0:64, 128 * h:128 * h + 128],
                                            lattn[:, 64 * h:64 * h + 64],
                                            id_bf[:, :]), waits=w)
                        ctx.retro_inc("tensor", sPE)
                        ltrm = sPE.cur()
                        lattn_free[0] = ltrm
                        nb()
                        E("vector", VCOPY(lattnT[0:64, :], psT1[0:64, 0:512]),
                          waits=[ltrm, lattnT_free[0]], inc=sDVE)
                        psT_free[1] = sDVE.cur()
                        latTm = sDVE.cur()
                        for h in range(LH):
                            w = [latTm, psW_free[1]] if h == 0 else ()
                            E("tensor", MM(
                                psW1[64 * (h % 2):64 * (h % 2) + 64,
                                     128 * (h // 2):128 * (h // 2) + 128],
                                rv_lm[0:64, 64 * h:64 * h + 64],
                                lattnT[0:64, 128 * h:128 * h + 128],
                                True, True), waits=w)
                        ctx.retro_inc("tensor", sPE)
                        lavm = sPE.cur()
                        lattnT_free[0] = lavm
                        E("scalar", ACTV(lat_fm[:, :], psW1[:, 0:256], AF.Copy),
                          waits=[lavm, latfm_free[0]], inc=sACT)
                        psW_free[1] = sACT.cur()
                        latfm_m = sACT.cur()
                        nb()
                        for n in range(2):
                            for c in range(2):
                                w = [latfm_m, wldRO, psSC_free[1]] \
                                    if (n == 0 and c == 0) else ()
                                E("tensor", MM(
                                    psSC1[:, 512 * n:512 * n + 512],
                                    lat_fm[:, 128 * c:128 * c + 128],
                                    wrout[:, 1024 * c + 512 * n:
                                          1024 * c + 512 * n + 512],
                                    c == 0, False), waits=w)
                            E("tensor", MM(psSC1[:, 512 * n:512 * n + 512],
                                           ones_bf[0:1, :],
                                           routb_row[0:1, 512 * n:512 * n + 512],
                                           False, True))
                        ctx.retro_inc("tensor", sPE)
                        remm = sPE.cur()
                        latfm_free[0] = remm
                        E("scalar", ACTV(rem[:, :], psSC1[:, :], AF.Copy),
                          waits=[remm, rem_free[0]], inc=sACT)
                        psSC_free[1] = sACT.cur()
                        rem_m = sACT.cur()
                        nb()

                    if sub >= 3:
                        E("vector", TT(gd[:, :], loc[:, :], glb[:, :], ALU.mult),
                          waits=[loc_m, wldM], inc=sDVE)
                        E("vector", RED3(g1[:, :],
                                         gd[:, :].rearrange("p (h d) -> p h d", d=64)),
                          waits=[sDVE.cur()], inc=sDVE)
                        E("vector", TT(gdB[:, :], rem[:, :], grb[:, :], ALU.mult),
                          waits=[rem_m], inc=sDVE)
                        E("vector", RED3(g2[:, :],
                                         gdB[:, :].rearrange("p (h d) -> p h d", d=64)),
                          waits=[sDVE.cur()], inc=sDVE)
                        E("vector", TT(g1[:, :], g1[:, :], g2[:, :], ALU.add),
                          waits=[gex_free[0], sDVE.cur()], inc=sDVE)
                        glm = sDVE.cur()
                        E("scalar", ACTV(gex[:, :], g1[:, :], AF.Exp, scale=-1.0),
                          waits=[glm], inc=sACT)
                        E("scalar", ACTV(gex[:, :], gex[:, :], AF.Identity,
                                         bias=1.0), waits=[sACT.cur()],
                          inc=sACT)
                        E("vector", RECIP(g_tm[:, :], gex[:, :]),
                          waits=[sACT.cur()], inc=sDVE)
                        gex_free[0] = sDVE.cur()
                        grc_m = sDVE.cur()
                        E("gpsimd", TT(dif[:, :], loc[:, :], rem[:, :], ALU.subtract),
                          waits=[loc_m, rem_m, mix_free[0]], inc=sPL)
                        difm = sPL.cur()
                        loc_free[0] = difm
                        for h in range(NH):
                            w = [difm, grc_m] if h == 0 else ()
                            E("vector", STT(mixed[:, 64 * h:64 * h + 64],
                                            dif[:, 64 * h:64 * h + 64],
                                            g_tm[:, h:h + 1],
                                            rem[:, 64 * h:64 * h + 64],
                                            ALU.mult, ALU.add),
                              waits=w, inc=sDVE if h == NH - 1 else None)
                        mixm = sDVE.cur()
                        rem_free[0] = mixm
                        tm, mfm_m = fm_transposes(mixed_fm, mixed,
                                                  [mixm, psT_free[0]],
                                                  [mfm_free[0]])
                        mix_free[0] = tm
                        for n in range(2):
                            for kc in range(KC):
                                w = [mfm_m, wldO, psSC_free[1]] \
                                    if (n == 0 and kc == 0) else ()
                                E("tensor", MM(
                                    psSC1[:, 512 * n:512 * n + 512],
                                    mixed_fm[:, 128 * kc:128 * kc + 128],
                                    wout[:, 1024 * kc + 512 * n:
                                         1024 * kc + 512 * n + 512],
                                    kc == 0, False), waits=w)
                            E("tensor", MM(psSC1[:, 512 * n:512 * n + 512],
                                           ones_bf[0:1, :],
                                           outb_row[0:1, 512 * n:512 * n + 512],
                                           False, True))
                        ctx.retro_inc("tensor", sPE)
                        opm = sPE.cur()
                        mfm_free[0] = opm
                        xmv = xmid[:, slot * D:(slot + 1) * D]
                        E("vector", TT(xmv, psSC1[:, :], xv, ALU.add),
                          waits=[opm, xm_free.get(t - 2), sXMs.at(16 * (t - 1))],
                          inc=sDVE)
                        psSC_free[1] = sDVE.cur()
                        xm_m = sDVE.cur()
                        b_free[t] = xm_m
                        E("scalar", DMA(xmid_d[t * TOK:(t + 1) * TOK, :], xmv),
                          waits=[xm_m], inc=sXMs, dn=16)
                        if dbg and t == 0:
                            for nm, buf, wt in [
                                    ("d_kr", kr[:, :], kv_rdy[0]),
                                    ("d_vr", vr[:, :], kv_rdy[0]),
                                    ("d_qfm", q_fm[:, :], qm),
                                    ("d_ybf", y_bf[:, :], kv_rdy[0]),
                                    ("d_att", att[:, :], loc_m),
                                    ("d_rem", rem[:, :], rem_m),
                                    ("d_gtm", g_tm[:, :], mixm),
                                    ("d_mixed", mixed[:, :], mixm),
                                    ("d_lattn", lattn[:, :], latm),
                                    ("d_rkfm", rk_fm[:, :], rkrv_m[0]),
                                    ("d_rvlm", rv_lm[0:64, :], rkrv_m[0]),
                                    ("d_latfm", lat_fm[:, :], latfm_m)]:
                                E("sync", DMA(dbg_d[nm], buf), waits=[wt],
                                  inc=sOUT, dn=16)
                        xm_sp[t] = xm_m

                    else:
                        xmv = xmid[:, slot * D:(slot + 1) * D]
                        E("vector", VCOPY(xmv, xv), waits=[xm_free.get(t - 2), sXMs.at(16 * (t - 1))], inc=sDVE)
                        xm_m = sDVE.cur()
                        b_free[t] = xm_m
                        E("scalar", DMA(xmid_d[t * TOK:(t + 1) * TOK, :], xmv), waits=[xm_m], inc=sXMs, dn=16)
                        xm_free[t] = xm_m
                    flush_b()
                    if sub >= 3 and t >= 1:
                        emit_ln2(t - 1)
                if sub >= 3:
                    emit_ln2(T - 1)
                y2m = sDVE.cur()
                E("tensor", TRP(psW0[0:1, 0:128], y2mx[:, 0:1], id_f32[:, :]),
                  waits=[y2m, psW_free[0], wldC], inc=sPE)
                E("scalar", SCOPY(m128[0:1, :], psW0[0:1, 0:128]),
                  waits=[sPE.cur()], inc=sACT)
                E("sync", DMA(ar1_in, m128[0:1, :]), waits=[sACT.cur()],
                  inc=sPC, dn=16)
                E("gpsimd", lambda e: e.collective_compute(
                    "AllReduce", ALU.max, replica_groups=G8,
                    ins=[ar1_in.opt()], outs=[ar1_out.opt()]),
                  waits=[sPC.cur()], inc=sAR1)

            if stage == 0:
                E("sync", DMA(out_d, x_d), waits=[sAG.cur()], inc=sOUT, dn=16)
                E("sync", NOP(), waits=[sOUT.cur()])
        ctx.run_block()

        wdT = psb("wdT", [128, NIT * 1024], BF16)
        swd_bc = psb("swd_bc", [128, D], F32)
        swd_cols = psb("swd_cols", [128, 8], F32)
        swd_row = psb("swd_row", [1, D], F32)
        rsw_wd = psb("rsw_wd", [128, 1], F32)

        if stage >= 2:
            with ExitStack() as B:
                def bsb(name, shape, dt):
                    return B.enter_context(nc.sbuf_tensor(name, shape, dt))

                xq_fm = bsb("xq_fm", [128, KC * 2048], BF16)
                xm_t = bsb("xm_t", [128, 2 * D], F32)
                y2 = bsb("y2", [128, D], F32)
                wd_st = bsb("wd_st", [128, DI], F32)
                wdq_bf = bsb("wdq_bf", [128, DI], BF16)
                xq_i32 = bsb("xq_i32", [128, D], I32)
                xq_bf = bsb("xq_bf", [128, D], BF16)
                wg_st = bsb("wg_st", [128, 2 * D], F32)
                wu_st = bsb("wu_st", [128, 2 * D], F32)
                w_scr = bsb("w_scr", [128, D], F32)
                w_i32 = bsb("w_i32", [128, D], I32)
                wqg_bf = bsb("wqg_bf", [128, 2 * D], BF16)
                wqu_bf = bsb("wqu_bf", [128, 2 * D], BF16)
                wgT = bsb("wgT", [128, 2 * D], BF16)
                wuT = bsb("wuT", [128, 2 * D], BF16)
                amx = bsb("amx", [128, 4], F32)
                w_act1 = bsb("w_act1", [128, 2], F32)
                amx2 = bsb("amx2", [128, 2], F32)
                swsx_g3 = bsb("swsx_g3", [128, 3], F32)
                swsx_u3 = bsb("swsx_u3", [128, 3], F32)
                rsw = bsb("rsw", [128, 2], F32)
                silu_sb = bsb("silu_sb", [128, 1024], F32)
                h_sb = bsb("h_sb", [128, 2 * 2048], F32)
                htmp = bsb("htmp", [128, 1], F32)
                htmp2 = bsb("htmp2", [128, 1], F32)

                psG = [B.enter_context(nc.psum_tensor(f"psG{i}", [128, 512], F32))
                       for i in range(4)]
                psTf = B.enter_context(nc.psum_tensor("psTf", [128, 1024], BF16))
                psTg = B.enter_context(nc.psum_tensor("psTg", [128, 1024], BF16))
                psTu = B.enter_context(nc.psum_tensor("psTu", [128, 1024], BF16))
                psM = B.enter_context(nc.psum_tensor("psM", [128, 512], F32))

                psG_free = [None, None, None, None]
                silu_free = [None, None, None, None]
                hm_marks = {}
                wqm_marks = {}
                gum_marks = {}
                swsx_wm = {}
                qm_g = {}
                qm_u = {}
                qmark = {}
                trpg_marks = {}
                trp_marks = {}
                ev_g = {}
                ev_u = {}
                silm_last = {}
                mul_prev = [None]

                def emit_wq_q(i):
                    r = i % 2
                    E("sync", DMA(wg_st[:, r * D:r * D + D],
                                  wg_d[128 * i:128 * i + 128, :]),
                      waits=[qmark.get(i - 2)], inc=sFW, dn=16)
                    gld = sFW.cur()
                    E("scalar", DMA(wu_st[:, r * D:r * D + D],
                                    wu_d[128 * i:128 * i + 128, :]),
                      inc=sFW, dn=16)
                    uld = sFW.cur()
                    wsm = [None, None]
                    for c, (wv, ldm) in enumerate((
                            (wg_st[:, r * D:r * D + D], gld),
                            (wu_st[:, r * D:r * D + D], uld))):
                        ac2 = 2 * (i % 2) + c
                        E("vector", RMAXA(amx[:, ac2:ac2 + 1], wv),
                          waits=[ldm, mul_prev[0]], inc=sDVE)
                        E("vector", VCOPY(amx2[:, c:c + 1], amx[:, ac2:ac2 + 1]),
                          waits=[sDVE.cur()], inc=sDVE)
                        am = sDVE.cur()

                        def mk_mul(c):
                            return lambda e: e.mul(out=w_act1[:, c:c + 1],
                                                   in_=amx2[:, c:c + 1],
                                                   mul=R127)
                        E("scalar", mk_mul(c), waits=[am], inc=sACT)
                        wsm[c] = sACT.cur()
                    mul_prev[0] = wsm[1]
                    for c, (wv, swsx, wq2) in enumerate((
                            (wg_st[:, r * D:r * D + D], swsx_g3, wqg_bf),
                            (wu_st[:, r * D:r * D + D], swsx_u3, wqu_bf))):
                        E("vector", RECIP(rsw[:, c:c + 1], w_act1[:, c:c + 1]),
                          waits=[wsm[c]], inc=sDVE)
                        rm = sDVE.cur()
                        E("vector", TS(w_scr[:, :], wv, rsw[:, c:c + 1], QHI,
                                       ALU.mult, ALU.min), waits=[rm])
                        E("vector", TS(w_i32[:, :], w_scr[:, :], QLO, None,
                                       ALU.max))
                        E("vector", VCOPY(wq2[:, r * D:r * D + D],
                                          w_i32[:, :]),
                          waits=[trp_marks.get(i - 2)], inc=sDVE)
                        (qm_g if c == 0 else qm_u)[i] = sDVE.cur()
                    qmark[i] = qm_u[i]

                def emit_trp(i):
                    r = i % 2
                    for c, (wq2, psT, qm, evp) in enumerate((
                            (wqg_bf, psTg, qm_g[i], ev_g.get(i - 1)),
                            (wqu_bf, psTu, qm_u[i], ev_u.get(i - 1)))):
                        for f in range(8):
                            w = [qm, evp] if f == 0 else ()
                            E("tensor", TRP(psT[:, 128 * f:128 * f + 128],
                                            wq2[:, r * D + 128 * f:
                                                r * D + 128 * f + 128],
                                            id_bf[:, :]), waits=w)
                        ctx.retro_inc("tensor", sPE)
                        (trpg_marks if c == 0 else trp_marks)[i] = sPE.cur()

                def emit_swsx(i):
                    s3 = i % 3
                    for c, swsx in ((0, swsx_g3), (1, swsx_u3)):
                        ac2 = 2 * (i % 2) + c
                        E("vector", TS(swsx[:, s3:s3 + 1], amx[:, ac2:ac2 + 1],
                                       R127, sxb[:, 0:1], ALU.mult, ALU.mult),
                          waits=[sxbm, silm_last.get(i - 3)], inc=sDVE)
                    swsx_wm[i] = sDVE.cur()

                def emit_ev(i):
                    r = i % 2
                    E("vector", VCOPY(wgT[:, r * D:r * D + D], psTg[:, :]),
                      waits=[trpg_marks[i], gum_marks.get(i - 2)], inc=sDVE)
                    ev_g[i] = sDVE.cur()
                    E("vector", VCOPY(wuT[:, r * D:r * D + D], psTu[:, :]),
                      waits=[trp_marks[i]], inc=sDVE)
                    ev_u[i] = sDVE.cur()
                    wqm_marks[i] = ev_u[i]
                emit_wq_q(0)
                emit_trp(0)
                emit_ev(0)
                emit_wq_q(1)
                emit_trp(1)
                emit_ev(1)

                E("sync", DMA(m128[0:1, :], ar1_out), waits=[sAR1.cur()],
                  inc=sCCR, dn=16)
                E("vector", RMAX(sc[0:1, 8:9], m128[0:1, :]),
                  waits=[sCCR.cur()], inc=sDVE)
                E("vector", VCOPY(sc[0:1, 0:1], sc[0:1, 8:9]),
                  waits=[sDVE.cur()], inc=sDVE)
                gmm = sDVE.cur()
                E("scalar", ACTV(sc[0:1, 1:2], clipx[0:1, 0:1], AF.Sigmoid),
                  inc=sACT)
                E("scalar", ACTV(sc[0:1, 2:3], sc[0:1, 1:2], AF.Identity,
                                 scale=sc[0:1, 0:1]),
                  waits=[gmm, sACT.cur()], inc=sACT)
                E("scalar", lambda e: e.mul(out=sc[0:1, 2:3],
                                            in_=sc[0:1, 2:3], mul=R127),
                  waits=[sACT.cur()], inc=sACT)
                sxactm = sACT.cur()
                E("vector", RECIP(sc[0:1, 3:4], sc[0:1, 2:3]),
                  waits=[sxactm], inc=sDVE)
                E("vector", VCOPY(sc[0:1, 9:10], sc[0:1, 3:4]),
                  waits=[sDVE.cur()], inc=sDVE)
                sxm = sDVE.cur()
                E("tensor", MM(psM[:, 0:1], ones_f32[0:1, :], sc[0:1, 2:3],
                               True, True), waits=[sxm])
                E("tensor", MM(psM[:, 2:3], ones_f32[0:1, :], sc[0:1, 9:10],
                               True, True), inc=sPE)
                E("vector", VCOPY(sxb[:, 0:1], psM[:, 0:1]), waits=[sPE.cur()])
                E("vector", VCOPY(rsxb[:, 0:1], psM[:, 2:3]), inc=sDVE)
                sxbm = sDVE.cur()
                psM_free = [sxbm]
                emit_swsx(0)
                emit_swsx(1)

                psTf_free = [None]
                xq_marks = {}
                trm_prev = [None]
                for t in range(T):
                    r = t % 2
                    xmv = xm_t[:, r * D:(r + 1) * D]
                    E("sync", DMA(xmv, xmid_d[t * TOK:(t + 1) * TOK, :]),
                      waits=[sXMs.at(16 * (t + 1)), xq_marks.get(t - 2)],
                      inc=sXMl, dn=16)
                    ldm = sXMl.cur()
                    E("vector", TS(y2[:, :], xmv, nm2[:, t:t + 1],
                                   rs2[:, t:t + 1], ALU.add, ALU.mult),
                      waits=[ldm, sxbm])
                    E("vector", TS(y2[:, :], y2[:, :], rsxb[:, 0:1], QHI,
                                   ALU.mult, ALU.min))
                    E("vector", TS(xq_i32[:, :], y2[:, :], QLO, None, ALU.max))
                    E("vector", VCOPY(xq_bf[:, :], xq_i32[:, :]),
                      waits=[trm_prev[0]], inc=sDVE)
                    xqbm = sDVE.cur()
                    if dbg and t == 0:
                        E("vector", VCOPY(silu_sb[:, 0:1024], y2[:, :]),
                          inc=sDVE)
                        E("sync", DMA(dbg_d["d_y2"], silu_sb[:, 0:1024]),
                          waits=[sDVE.cur()], inc=sOUT, dn=16)
                        E("sync", DMA(dbg_d["d_nm2"], nm2[:, :]),
                          waits=[sDVE.cur()], inc=sOUT, dn=16)
                        E("sync", DMA(dbg_d["d_rs2"], rs2[:, :]),
                          waits=[sDVE.cur()], inc=sOUT, dn=16)
                    for f in range(8):
                        w = [xqbm, psTf_free[0]] if f == 0 else ()
                        E("tensor", TRP(psTf[:, 128 * f:128 * f + 128],
                                        xq_bf[:, 128 * f:128 * f + 128],
                                        id_bf[:, :]), waits=w)
                    ctx.retro_inc("tensor", sPE)
                    trm = sPE.cur()
                    trm_prev[0] = trm
                    E("vector", VCOPY(
                        xq_fm[:, :].rearrange("p (f n) -> p f n", n=2048)[
                            :, :, 128 * t:128 * t + 128],
                        psTf[:, :].rearrange("p (f n) -> p f n", n=128)),
                      waits=[trm], inc=sDVE)
                    psTf_free[0] = sDVE.cur()
                    xq_marks[t] = sDVE.cur()
                xq_all = sDVE.cur()

                wd_j = [0]
                wdT_marks = []
                wd_pe_prev = [None]

                def emit_wd_prep(j):
                    E("sync", DMA(wd_st[:, :], wd_d[128 * j:128 * j + 128, :]),
                      waits=[wdT_marks[j - 1] if j > 0 else None], inc=sFW, dn=16)
                    ldm = sFW.cur()
                    E("vector", RMAXA(swd_cols[:, j:j + 1], wd_st[:, :]),
                      waits=[ldm], inc=sDVE)
                    E("vector", VCOPY(amx2[:, 0:1], swd_cols[:, j:j + 1]),
                      waits=[sDVE.cur()], inc=sDVE)
                    E("scalar", lambda e: e.mul(out=w_act1[:, 0:1],
                                                in_=amx2[:, 0:1], mul=R127),
                      waits=[sDVE.cur()], inc=sACT)
                    E("vector", RECIP(rsw_wd[:, 0:1], w_act1[:, 0:1]),
                      waits=[sACT.cur()], inc=sDVE)
                    rwm = sDVE.cur()
                    for cc in range(3):
                        wdt = 1024 if cc < 2 else DI - 2048
                        E("vector", TS(w_scr[:, 0:wdt],
                                       wd_st[:, 1024 * cc:1024 * cc + wdt],
                                       rsw_wd[:, 0:1], QHI, ALU.mult, ALU.min),
                          waits=[wd_pe_prev[0], rwm] if cc == 0 else ())
                        E("vector", TS(w_i32[:, 0:wdt], w_scr[:, 0:wdt], QLO,
                                       None, ALU.max))
                        E("vector", VCOPY(wdq_bf[:, 1024 * cc:1024 * cc + wdt],
                                          w_i32[:, 0:wdt]))
                    ctx.retro_inc("vector", sDVE)
                    qm = sDVE.cur()
                    for i in range(NIT):
                        half = i % 8
                        w = [qm, psTf_free[0]] if half == 0 else ()
                        E("tensor", TRP(psTf[:, 128 * half:128 * half + 128],
                                        wdq_bf[:, 128 * i:128 * i + 128],
                                        id_bf[:, :]), waits=w)
                        if half == 7 or i == NIT - 1:
                            ctx.retro_inc("tensor", sPE)
                            tm = sPE.cur()
                            lo = i - half
                            E("vector", VCOPY(
                                wdT[:, :].rearrange("p (i n) -> p i n", n=1024)[
                                    :, lo:lo + half + 1, 128 * j:128 * j + 128],
                                psTf[:, 0:(half + 1) * 128].rearrange(
                                    "p (i n) -> p i n", n=128)),
                              waits=[tm], inc=sDVE)
                            psTf_free[0] = sDVE.cur()
                    wd_pe_prev[0] = sPE.cur()
                    wdT_marks.append(sDVE.cur())

                for i in range(NIT):
                    r = i % 2
                    if i + 2 < NIT:
                        emit_wq_q(i + 2)
                        emit_swsx(i + 2)
                    wq_m = wqm_marks[i]
                    hv = h_sb[:, r * 2048:r * 2048 + 2048]
                    h_first = [True]
                    for n in range(4):
                        pg = psG[(2 * n) % 4]
                        pu = psG[(2 * n + 1) % 4]
                        for kc in range(KC):
                            w = [wq_m, xq_marks[min(4 * n + 3, T - 1)],
                                 psG_free[(2 * n) % 4]] \
                                if kc == 0 else ()
                            E("tensor", MM(
                                pg[:, :],
                                wgT[:, r * D + 128 * kc:r * D + 128 * kc + 128],
                                xq_fm[:, 2048 * kc + 512 * n:
                                      2048 * kc + 512 * n + 512],
                                kc == 0, kc == KC - 1), waits=w)
                        ctx.retro_inc("tensor", sPE)
                        gmm2 = sPE.cur()
                        for kc in range(KC):
                            w = [psG_free[(2 * n + 1) % 4]] if kc == 0 else ()
                            E("tensor", MM(
                                pu[:, :],
                                wuT[:, r * D + 128 * kc:r * D + 128 * kc + 128],
                                xq_fm[:, 2048 * kc + 512 * n:
                                      2048 * kc + 512 * n + 512],
                                kc == 0, kc == KC - 1), waits=w)
                        ctx.retro_inc("tensor", sPE)
                        umm = sPE.cur()
                        s3 = i % 3
                        sl = n % 2
                        E("scalar", ACTV(silu_sb[:, 512 * sl:512 * sl + 512],
                                         pg[:, :], AF.Silu,
                                         scale=swsx_g3[:, s3:s3 + 1]),
                          waits=[gmm2, silu_free[sl], swsx_wm[i]], inc=sACT)
                        psG_free[(2 * n) % 4] = sACT.cur()
                        silm = sACT.cur()
                        silm_last[i] = silm
                        w_extra = [sHw.at(16 * (i - 1))] if h_first[0] else []
                        h_first[0] = False
                        E("vector", STT(hv[:, 512 * n:512 * n + 512], pu[:, :],
                                        swsx_u3[:, s3:s3 + 1],
                                        silu_sb[:, 512 * sl:512 * sl + 512],
                                        ALU.mult, ALU.mult),
                          waits=[umm, silm] + w_extra, inc=sDVE)
                        psG_free[(2 * n + 1) % 4] = sDVE.cur()
                        silu_free[sl] = sDVE.cur()
                    gum_marks[i] = sPE.cur()
                    hm = sDVE.cur()
                    if i + 2 < NIT:
                        emit_trp(i + 2)
                        emit_ev(i + 2)
                    E("vector", RMAXA(htmp[:, 0:1], hv), waits=[hm],
                      inc=sDVE)
                    E("vector", TT(hmx[:, :], hmx[:, :], htmp[:, 0:1],
                                   ALU.max), waits=[sDVE.cur()], inc=sDVE)
                    hm_marks[i] = sDVE.cur()
                    E("scalar", DMA(h_d[128 * i:128 * i + 128, :], hv),
                      waits=[hm], inc=sHw, dn=16)
                    if i % 3 == 2 and wd_j[0] < 8:
                        emit_wd_prep(wd_j[0])
                        wd_j[0] += 1
                while wd_j[0] < 8:
                    emit_wd_prep(wd_j[0])
                    wd_j[0] += 1
                wdT_done[0] = wdT_marks[-1]
                if dbg:
                    E("sync", DMA(dbg_d["d_hsb"],
                                  h_sb[:, ((NIT - 1) % 2) * 2048:
                                       ((NIT - 1) % 2) * 2048 + 2048]),
                      waits=[hm_marks[NIT - 1]], inc=sOUT, dn=16)

                if dbg:
                    E("sync", DMA(dbg_d["d_xq"], xq_fm[:, 0:2048]),
                      waits=[xq_all], inc=sOUT, dn=16)
                    E("sync", DMA(dbg_d["d_swsxg"], swsx_g3[:, 0:1]),
                      waits=[sDVE.cur()], inc=sOUT, dn=16)
                    E("sync", DMA(dbg_d["d_hmx"], hmx[:, 0:1]),
                      waits=[sDVE.cur()], inc=sOUT, dn=16)
                hmxm = sDVE.cur()
                E("tensor", TRP(psM[0:1, 0:128], hmx[:, 0:1], id_f32[:, :]),
                  waits=[hmxm, psM_free[0]], inc=sPE)
                E("scalar", SCOPY(m128[0:1, :], psM[0:1, 0:128]),
                  waits=[sPE.cur()], inc=sACT)
                E("sync", DMA(ar2_in, m128[0:1, :]), waits=[sACT.cur()],
                  inc=sPC, dn=16)
                E("gpsimd", lambda e: e.collective_compute(
                    "AllReduce", ALU.max, replica_groups=G8,
                    ins=[ar2_in.opt()], outs=[ar2_out.opt()]),
                  waits=[sPC.cur()], inc=sAR2)

        if stage in (1, 2):
            E("sync", DMA(out_d, xmid_d), waits=[sXMs.at(16 * T)], inc=sOUT, dn=16)
            E("sync", NOP(), waits=[sOUT.cur()])
        ctx.run_block()

        if stage >= 3:
            with ExitStack() as C:
                def csb(name, shape, dt):
                    return C.enter_context(nc.sbuf_tensor(name, shape, dt))

                hrb = csb("hrb", [128, 2 * DI], F32)
                hq_scr = csb("hq_scr", [128, DI], F32)
                hq_i32 = csb("hq_i32", [128, DI], I32)
                hq_bf = csb("hq_bf", [128, 2 * DI], BF16)
                xm_t2 = csb("xm_t2", [128, 2 * D], F32)
                o1 = csb("o1", [128, D], F32)
                out_sb = csb("out_sb", [128, 2 * D], F32)

                psD0 = C.enter_context(nc.psum_tensor("psD0", [128, 1024], F32))
                psD1 = C.enter_context(nc.psum_tensor("psD1", [128, 1024], F32))
                psS = C.enter_context(nc.psum_tensor("psS", [128, 512], F32))
                psS2 = C.enter_context(nc.psum_tensor("psS2", [128, 512], F32))

                E("sync", DMA(m128[0:1, :], ar2_out), waits=[sAR2.cur()],
                  inc=sCCR, dn=16)
                E("vector", RMAX(sc[0:1, 12:13], m128[0:1, :]),
                  waits=[sCCR.cur()], inc=sDVE)
                E("vector", VCOPY(sc[0:1, 4:5], sc[0:1, 12:13]),
                  waits=[sDVE.cur()], inc=sDVE)
                gm2 = sDVE.cur()
                E("scalar", ACTV(sc[0:1, 5:6], clipd[0:1, 0:1], AF.Sigmoid),
                  inc=sACT)
                sig2 = sACT.cur()
                E("scalar", ACTV(sc[0:1, 6:7], sc[0:1, 5:6], AF.Identity,
                                 scale=sc[0:1, 4:5]), waits=[gm2, sig2],
                  inc=sACT)
                E("scalar", lambda e: e.mul(out=sc[0:1, 6:7],
                                            in_=sc[0:1, 6:7], mul=R127),
                  waits=[sACT.cur()], inc=sACT)
                E("vector", RECIP(sc[0:1, 7:8], sc[0:1, 6:7]),
                  waits=[sACT.cur()], inc=sDVE)
                E("vector", VCOPY(sc[0:1, 13:14], sc[0:1, 7:8]),
                  waits=[sDVE.cur()], inc=sDVE)
                shm = sDVE.cur()
                E("tensor", MM(psS[:, 0:1], ones_f32[0:1, :], sc[0:1, 13:14],
                               True, True), waits=[shm], inc=sPE)
                E("vector", VCOPY(rshb[:, 0:1], psS[:, 0:1]), waits=[sPE.cur()],
                  inc=sDVE)
                rshm = sDVE.cur()
                acm = [None]
                for j in range(8):
                    w = [sPE.cur()] if j == 0 else ([acm[0]] if j == 4 else ())
                    E("tensor", TRP(psS2[0:1, 128 * (j % 4):128 * (j % 4) + 128],
                                    swd_cols[:, j:j + 1], id_f32[:, :]), waits=w)
                    if j % 4 == 3:
                        ctx.retro_inc("tensor", sPE)
                        tm = sPE.cur()
                        lo = (j - 3) * 128
                        E("scalar", SCOPY(swd_row[0:1, lo:lo + 512],
                                          psS2[0:1, 0:512]), waits=[tm], inc=sACT)
                        acm[0] = sACT.cur()
                swdr = sACT.cur()
                E("vector", TS(swd_row[0:1, :], swd_row[0:1, :], R127,
                               sc[0:1, 6:7], ALU.mult, ALU.mult),
                  waits=[swdr, shm], inc=sDVE)
                swdm = sDVE.cur()
                E("tensor", MM(psS[:, 0:512], ones_f32[0:1, :],
                               swd_row[0:1, 0:512], True, True),
                  waits=[swdm, sDVE.cur()])
                E("tensor", MM(psS2[:, 0:512], ones_f32[0:1, :],
                               swd_row[0:1, 512:1024], True, True), inc=sPE)
                bcm = sPE.cur()
                E("vector", VCOPY(swd_bc[:, 0:512], psS[:, 0:512]), waits=[bcm])
                E("vector", VCOPY(swd_bc[:, 512:1024], psS2[:, 0:512]), inc=sDVE)
                swdbcm = sDVE.cur()

                if dbg:
                    E("sync", DMA(dbg_d["d_sc"], sc[0:1, :]),
                      waits=[shm], inc=sOUT, dn=16)
                    E("sync", DMA(dbg_d["d_swdrow"], swd_row[0:1, :]),
                      waits=[swdm], inc=sOUT, dn=16)
                    E("sync", DMA(dbg_d["d_rshb"], rshb[:, 0:1]),
                      waits=[rshm], inc=sOUT, dn=16)
                hq_marks = {}
                out_done = {}
                dmm_marks = {}
                hview = h_d.rearrange("(i p) q -> p i q", p=128)
                E("sync", DMA(hrb[:, 0:DI].rearrange("p (i n) -> p i n", n=128),
                              hview[:, :, 0:128]),
                  waits=[sHw.at(16 * NIT)], inc=sHr, dn=16)

                def emit_hq(t):
                    r = t % 2
                    if t + 1 < T:
                        nr = (t + 1) % 2
                        E("sync", DMA(
                            hrb[:, nr * DI:nr * DI + DI].rearrange(
                                "p (i n) -> p i n", n=128),
                            hview[:, :, 128 * (t + 1):128 * (t + 1) + 128]),
                          waits=[hq_marks.get(t - 1)], inc=sHr, dn=16)
                    hld = sHr.at(16 * (t + 1))
                    hv = hrb[:, r * DI:r * DI + DI]
                    hqv = hq_bf[:, r * DI:r * DI + DI]
                    E("vector", TS(hq_scr[:, :], hv, rshb[:, 0:1], QHI,
                                   ALU.mult, ALU.min), waits=[hld, rshm])
                    E("vector", TS(hq_i32[:, :], hq_scr[:, :], QLO, None,
                                   ALU.max))
                    E("vector", VCOPY(hqv, hq_i32[:, :]),
                      waits=[dmm_marks.get(t - 2)], inc=sDVE)
                    hq_marks[t] = sDVE.cur()

                emit_hq(0)
                for t in range(T):
                    r = t % 2
                    if t + 1 < T:
                        emit_hq(t + 1)
                    hqm = hq_marks[t]
                    hqv = hq_bf[:, r * DI:r * DI + DI]
                    if dbg and t == 0:
                        E("sync", DMA(dbg_d["d_hq"], hqv), waits=[hqm],
                          inc=sOUT, dn=16)
                    xmv = xm_t2[:, r * D:(r + 1) * D]
                    E("scalar", DMA(xmv, xmid_d[t * TOK:(t + 1) * TOK, :]),
                      waits=[out_done.get(t - 2)], inc=sXMl, dn=16)
                    xml = sXMl.cur()
                    psD = psD0 if r == 0 else psD1
                    for half in range(2):
                        for kc in range(NIT):
                            w = [hqm, wdT_done[0], out_done.get(t - 2)] \
                                if (half == 0 and kc == 0) else ()
                            E("tensor", MM(
                                psD[:, 512 * half:512 * half + 512],
                                hqv[:, 128 * kc:128 * kc + 128],
                                wdT[:, 1024 * kc + 512 * half:
                                    1024 * kc + 512 * half + 512],
                                kc == 0, kc == NIT - 1), waits=w)
                    ctx.retro_inc("tensor", sPE)
                    dmm_marks[t] = sPE.cur()
                    dmm = sPE.cur()
                    E("vector", TT(o1[:, :], psD[:, :], swd_bc[:, :], ALU.mult),
                      waits=[dmm, swdbcm, xml])
                    ov = out_sb[:, r * D:(r + 1) * D]
                    E("vector", TT(ov, o1[:, :], xmv, ALU.add),
                      waits=[sOUT.at(16 * (t - 1))], inc=sDVE)
                    out_done[t] = sDVE.cur()
                    E("sync", DMA(out_d[t * TOK:(t + 1) * TOK, :], ov),
                      waits=[out_done[t]], inc=sOUT, dn=16)
                E("sync", NOP(), waits=[sOUT.cur()])

        ctx.run_block()

    return nc


def kernel(L0_attn_x, L0_attn_qkv_w, L0_attn_qkv_b, L0_attn_rq_w, L0_attn_rq_b,
           L0_attn_rkv_w, L0_attn_rkv_b, L0_attn_rout_w, L0_attn_rout_b,
           L0_attn_out_w, L0_attn_out_b, L0_attn_gl_w, L0_attn_gr_w,
           L0_ffn_clip_x, L0_ffn_clip_down, L0_ffn_w_gate, L0_ffn_w_up,
           L0_ffn_w_down):
    X = np.ascontiguousarray(np.asarray(L0_attn_x, dtype=np.float32))
    assert X.shape == (2, 8192, 1024)

    def f32(a):
        return np.ascontiguousarray(np.asarray(a, dtype=np.float32))

    qkv_b = f32(L0_attn_qkv_b)
    rq_b = f32(L0_attn_rq_b)
    rkv_b = f32(L0_attn_rkv_b)
    gl_w = f32(L0_attn_gl_w)
    gr_w = f32(L0_attn_gr_w)

    qi = np.arange(128)[:, None]
    kj = np.arange(256)[None, :]
    band = (qi <= kj) & (kj < qi + 128)
    bandE = band.copy()
    bandE[0, 0] = False
    band0 = band & (kj >= 128)

    def mk_mask(b):
        bT = np.concatenate([b[:, 0:128].T, b[:, 128:256].T], axis=1)
        bias = (1.0 - bT.astype(np.float32)) * -1.0e5
        return np.ascontiguousarray(
            np.tile(bias, (1, 4)).astype(ml_dtypes.bfloat16))

    maskE_np = mk_mask(bandE)
    maskO_np = mk_mask(band)
    mask0_np = mk_mask(band0)

    glb_np = np.ascontiguousarray(
        np.broadcast_to(np.tile(gl_w[:, 0], 16)[None, :], (128, 1024))
        .astype(np.float32))
    grb_np = np.ascontiguousarray(
        np.broadcast_to(np.tile(gr_w[:, 0], 16)[None, :], (128, 1024))
        .astype(np.float32))

    shared = {
        "wqkv": f32(L0_attn_qkv_w), "wrq": f32(L0_attn_rq_w),
        "wrkv": f32(L0_attn_rkv_w), "wrout": f32(L0_attn_rout_w),
        "wout": f32(L0_attn_out_w),
        "qkvb_col": np.ascontiguousarray(qkv_b[:2048].reshape(16, 128).T),
        "qkvb_vrow": np.ascontiguousarray(qkv_b[2048:].reshape(1, 1024)),
        "rqb_col": np.ascontiguousarray(rq_b.reshape(2, 128).T),
        "rkvb_col": np.ascontiguousarray(rkv_b[:256].reshape(2, 128).T),
        "rvb_row": np.ascontiguousarray(rkv_b[256:].reshape(1, 256)),
        "routb_row": np.ascontiguousarray(f32(L0_attn_rout_b).reshape(1, 1024)),
        "outb_row": np.ascontiguousarray(f32(L0_attn_out_b).reshape(1, 1024)),
        "glb": glb_np, "grb": grb_np,
        "maskE": maskE_np, "maskO": maskO_np,
        "clipx": f32(L0_ffn_clip_x).reshape(1, 1),
        "clipd": f32(L0_ffn_clip_down).reshape(1, 1),
        "wg": f32(L0_ffn_w_gate), "wu": f32(L0_ffn_w_up),
        "wd": f32(L0_ffn_w_down),
    }

    in_maps = []
    for c in range(NCORES):
        b, j = divmod(c, 4)
        lo = j * 2048
        xs = np.ascontiguousarray(X[b, lo:lo + 2048, :])
        xh = (np.ascontiguousarray(X[b, lo - 128:lo, :]) if j > 0
              else np.zeros((128, 1024), np.float32))
        z = np.zeros((128, 1), np.float32)
        if j == 0:
            z[0, 0] = 1.0
        lm = np.full((T, 65), NEG, np.float32)
        g0 = 16 * j
        for t in range(T):
            cg = g0 + t
            lm[t, :min(cg, 64)] = 0.0
            lm[t, 64] = 0.0 if cg == 0 else NEG
        m = dict(shared)
        m["x"] = xs
        m["xh"] = xh
        m["z_col"] = z
        m["z_row"] = np.ascontiguousarray(z.reshape(1, 128))
        lmrow = lm.reshape(T * 65)
        lm65 = np.zeros((65, T * 65), np.float32)
        lm65[0, :] = lmrow
        lm65[64, :] = lmrow
        m["lmask"] = np.ascontiguousarray(lm65.astype(ml_dtypes.bfloat16))
        m["mask0"] = mask0_np if j == 0 else maskE_np
        in_maps.append(m)

    nc = build_nc()
    res = run_bass_kernel_spmd(nc, in_maps, core_ids=list(range(NCORES)))
    parts = [np.asarray(res.results[c]["out"]) for c in range(NCORES)]
    return np.concatenate(parts, axis=0).reshape(2, 8192, 1024).astype(
        np.float32)
